# version 14
# baseline (speedup 1.0000x reference)
import contextlib

DEBUG_RT = False
import numpy as np

import concourse.bass as bass
import concourse.mybir as mybir
from concourse.bass import IndirectOffsetOnAxis
from concourse.bass_utils import run_bass_kernel_spmd

F32 = mybir.dt.float32
BF16 = mybir.dt.bfloat16
I32 = mybir.dt.int32
AF = mybir.ActivationFunctionType
ALU = mybir.AluOpType
AX = mybir.AxisListType

D = 1024
S = 4096
NP = S // 128
NCD = D // 128
DFF = 4096
NFC = DFF // 128
NE = 16
LOCAL_CAP = 384
NSLOT = NE * LOCAL_CAP
QBLK = 512
NBLK = S // QBLK
MASK_NEG = -1.0e6

ENGINES = ("sync", "tensor", "vector", "scalar", "gpsimd")


class Sem:
    def __init__(self, handle, name):
        self.h = handle
        self.name = name
        self.n = 0


class Prog:
    def __init__(self, nc, stack):
        self.nc = nc
        self.q = {e: [] for e in ENGINES}
        self._stack = stack
        self._last_wait = {}
        self.sems = []

    def sem(self, name):
        h = self._stack.enter_context(self.nc.semaphore(name))
        s = Sem(h, name)
        self.sems.append(s)
        return s

    def emit(self, eng, fn, wait=(), inc=()):
        wait = [w for w in wait if w is not None]
        inc = list(inc)
        for s, v in wait:
            assert v <= s.n, (
                f"[{eng}] wait {s.name}>={v} but only {s.n} recorded (deadlock)")
        ms = []
        for s, k in inc:
            s.n += k
            ms.append(s.n)
        eff = []
        for s, v in wait:
            key = (eng, s.name)
            if self._last_wait.get(key, -1) >= v:
                continue
            self._last_wait[key] = v
            eff.append((s, v))

        def run(engine, _fn=fn, _w=eff, _i=inc):
            for s, v in _w:
                engine.wait_ge(s.h, v)
            inst = _fn(engine)
            if inst is not None:
                for s, k in _i:
                    inst.then_inc(s.h, k)
            return inst

        self.q[eng].append(run)
        return ms

    def barrier(self, skip=()):
        snap = [(s, s.n) for s in self.sems if s.name not in skip]
        for eng in ENGINES:
            self.emit(eng, lambda e: None, wait=snap)

    def finalize(self, block):
        @block.sync
        def _(e):
            for f in self.q["sync"]:
                f(e)

        @block.tensor
        def _(e):
            for f in self.q["tensor"]:
                f(e)

        @block.vector
        def _(e):
            for f in self.q["vector"]:
                f(e)

        @block.scalar
        def _(e):
            for f in self.q["scalar"]:
                f(e)

        @block.gpsimd
        def _(e):
            for f in self.q["gpsimd"]:
                f(e)


def build_program(n_sub=4):
    nc = bass.Bass("TRN2", target_bir_lowering=False, debug=False)

    x_in = nc.dram_tensor("x_in", [S, D], F32, kind="ExternalInput").ap()
    wq_a = nc.dram_tensor("wq_a", [D, D], F32, kind="ExternalInput").ap()
    wk_a = nc.dram_tensor("wk_a", [D, D], F32, kind="ExternalInput").ap()
    wv_a = nc.dram_tensor("wv_a", [D, D], F32, kind="ExternalInput").ap()
    wq_b = nc.dram_tensor("wq_b", [D, D], F32, kind="ExternalInput").ap()
    wk_b = nc.dram_tensor("wk_b", [D, D], F32, kind="ExternalInput").ap()
    wv_b = nc.dram_tensor("wv_b", [D, D], F32, kind="ExternalInput").ap()
    wo_b = nc.dram_tensor("wo_b", [D, D], F32, kind="ExternalInput").ap()
    bo_b = nc.dram_tensor("bo_b", [D, 1], F32, kind="ExternalInput").ap()
    gate_td = nc.dram_tensor("gate_td", [D, NE], F32, kind="ExternalInput").ap()
    w1 = nc.dram_tensor("w1", [NE, D, DFF], F32, kind="ExternalInput").ap()
    w2 = nc.dram_tensor("w2", [NE, DFF, D], F32, kind="ExternalInput").ap()
    y_out = nc.dram_tensor("y_out", [S, D], F32, kind="ExternalOutput").ap()

    resid = [x_in]
    for i in range(1, n_sub):
        resid.append(nc.dram_tensor(f"resid{i}", [S, D], F32, kind="Internal").ap())
    resid.append(y_out)

    qbuf = nc.dram_tensor("qbuf", [NCD, 128, S], BF16, kind="Internal").ap()
    kbufA = nc.dram_tensor("kbufA", [NP, 128, D], BF16, kind="Internal").ap()
    kbufB = nc.dram_tensor("kbufB", [16, 64, S], BF16, kind="Internal").ap()
    vbufA = nc.dram_tensor("vbufA", [NP, 128, 1040], BF16, kind="Internal").ap()
    vbufB = nc.dram_tensor("vbufB", [8, S, 130], BF16, kind="Internal").ap()
    w1b = nc.dram_tensor("w1b", [NE, D, DFF], BF16, kind="Internal").ap()
    w2b = nc.dram_tensor("w2b", [NE, DFF, D], BF16, kind="Internal").ap()
    disp = nc.dram_tensor("disp", [NSLOT + 1, D], BF16, kind="Internal").ap()
    dbg = nc.dram_tensor("dbg", [5, 128, 64], F32, kind="Internal").ap()
    dbg_x = nc.dram_tensor("dbg_x", [128, 1024], F32, kind="Internal").ap()
    yexp = nc.dram_tensor("yexp", [NSLOT + 1, D], BF16, kind="Internal").ap()

    stack = contextlib.ExitStack()
    with stack:
        p = Prog(nc, stack)
        PE = p.sem("pe")
        VE = p.sem("ve")
        SE = p.sem("se")
        GC = p.sem("gc")
        sXP = [p.sem("sXP0"), p.sem("sXP1")]
        sXR = [p.sem("sXR0"), p.sem("sXR1")]
        sOS = [p.sem("sOS0"), p.sem("sOS1")]
        sQS = [p.sem("sQS0"), p.sem("sQS1")]
        sKS = [p.sem("sKS0"), p.sem("sKS1")]
        sVS = p.sem("sVS")
        sQB = p.sem("sQB")
        sKT = [p.sem(f"sKT{i}") for i in range(4)]
        sVT = [p.sem(f"sVT{i}") for i in range(4)]
        sKP2 = [p.sem("sKP0"), p.sem("sKP1")]
        sVP2 = [p.sem("sVP0"), p.sem("sVP1")]
        sWP = p.sem("sWP")
        sBO = p.sem("sBO")
        sGW = p.sem("sGW")
        sSC = [p.sem(f"sSC{i}") for i in range(4)]
        sXG = p.sem("sXG")
        sW1 = [p.sem("sW10"), p.sem("sW11")]
        sW2 = p.sem("sW2")
        sW1c = [p.sem("sW10c"), p.sem("sW11c")]
        sW2c = p.sem("sW2c")
        sYS = p.sem("sYS")
        sYG = p.sem("sYG")
        sZR = p.sem("sZR")
        sWB = [p.sem("sWB0"), p.sem("sWB1"), p.sem("sWB2")]
        sCA = [p.sem(f"sCA{i}") for i in range(4)]
        sCC = [p.sem(f"sCC{i}") for i in range(4)]
        sYG2 = p.sem("sYG2")
        sDB = p.sem("sDB")
        cast_go = [0]
        sb = stack.enter_context

        ident_bf = sb(nc.sbuf_tensor("ident_bf", [128, 128], BF16))
        ident_f32 = sb(nc.sbuf_tensor("ident_f32", [128, 128], F32))
        ustrict = sb(nc.sbuf_tensor("ustrict", [128, 128], F32))
        cmask = sb(nc.sbuf_tensor("cmask", [128, 4 * QBLK], BF16))
        iotaE = sb(nc.sbuf_tensor("iotaE", [128, NE], F32))
        ones_r = sb(nc.sbuf_tensor("ones_r", [1, 128], F32))
        ones_cb = sb(nc.sbuf_tensor("ones_cb", [128, 1], BF16))
        ones_cf = sb(nc.sbuf_tensor("ones_cf", [128, 1], F32))

        AR1 = sb(nc.sbuf_tensor("AR1", [128, 32768], BF16))
        AR2 = sb(nc.sbuf_tensor("AR2", [128, 39040], BF16))
        wproj = AR1[:, 0:24576]
        expT = AR2[:, 0:16384]
        zst = AR2[:, 16384:20480]
        zst2 = AR2[:, 20480:24576]
        vpair2 = [AR2[:, 24576:28736], AR2[:, 6144:10304]]
        kpair2 = [AR2[:, 34928:39024], AR2[:, 10304:14400]]
        vtile = [AR2[:, 28736 + i * 1040:28736 + (i + 1) * 1040] for i in range(4)]
        vst = AR2[:, 30816:31856]
        kst = AR2[:, 31856:32880]
        qst = AR2[:, 32880:33904]
        ktile = [AR2[:, 34928 + i * 1024:34928 + (i + 1) * 1024] for i in range(4)]
        w2sb = AR1[:, 0:32768]
        w1sb = [AR2[:, 0:8192], AR2[:, 8192:16384]]
        hsb = AR2[:, 16384:28672]
        xg = AR2[:, 28672:31744]
        xgT = AR2[:, 31744:34816]
        ytk = AR2[:, 34816:37888]
        yga = AR2[:, 37888:38912]

        ARF = sb(nc.sbuf_tensor("ARF", [128, 3072], F32))
        xres = [ARF[:, 0:1024], ARF[:, 1024:2048]]
        sqs = ARF[:, 0:1024]
        normB = ARF[:, 2048:2560]
        fq = ARF[0:1, 2560:3072]
        yfm = ARF[:, 0:3072]

        qgrp = sb(nc.sbuf_tensor("qgrp", [128, NCD * QBLK], BF16))
        xln_tok = sb(nc.sbuf_tensor("xln_tok", [128, 2 * D], BF16))
        xln_fm = sb(nc.sbuf_tensor("xln_fm", [128, 2 * NCD * 512], BF16))
        xpan = sb(nc.sbuf_tensor("xpan", [128, 2 * D], F32))
        outs2 = sb(nc.sbuf_tensor("outs2", [128, 2 * D], F32))
        outst = [outs2[:, 0:1024], outs2[:, 1024:2048]]
        xfm = outs2[:, 0:1024]
        ysc = xpan[:, 0:1024]
        lnsq = sb(nc.sbuf_tensor("lnsq", [128, D], F32))
        lnst = sb(nc.sbuf_tensor("lnst", [128, 8], F32))
        mrun = sb(nc.sbuf_tensor("mrun", [128, 2], F32))
        csc = sb(nc.sbuf_tensor("csc", [1, 8], F32))
        negC = sb(nc.sbuf_tensor("negC", [128, 1], F32))
        bo_sb = sb(nc.sbuf_tensor("bo_sb", [128, NCD], F32))

        gate_sb = sb(nc.sbuf_tensor("gate_sb", [128, NCD * NE], F32))
        gsumB = sb(nc.sbuf_tensor("gsumB", [128, NE], F32))
        gsrow = sb(nc.sbuf_tensor("gsrow", [1, NE], F32))
        offrow = sb(nc.sbuf_tensor("offrow", [1, NE], F32))
        wtok = sb(nc.sbuf_tensor("wtok", [128, NP], F32))
        sloti = sb(nc.sbuf_tensor("sloti", [128, NP], I32))
        rrec = sb(nc.sbuf_tensor("rrec", [128, 2 * NP], F32))
        routg = ARF[:, 2048:2240]
        rout1 = sb(nc.sbuf_tensor("rout1", [128, 32], F32))

        psA = sb(nc.psum_tensor("psA", [128, 2048], F32))
        psB = sb(nc.psum_tensor("psB", [128, 1024], F32))
        pS = sb(nc.psum_tensor("pS", [128, 512], F32))
        pT = sb(nc.psum_tensor("pT", [128, 1024], BF16))

        guards = {r: [] for r in ("A0", "A1", "A2", "A3", "B0", "B1", "S", "T")}

        def reg_ap(r):
            if r[0] == "A":
                i = int(r[1])
                return psA[:, i * 512:(i + 1) * 512]
            if r[0] == "B":
                i = int(r[1])
                return psB[:, i * 512:(i + 1) * 512]
            if r == "S":
                return pS[:, :]
            return pT[:, :]

        def take(r):
            w = guards[r]
            guards[r] = []
            return w

        def used(r, sem, ms):
            guards[r].append((sem, ms))

        def t(fn, wait=()):
            return p.emit("tensor", fn, wait, [(PE, 1)])[0]

        def tq(fns, wait=()):
            for i, fn in enumerate(fns):
                if i == len(fns) - 1:
                    return p.emit("tensor", fn, wait if i == 0 else (), [(PE, 1)])[0]
                p.emit("tensor", fn, wait if i == 0 else ())

        def v(fn, wait=()):
            return p.emit("vector", fn, wait, [(VE, 1)])[0]

        def s_(fn, wait=()):
            return p.emit("scalar", fn, wait, [(SE, 1)])[0]

        def g(fn, wait=()):
            return p.emit("gpsimd", fn, wait, [(GC, 1)])[0]

        def dq(sem, fn, wait=()):
            return p.emit("sync", fn, wait, [(sem, 16)])[0]

        def gq(sem, fn, wait=()):
            return p.emit("gpsimd", fn, wait, [(sem, 16)])[0]

        m = g(lambda e: e.memset(ident_bf[:, :], 0.0))
        g(lambda e: e.affine_select(out=ident_bf[:, :], in_=ident_bf[:, :],
                                    compare_op=ALU.not_equal, fill=1.0, base=0,
                                    pattern=[[-1, 128]], channel_multiplier=1),
          wait=[(GC, m)])
        m = g(lambda e: e.memset(ident_f32[:, :], 0.0))
        g(lambda e: e.affine_select(out=ident_f32[:, :], in_=ident_f32[:, :],
                                    compare_op=ALU.not_equal, fill=1.0, base=0,
                                    pattern=[[-1, 128]], channel_multiplier=1),
          wait=[(GC, m)])
        m = g(lambda e: e.memset(ustrict[:, :], 0.0))
        g(lambda e: e.affine_select(out=ustrict[:, :], in_=ustrict[:, :],
                                    compare_op=ALU.is_ge, fill=1.0, base=0,
                                    pattern=[[-1, 128]], channel_multiplier=1),
          wait=[(GC, m)])
        m = g(lambda e: e.memset(cmask[:, :], 0.0))
        for tt in range(4):
            g(lambda e, tt=tt: e.affine_select(
                out=cmask[:, tt * QBLK:(tt + 1) * QBLK],
                in_=cmask[:, tt * QBLK:(tt + 1) * QBLK],
                compare_op=ALU.is_ge, fill=-1.0e9,
                base=-128 * tt, pattern=[[1, QBLK]], channel_multiplier=-1),
                wait=[(GC, m)])
        g(lambda e: e.iota(iotaE[:, :], pattern=[[1, NE]], base=0,
                           channel_multiplier=0,
                           allow_small_or_imprecise_dtypes=True))
        g(lambda e: e.memset(ones_r[:, :], 1.0))
        g(lambda e: e.memset(ones_cb[:, :], 1.0))
        g(lambda e: e.memset(ones_cf[:, :], 1.0))
        dq(sBO, lambda e: e.dma_start(
            out=bo_sb[:, :], in_=bo_b.rearrange("(c p) o -> p (c o)", p=128),
            allow_slow_non_contiguous=True))
        mzr = g(lambda e: e.memset(AR1[:, 24576:32768], 0.0))
        for buf in (disp, yexp):
            dq(sZR, lambda e, buf=buf: e.dma_start(
                out=buf[NSLOT:NSLOT + 1, :], in_=AR1[0:1, 24576:25600]),
                wait=[(GC, mzr)])
            for r in range(6):
                rows = min(1024, NSLOT - r * 1024)
                dq(sZR, lambda e, buf=buf, r=r, rows=rows: e.dma_start(
                    out=buf[r * 1024:r * 1024 + rows, :].rearrange(
                        "(a p) d -> p a d", p=128) if rows % 128 == 0 else
                    buf[r * 1024:r * 1024 + rows, :],
                    in_=AR1.ap().rearrange("p (a d) -> p a d", d=1024)[:, 24:24 + rows // 128]
                    if rows % 128 == 0 else AR1[0:rows, 24576:24576 + 1024]),
                    wait=[(GC, mzr)])
        m_zr = sZR.n
        p.barrier(skip=("sZR",))

        def emit_ln_panel(xsrc, pi, pb_, st):
            ml = dq(sXP[pb_], lambda e, pi=pi, pb_=pb_: e.dma_start(
                out=xpan[:, pb_ * D:(pb_ + 1) * D],
                in_=xsrc[pi * 128:(pi + 1) * 128, :]),
                wait=[(VE, st["xpan_v"][pb_]), (SE, st["xpan_s"][pb_]),
                      (PE, st["xpan_p"][pb_])])
            mv1 = v(lambda e, pb_=pb_: e.reduce_sum(
                out=lnst[:, 0:1], in_=xpan[:, pb_ * D:(pb_ + 1) * D], axis=AX.X),
                wait=[(sXP[pb_], ml), (VE, st["lnst_v"]), (SE, st["lnst_s"])])
            mv2 = v(lambda e: e.tensor_scalar_mul(
                lnst[:, 0:1], lnst[:, 0:1], 1.0 / D), wait=[(VE, mv1)])
            ms1 = s_(lambda e, pb_=pb_: e.activation(
                out=lnsq[:, :], in_=xpan[:, pb_ * D:(pb_ + 1) * D], func=AF.Square),
                wait=[(sXP[pb_], ml), (VE, st["sq_v"])])
            st["xpan_s"][pb_] = ms1
            mv3 = v(lambda e: e.reduce_sum(
                out=lnst[:, 1:2], in_=lnsq[:, :], axis=AX.X), wait=[(SE, ms1)])
            st["sq_v"] = mv3
            mv4 = v(lambda e: e.tensor_scalar(
                out=lnst[:, 2:3], in0=lnst[:, 0:1], scalar1=lnst[:, 0:1],
                scalar2=1e-5, op0=ALU.mult, op1=ALU.subtract), wait=[(VE, mv2)])
            mv5 = v(lambda e: e.scalar_tensor_tensor(
                out=lnst[:, 2:3], in0=lnst[:, 1:2], scalar=1.0 / D,
                in1=lnst[:, 2:3], op0=ALU.mult, op1=ALU.subtract),
                wait=[(VE, mv4), (VE, mv3)])
            ms2 = s_(lambda e: e.activation(
                out=lnst[:, 3:4], in_=lnst[:, 2:3], func=AF.Sqrt),
                wait=[(VE, mv5)])
            st["lnst_s"] = ms2
            mv6 = v(lambda e: e.reciprocal(
                out=lnst[:, 3:4], in_=lnst[:, 3:4]), wait=[(SE, ms2)])
            xld = st.get("xln_dst", xln_tok)
            nsl = st.get("xln_slots", 2)
            sl2 = pi % nsl
            mv7 = v(lambda e, pb_=pb_, sl2=sl2, xld=xld: e.tensor_scalar(
                out=xld[:, sl2 * D:(sl2 + 1) * D],
                in0=xpan[:, pb_ * D:(pb_ + 1) * D],
                scalar1=lnst[:, 0:1], scalar2=lnst[:, 3:4],
                op0=ALU.subtract, op1=ALU.mult),
                wait=[(VE, mv6), (PE, st["xlt_p"][sl2]),
                      (sSC[sl2], st["xlt_g"][sl2])])
            st["xpan_v"][pb_] = mv7
            st["lnst_v"] = mv7
            return ml, mv2, mv6, mv7

        def new_st():
            return dict(xpan_v=[0, 0], xpan_s=[0, 0], xpan_p=[0, 0],
                        xlt_p=[0, 0, 0, 0], xlt_g=[0, 0, 0, 0],
                        lnst_v=0, lnst_s=0, sq_v=0)

        def emit_attn(sub, kind):
            xsrc = resid[sub]
            xdst = resid[sub + 1]
            sc = 1.0 / np.sqrt(1024.0) if kind == "a" else 1.0 / np.sqrt(64.0)
            wq, wk, wv = (wq_a, wk_a, wv_a) if kind == "a" else (wq_b, wk_b, wv_b)

            for wi, w in enumerate((wq, wk, wv)):
                gq(sWP, lambda e, wi=wi, w=w: e.dma_start(
                    out=wproj.rearrange("p (i c o) -> p i c o", i=3, c=NCD)[:, wi],
                    in_=w.rearrange("(c p) o -> p c o", p=128)))
            m_wp = sWP.n
            if kind == "b":
                mvo = g(lambda e: e.memset(
                    vst.rearrange("p (pr u) -> p pr u", u=130)[:, :, 64:65], 1.0))
                mvo2 = g(lambda e: e.memset(
                    vst.rearrange("p (pr u) -> p pr u", u=130)[:, :, 129:130], 1.0),
                    wait=[(GC, mvo)])
            else:
                mvo2 = g(lambda e: e.memset(
                    vst.rearrange("p (pr u) -> p pr u", u=130)[:, :, 128:130], 0.0))

            st = new_st()
            qst_use = [0, 0]
            kst_use = [0, 0]
            vst_use = 0
            mev = 0
            xln_fm_p = [0, 0]
            nrm_pe = [0, 0]
            knrm_pe = [0, 0]
            csc_ve = [0]
            qpend = []
            kpend = []
            mm = 0
            for grp in range(8):
                gb = grp % 2
                if grp == 4:
                    cast_go[0] = PE.n
                for po in range(4):
                    pi = grp * 4 + po
                    pb_ = pi % 2
                    ml, _, _, mv7 = emit_ln_panel(xsrc, pi, pb_, st)
                    w0 = take("T") + [(VE, mv7)]
                    for c in range(NCD):
                        mtl = p.emit("tensor", lambda e, c=c, pb_=pb_: e.transpose(
                            pT[:, c * 128:(c + 1) * 128],
                            xln_tok[:, pb_ * D + c * 128:pb_ * D + c * 128 + 128],
                            ident_bf[:, :]),
                            wait=w0 if c == 0 else (),
                            inc=[(PE, 1)] if c == NCD - 1 else [])
                    mtr = mtl[0]
                    st["xlt_p"][pb_] = mtr
                    mev = v(lambda e, gb=gb, po=po: e.tensor_copy(
                        out=xln_fm.rearrange(
                            "p (b c u) -> p b c u", b=2, c=NCD)[:, gb, :, po * 128:(po + 1) * 128],
                        in_=pT.ap().rearrange("p (c u) -> p c u", c=NCD)),
                        wait=[(PE, mtr), (PE, xln_fm_p[gb])])
                    used("T", VE, mev)
                for wi in range(2):
                    for oc in range(NCD):
                        reg = ["A0", "A1"][oc % 2]
                        ww = take(reg) + [(VE, mev), (sWP, m_wp)]
                        fns = []
                        for c in range(NCD):
                            fns.append(lambda e, c=c, oc=oc, wi=wi, gb=gb, reg=reg: e.matmul(
                                reg_ap(reg),
                                wproj[:, (wi * NCD + c) * 1024 + oc * 128:
                                      (wi * NCD + c) * 1024 + oc * 128 + 128],
                                xln_fm[:, gb * 4096 + c * 512:gb * 4096 + (c + 1) * 512],
                                start=(c == 0), stop=(c == NCD - 1)))
                        mm = tq(fns, wait=ww)
                        if wi == 0:
                            qs = oc % 2
                            mevq = v(lambda e, qs=qs, reg=reg: e.tensor_copy(
                                out=qst[:, qs * 512:(qs + 1) * 512], in_=reg_ap(reg)),
                                wait=[(PE, mm), (sQS[qs], qst_use[qs])])
                            used(reg, VE, mevq)
                            qst_use[qs] = dq(sQS[qs], lambda e, oc=oc, grp=grp, qs=qs: e.dma_start(
                                out=qbuf[oc, :, grp * 512:(grp + 1) * 512],
                                in_=qst[:, qs * 512:(qs + 1) * 512]),
                                wait=[(VE, mevq)])
                            msq = s_(lambda e, qs=qs: e.activation(
                                out=sqs[:, qs * 512:(qs + 1) * 512],
                                in_=qst[:, qs * 512:(qs + 1) * 512], func=AF.Square),
                                wait=[(VE, mevq), (PE, nrm_pe[qs])])
                            qpend.append((qs, msq, oc))
                            if oc == NCD - 1:
                                flushn = list(qpend)
                                qpend.clear()
                            elif len(qpend) >= 2:
                                flushn = [qpend.pop(0)]
                            else:
                                flushn = []
                            for qs_f, msq_f, oc_f in flushn:
                                if kind == "a":
                                    wN = take("S") if oc_f == 0 else []
                                    nrm_pe[qs_f] = t(lambda e, qs=qs_f, oc2=oc_f: e.matmul(
                                        pS[0:1, 0:512], ones_cf[:, :],
                                        sqs[:, qs * 512:(qs + 1) * 512],
                                        start=(oc2 == 0), stop=(oc2 == NCD - 1),
                                        skip_group_check=True),
                                        wait=wN + [(SE, msq_f)])
                                    if oc_f == NCD - 1:
                                        mrq = v(lambda e: e.reduce_max(
                                            out=csc[0:1, 2:3], in_=pS[0:1, 0:512],
                                            axis=AX.X),
                                            wait=[(PE, nrm_pe[qs_f]), (VE, csc_ve[0])])
                                        used("S", VE, mrq)
                                        if grp == 0:
                                            csc_ve[0] = v(lambda e: e.tensor_copy(
                                                out=csc[0:1, 0:1], in_=csc[0:1, 2:3]),
                                              wait=[(VE, mrq)])
                                        else:
                                            csc_ve[0] = v(lambda e: e.tensor_max(
                                                csc[0:1, 0:1], csc[0:1, 0:1],
                                                csc[0:1, 2:3]), wait=[(VE, mrq)])
                                else:
                                    for hh in range(2):
                                        r = hh * 64
                                        wN = take("S")
                                        nrm_pe[qs_f] = t(lambda e, qs=qs_f, r=r: e.matmul(
                                            pS[0:1, 0:512], ones_cf[r:r + 64, :],
                                            sqs[r:r + 64, qs * 512:(qs + 1) * 512],
                                            start=True, stop=True,
                                            tile_position=(r, 0)),
                                            wait=wN + [(SE, msq_f)])
                                        mrq = v(lambda e: e.reduce_max(
                                            out=csc[0:1, 2:3], in_=pS[0:1, 0:512],
                                            axis=AX.X),
                                            wait=[(PE, nrm_pe[qs_f]), (VE, csc_ve[0])])
                                        used("S", VE, mrq)
                                        if grp == 0 and oc_f == 0 and hh == 0:
                                            csc_ve[0] = v(lambda e: e.tensor_copy(
                                                out=csc[0:1, 0:1], in_=csc[0:1, 2:3]),
                                              wait=[(VE, mrq)])
                                        else:
                                            csc_ve[0] = v(lambda e: e.tensor_max(
                                                csc[0:1, 0:1], csc[0:1, 0:1],
                                                csc[0:1, 2:3]), wait=[(VE, mrq)])
                        else:
                            ks = oc % 2
                            mevk = v(lambda e, ks=ks, reg=reg: e.tensor_copy(
                                out=kst[:, ks * 512:(ks + 1) * 512], in_=reg_ap(reg)),
                                wait=[(PE, mm), (sKS[ks], kst_use[ks])])
                            used(reg, VE, mevk)
                            msk = s_(lambda e, ks=ks: e.activation(
                                out=sqs[:, ks * 512:(ks + 1) * 512],
                                in_=kst[:, ks * 512:(ks + 1) * 512], func=AF.Square),
                                wait=[(VE, mevk), (PE, knrm_pe[ks])])
                            kpend.append((ks, msk, oc))
                            if oc == NCD - 1:
                                flushn = list(kpend)
                                kpend.clear()
                            elif len(kpend) >= 2:
                                flushn = [kpend.pop(0)]
                            else:
                                flushn = []
                            for ks_f, msk_f, oc_f in flushn:
                                if kind == "a":
                                    wN = take("B0") if oc_f == 0 else []
                                    knrm_pe[ks_f] = t(lambda e, ks=ks_f, oc2=oc_f: e.matmul(
                                        psB[0:1, 0:512], ones_cf[:, :],
                                        sqs[:, ks * 512:(ks + 1) * 512],
                                        start=(oc2 == 0), stop=(oc2 == NCD - 1),
                                        skip_group_check=True),
                                        wait=wN + [(SE, msk_f)])
                                    if oc_f == NCD - 1:
                                        mrk2 = v(lambda e: e.reduce_max(
                                            out=csc[0:1, 2:3], in_=psB[0:1, 0:512],
                                            axis=AX.X),
                                            wait=[(PE, knrm_pe[ks_f]), (VE, csc_ve[0])])
                                        used("B0", VE, mrk2)
                                        if grp == 0:
                                            csc_ve[0] = v(lambda e: e.tensor_copy(
                                                out=csc[0:1, 1:2], in_=csc[0:1, 2:3]),
                                              wait=[(VE, mrk2)])
                                        else:
                                            csc_ve[0] = v(lambda e: e.tensor_max(
                                                csc[0:1, 1:2], csc[0:1, 1:2],
                                                csc[0:1, 2:3]), wait=[(VE, mrk2)])
                                else:
                                    for hh in range(2):
                                        r = hh * 64
                                        wN = take("B0")
                                        knrm_pe[ks_f] = t(lambda e, ks=ks_f, r=r: e.matmul(
                                            psB[0:1, 0:512], ones_cf[r:r + 64, :],
                                            sqs[r:r + 64, ks * 512:(ks + 1) * 512],
                                            start=True, stop=True,
                                            tile_position=(r, 0)),
                                            wait=wN + [(SE, msk_f)])
                                        mrk2 = v(lambda e: e.reduce_max(
                                            out=csc[0:1, 2:3], in_=psB[0:1, 0:512],
                                            axis=AX.X),
                                            wait=[(PE, knrm_pe[ks_f]), (VE, csc_ve[0])])
                                        used("B0", VE, mrk2)
                                        if grp == 0 and oc_f == 0 and hh == 0:
                                            csc_ve[0] = v(lambda e: e.tensor_copy(
                                                out=csc[0:1, 1:2], in_=csc[0:1, 2:3]),
                                              wait=[(VE, mrk2)])
                                        else:
                                            csc_ve[0] = v(lambda e: e.tensor_max(
                                                csc[0:1, 1:2], csc[0:1, 1:2],
                                                csc[0:1, 2:3]), wait=[(VE, mrk2)])
                            if kind == "a":
                                kst_use[ks] = dq(sKS[ks], lambda e, oc=oc, grp=grp, ks=ks: e.dma_start(
                                    out=kbufA[grp * 4:(grp + 1) * 4, :,
                                              oc * 128:(oc + 1) * 128].rearrange(
                                                  "j u c -> u j c"),
                                    in_=kst[:, ks * 512:(ks + 1) * 512].rearrange(
                                        "u (j c) -> u j c", j=4)),
                                    wait=[(VE, mevk)])
                            else:
                                dq(sKS[ks], lambda e, oc=oc, grp=grp, ks=ks: e.dma_start(
                                    out=kbufB[2 * oc, :, grp * 512:(grp + 1) * 512],
                                    in_=kst[0:64, ks * 512:(ks + 1) * 512]),
                                    wait=[(VE, mevk)])
                                kst_use[ks] = dq(sKS[ks], lambda e, oc=oc, grp=grp, ks=ks: e.dma_start(
                                    out=kbufB[2 * oc + 1, :, grp * 512:(grp + 1) * 512],
                                    in_=kst[64:128, ks * 512:(ks + 1) * 512]))
                for st4 in range(4):
                    tile = grp * 4 + st4
                    mvv = 0
                    for half in range(2):
                        reg = ["A2", "A3"][half]
                        ww = take(reg) + [(VE, mev), (sWP, m_wp),
                                          (sVS, vst_use), (GC, mvo2)]
                        fns = []
                        for c in range(NCD):
                            fns.append(lambda e, c=c, st4=st4, half=half, gb=gb, reg=reg: e.matmul(
                                reg_ap(reg),
                                xln_fm[:, gb * 4096 + c * 512 + st4 * 128:
                                       gb * 4096 + c * 512 + st4 * 128 + 128],
                                wproj[:, (2 * NCD + c) * 1024 + half * 512:
                                      (2 * NCD + c) * 1024 + half * 512 + 512],
                                start=(c == 0), stop=(c == NCD - 1)))
                        mm = tq(fns, wait=ww)
                        if kind == "a":
                            for cc in range(4):
                                c = half * 4 + cc
                                mvv = v(lambda e, c=c, cc=cc, reg=reg: e.tensor_copy(
                                    out=vst[:, c * 130:c * 130 + 128],
                                    in_=reg_ap(reg)[:, cc * 128:(cc + 1) * 128]),
                                    wait=[(PE, mm)] if cc == 0 else ())
                        else:
                            for hh in range(8):
                                h = half * 8 + hh
                                pr, u = h // 2, (h % 2) * 65
                                mvv = v(lambda e, pr=pr, u=u, hh=hh, reg=reg: e.tensor_copy(
                                    out=vst[:, pr * 130 + u:pr * 130 + u + 64],
                                    in_=reg_ap(reg)[:, hh * 64:(hh + 1) * 64]),
                                    wait=[(PE, mm)] if hh == 0 else ())
                        used(reg, VE, mvv)
                    if kind == "a":
                        vst_use = dq(sVS, lambda e, tile=tile: e.dma_start(
                            out=vbufA[tile], in_=vst[:, :]), wait=[(VE, mvv)])
                    else:
                        vst_use = dq(sVS, lambda e, tile=tile: e.dma_start(
                            out=vbufB[:, tile * 128:(tile + 1) * 128, :].rearrange(
                                "pr u c -> u pr c"),
                            in_=vst.rearrange("u (pr c) -> u pr c", c=130)),
                            wait=[(VE, mvv)])
                xln_fm_p[gb] = mm
                if grp == 1:
                    p._early_ve = VE.n
            msq2 = s_(lambda e: e.activation(out=csc[0:1, 0:2], in_=csc[0:1, 0:2],
                                             func=AF.Sqrt),
                      wait=[(VE, VE.n)])
            mcm = v(lambda e: e.tensor_mul(csc[0:1, 3:4], csc[0:1, 0:1],
                                           csc[0:1, 1:2]), wait=[(SE, msq2)])
            mcm = v(lambda e: e.tensor_scalar_mul(csc[0:1, 3:4], csc[0:1, 3:4],
                                                  -sc), wait=[(VE, mcm)])
            wS = take("S")
            mt2 = t(lambda e: e.matmul(
                pS[:, 0:1], ones_r[:, :], csc[0:1, 3:4], start=True, stop=True),
                wait=wS + [(VE, mcm)])
            mng = v(lambda e: e.tensor_copy(out=negC[:, :], in_=pS[:, 0:1]),
                    wait=[(PE, mt2)])
            used("S", VE, mng)
            m_proj_pe = PE.n
            m_proj_done_v = VE.n
            mq_all = [(s, s.n) for s in sQS]
            mk_all = [(s, s.n) for s in sKS]
            mv_all = [(sVS, sVS.n)]
            if kind == "b":
                mwo = gq(sWP, lambda e: e.dma_start(
                    out=wproj.rearrange("p (i c o) -> p i c o", i=3, c=NCD)[:, 0],
                    in_=wo_b.rearrange("(c p) o -> p c o", p=128)),
                    wait=[(PE, m_proj_pe)])
            else:
                mwo = sWP.n

            tail = dict(xres=[0, 0], outst=[0, 0])
            kt_pe = [0, 0, 0, 0]
            vt_pe = [0, 0, 0, 0]
            kp_pe = 0
            vp_pe = 0
            kp_use = [0, 0]
            vp_use = [0, 0]
            zev_use = [0, 0]
            expT_pe = 0
            expT_slots = [0, 0, 0, 0, 0, 0]
            qb_prev = 0
            for b in range(NBLK):
                nj = 4 * (b + 1)
                mqb = dq(sQB, lambda e, b=b: e.dma_start(
                    out=qgrp.ap().rearrange("p (c u) -> p c u", c=NCD),
                    in_=qbuf[:, :, b * 512:(b + 1) * 512].rearrange(
                        "c p u -> p c u")),
                    wait=mq_all + [(PE, qb_prev)])

                if kind == "a":
                    mex_last = 0
                    msum = 0
                    for j in range(nj):
                        kb = j % 4
                        mkl = dq(sKT[kb], lambda e, j=j, kb=kb: e.dma_start(
                            out=ktile[kb], in_=kbufA[j]),
                            wait=mk_all + [(PE, kt_pe[kb])])
                        reg = ["B0", "B1"][j % 2]
                        wB = take(reg)
                        fns = []
                        for c in range(NCD):
                            fns.append(lambda e, c=c, kb=kb, reg=reg: e.matmul(
                                reg_ap(reg),
                                ktile[kb][:, c * 128:(c + 1) * 128],
                                qgrp[:, c * 512:(c + 1) * 512],
                                start=(c == 0), stop=(c == NCD - 1)))
                        mm = tq(fns, wait=wB + [(sKT[kb], mkl), (sQB, mqb),
                                                (VE, m_proj_done_v)])
                        kt_pe[kb] = mm
                        exw = [(PE, mm), (VE, mng), (PE, expT_pe), (sZR, m_zr)]
                        if j >= nj - 4:
                            tt = j - (nj - 4)
                            mmk = v(lambda e, reg=reg, tt=tt: e.tensor_add(
                                reg_ap(reg), reg_ap(reg),
                                cmask[:, tt * QBLK:(tt + 1) * QBLK]),
                                wait=[(PE, mm)])
                            mex = s_(lambda e, j=j, reg=reg: e.activation(
                                out=expT[:, j * 512:(j + 1) * 512], in_=reg_ap(reg),
                                func=AF.Exp, bias=negC[:, :], scale=sc),
                                wait=exw + [(VE, mmk)])
                        else:
                            mex = s_(lambda e, j=j, reg=reg: e.activation(
                                out=expT[:, j * 512:(j + 1) * 512], in_=reg_ap(reg),
                                func=AF.Exp, bias=negC[:, :], scale=sc),
                                wait=exw)
                        used(reg, SE, mex)
                        mex_last = mex
                    wS = take("S")
                    fns = []
                    for j in range(nj):
                        fns.append(lambda e, j=j, nj=nj: e.matmul(
                            pS[0:1, 0:512], ones_cb[:, :],
                            expT[:, j * 512:(j + 1) * 512],
                            start=(j == 0), stop=(j == nj - 1)))
                    msum = tq(fns, wait=wS + [(SE, mex_last)])
                    mfq = v(lambda e: e.reciprocal(out=fq[:, :], in_=pS[0:1, 0:512]),
                            wait=[(PE, msum)])
                    used("S", VE, mfq)
                    wB = take("B0")
                    mnb = t(lambda e: e.matmul(
                        psB[:, 0:512], ones_r[:, :], fq[:, :], start=True, stop=True),
                        wait=wB + [(VE, mfq)])
                    mnbe = v(lambda e: e.tensor_copy(out=normB[:, :], in_=psB[:, 0:512]),
                             wait=[(PE, mnb)])
                    used("B0", VE, mnbe)
                    m_z = 0
                    for hf in range(2):
                        wA = take("A0") + take("A1") + take("A2") + take("A3")
                        mm = None
                        for j in range(nj):
                            vb = j % 4
                            mvl = dq(sVT[vb], lambda e, j=j, vb=vb: e.dma_start(
                                out=vtile[vb], in_=vbufA[j]),
                                wait=mv_all + [(PE, vt_pe[vb])])
                            fns = []
                            for q_ in range(4):
                                oc = hf * 4 + q_
                                fns.append(lambda e, q_=q_, oc=oc, vb=vb, j=j, nj=nj: e.matmul(
                                    psA[:, q_ * 512:(q_ + 1) * 512],
                                    vtile[vb][:, oc * 130:oc * 130 + 128],
                                    expT[:, j * 512:(j + 1) * 512],
                                    start=(j == 0), stop=(j == nj - 1),
                                    skip_group_check=True))
                            mm = tq(fns, wait=(wA + [(SE, mex_last), (sVT[vb], mvl)])
                                    if j == 0 else [(sVT[vb], mvl)])
                            vt_pe[vb] = mm
                        expT_pe = mm
                        for q_ in range(4):
                            oc = hf * 4 + q_
                            mz = v(lambda e, q_=q_, oc=oc: e.tensor_mul(
                                zst[:, oc * 512:(oc + 1) * 512],
                                psA[:, q_ * 512:(q_ + 1) * 512], normB[:, :]),
                                wait=[(PE, mm), (VE, mnbe)])
                            used(f"A{q_}", VE, mz)
                            m_z = mz
                    zsrc = zst
                else:
                    m_z = 0

                    def pair_load(hp):
                        sl = hp % 2
                        dq(sKP2[sl], lambda e, hp=hp, nj=nj, sl=sl: e.dma_start(
                            out=kpair2[sl][0:64, 0:nj * 128],
                            in_=kbufB[2 * hp, :, 0:nj * 128]),
                            wait=mk_all + [(PE, kp_use[sl])])
                        dq(sKP2[sl], lambda e, hp=hp, nj=nj, sl=sl: e.dma_start(
                            out=kpair2[sl][64:128, 0:nj * 128],
                            in_=kbufB[2 * hp + 1, :, 0:nj * 128]))
                        mvp = dq(sVP2[sl], lambda e, hp=hp, nj=nj, sl=sl: e.dma_start(
                            out=vpair2[sl][:, 0:nj * 130],
                            in_=vbufB[hp, 0:nj * 128, :].rearrange(
                                "(j u) c -> u j c", u=128),
                            allow_slow_non_contiguous=True),
                            wait=mv_all + [(PE, vp_use[sl])])
                        return sKP2[sl].n, mvp

                    pl = {0: pair_load(0)}
                    pending_norm = [None]
                    for hp in range(8):
                        sl = hp % 2
                        if hp + 1 < 8:
                            pl[hp + 1] = pair_load(hp + 1)
                        mkp, mvp = pl[hp]
                        kpair = kpair2[sl]
                        vpair = vpair2[sl]
                        av_done = [0, 0]
                        pgs = [("B0", "B1", psB, 0), ("A2", "A3", psA, 1024)]
                        sc_mm = [0] * nj

                        def b_scores(j, hp=hp, mkp=mkp, mvp=mvp, kp=kpair):
                            g0, g1, pst, off = pgs[j % 2]
                            wB = take(g0) + take(g1)
                            fns = []
                            for hh in range(2):
                                r = hh * 64
                                fns.append(lambda e, j=j, r=r, hp=hp, pst=pst,
                                           off=off, kp=kp: e.matmul(
                                    pst[:, off + (r // 64) * 512:
                                        off + (r // 64) * 512 + 512],
                                    kp[r:r + 64, j * 128:(j + 1) * 128],
                                    qgrp[r:r + 64, hp * 512:(hp + 1) * 512],
                                    start=True, stop=True, tile_position=(r, 0)))
                            sc_mm[j] = tq(fns, wait=wB + [(sKP2[hp % 2], mkp),
                                                          (sVP2[hp % 2], mvp),
                                                          (sQB, mqb)])

                        def b_expav(j, hp=hp, nj=nj, vp=vpair):
                            nonlocal kp_pe, m_exp_last
                            g0, g1, pst, off = pgs[j % 2]
                            slot = j % 6
                            exw = [(PE, sc_mm[j]), (VE, mng),
                                   (PE, expT_slots[slot]), (sZR, m_zr)]
                            if j >= nj - 4:
                                tt = j - (nj - 4)
                                mmk = 0
                                for hh in range(2):
                                    mmk = v(lambda e, pst=pst, off=off, hh=hh,
                                            tt=tt: e.tensor_add(
                                        pst[:, off + hh * 512:off + hh * 512 + 512],
                                        pst[:, off + hh * 512:off + hh * 512 + 512],
                                        cmask[:, tt * QBLK:(tt + 1) * QBLK]),
                                        wait=[(PE, sc_mm[j])] if hh == 0 else [(VE, mmk)])
                                mex = s_(lambda e, slot=slot, pst=pst, off=off: e.activation(
                                    out=expT[:, slot * 1024:slot * 1024 + 1024],
                                    in_=pst[:, off:off + 1024], func=AF.Exp,
                                    bias=negC[:, :], scale=sc),
                                    wait=exw + [(VE, mmk)])
                            else:
                                mex = s_(lambda e, slot=slot, pst=pst, off=off: e.activation(
                                    out=expT[:, slot * 1024:slot * 1024 + 1024],
                                    in_=pst[:, off:off + 1024], func=AF.Exp,
                                    bias=negC[:, :], scale=sc),
                                    wait=exw)
                            used(g0, SE, mex)
                            used(g1, SE, mex)
                            for hh in range(2):
                                zreg = ["A0", "A1"][hh]
                                wz = take(zreg) if j == 0 else []
                                mav = p.emit("tensor", lambda e, j=j, hh=hh,
                                             slot=slot, zreg=zreg, nj=nj,
                                             vp=vp: e.matmul(
                                    reg_ap(zreg)[0:65, :],
                                    vp[:, j * 130 + hh * 65:j * 130 + hh * 65 + 65],
                                    expT[:, slot * 1024 + hh * 512:
                                         slot * 1024 + hh * 512 + 512],
                                    start=(j == 0), stop=(j == nj - 1),
                                    skip_group_check=True),
                                    wait=(wz + [(SE, mex)]),
                                    inc=[(PE, 1)])[0]
                                av_done[hh] = mav
                            expT_slots[slot] = av_done[1]
                            kp_pe = av_done[1]

                        m_exp_last = 0
                        for j in range(nj):
                            b_scores(j)
                            if j == 1 and pending_norm[0] is not None:
                                pending_norm[0]()
                                pending_norm[0] = None
                            if j >= 1:
                                b_expav(j - 1)
                        b_expav(nj - 1)
                        kp_use[sl] = av_done[1]
                        vp_use[sl] = av_done[1]
                        zsl = hp % 2
                        mze_h = [0, 0]
                        for hh in range(2):
                            zreg = ["A0", "A1"][hh]
                            mze = v(lambda e, zsl=zsl, hh=hh, zreg=zreg: e.tensor_copy(
                                out=xpan[0:65, zsl * 1024 + hh * 512:
                                         zsl * 1024 + hh * 512 + 512],
                                in_=reg_ap(zreg)[0:65, :]),
                                wait=[(PE, av_done[hh]), (VE, zev_use[zsl])])
                            used(zreg, VE, mze)
                            mze_h[hh] = mze

                        def do_norm(hp=hp, zsl=zsl, mze_h=mze_h):
                            mzl = 0
                            for hh in range(2):
                                h = hp * 2 + hh
                                c, r = h // 2, (h % 2) * 64
                                mfq = v(lambda e, zsl=zsl, hh=hh: e.reciprocal(
                                    out=fq[:, :],
                                    in_=xpan[64:65, zsl * 1024 + hh * 512:
                                             zsl * 1024 + hh * 512 + 512]),
                                    wait=[(VE, mze_h[hh])])
                                wB = take("S")
                                mnb = t(lambda e: e.matmul(
                                    pS[0:64, 0:512], ones_r[:, 0:64], fq[:, :],
                                    start=True, stop=True),
                                    wait=wB + [(VE, mfq)])
                                mnbe = v(lambda e: e.tensor_copy(
                                    out=normB[0:64, :], in_=pS[0:64, 0:512]),
                                    wait=[(PE, mnb)])
                                used("S", VE, mnbe)
                                mzl = v(lambda e, c=c, r=r, zsl=zsl, hh=hh: e.tensor_mul(
                                    zst[r:r + 64, c * 512:(c + 1) * 512],
                                    xpan[0:64, zsl * 1024 + hh * 512:
                                         zsl * 1024 + hh * 512 + 512],
                                    normB[0:64, :]),
                                    wait=[(VE, mnbe)])
                            zev_use[zsl] = mzl
                            return mzl

                        if hp < 7:
                            pending_norm[0] = do_norm
                        else:
                            m_z = do_norm()
                    mzo = 0
                    for dc in range(NCD):
                        reg = ["B0", "B1"][dc % 2]
                        wB = take(reg)
                        fns = []
                        for c in range(NCD):
                            fns.append(lambda e, c=c, dc=dc, reg=reg: e.matmul(
                                reg_ap(reg),
                                wproj[:, c * 1024 + dc * 128:c * 1024 + dc * 128 + 128],
                                zst[:, c * 512:(c + 1) * 512],
                                start=(c == 0), stop=(c == NCD - 1)))
                        mm = tq(fns, wait=wB + [(VE, m_z), (sWP, mwo)])
                        mzo = v(lambda e, dc=dc, reg=reg: e.tensor_scalar_add(
                            zst2[:, dc * 512:(dc + 1) * 512], reg_ap(reg),
                            bo_sb[:, dc:dc + 1]),
                            wait=[(PE, mm), (sBO, 16)])
                        used(reg, VE, mzo)
                    m_z = mzo
                    zsrc = zst2

                for po in range(4):
                    pi = b * 4 + po
                    tb = po % 2
                    mlr = dq(sXR[tb], lambda e, pi=pi, tb=tb: e.dma_start(
                        out=xres[tb], in_=xsrc[pi * 128:(pi + 1) * 128, :]),
                        wait=[(VE, tail["xres"][tb])])
                    wT = take("T")
                    for c in range(NCD):
                        mtl = p.emit("tensor", lambda e, c=c, po=po, zsrc=zsrc: e.transpose(
                            pT[:, c * 128:(c + 1) * 128],
                            zsrc[:, c * 512 + po * 128:c * 512 + po * 128 + 128],
                            ident_bf[:, :]),
                            wait=(wT + [(VE, m_z)]) if c == 0 else (),
                            inc=[(PE, 1)] if c == NCD - 1 else [])
                    mtr = mtl[0]
                    mad = v(lambda e, tb=tb: e.tensor_add(
                        outst[tb], pT[:, :], xres[tb]),
                        wait=[(PE, mtr), (sXR[tb], mlr), (sOS[tb], tail["outst"][tb])])
                    used("T", VE, mad)
                    tail["xres"][tb] = mad
                    tail["outst"][tb] = dq(sOS[tb], lambda e, pi=pi, tb=tb: e.dma_start(
                        out=xdst[pi * 128:(pi + 1) * 128, :], in_=outst[tb]),
                        wait=[(VE, mad)])
                qb_prev = PE.n

            if sub == 2:
                prev = [0, 0, 0, 0]
                for e_i in range(NE):
                    ch = e_i % 4
                    for pc in range(8):
                        prev[ch] = gq(sCC[ch], lambda e, e_i=e_i, pc=pc: e.dma_start(
                            out=w1b[e_i, pc * 128:(pc + 1) * 128, :],
                            in_=w1[e_i, pc * 128:(pc + 1) * 128, :]),
                            wait=[(sCC[ch], prev[ch]), (PE, cast_go[0])])
                    for pc in range(8):
                        prev[ch] = gq(sCC[ch], lambda e, e_i=e_i, pc=pc: e.dma_start(
                            out=w2b[e_i, pc * 512:(pc + 1) * 512, :],
                            in_=w2[e_i, pc * 512:(pc + 1) * 512, :]),
                            wait=[(sCC[ch], prev[ch]), (PE, cast_go[0])])

        def emit_ffn(sub):
            xsrc = resid[sub]
            xdst = resid[sub + 1]
            bfw = (sub == 3)
            mxe_prev = [0]
            rt_se = [0]
            v(lambda e: e.memset(rrec[:, :], 0.0))
            fW1 = sW1 if bfw else sW1c
            fW2 = sW2 if bfw else sW2c
            mgw = dq(sGW, lambda e: e.dma_start(
                out=gate_sb.ap().rearrange("p (c e2) -> p c e2", c=NCD),
                in_=gate_td.rearrange("(c p) e2 -> p c e2", p=128)))
            mz0 = v(lambda e: e.memset(offrow[:, :], 0.0))
            wS = take("S")
            fns = []
            for c in range(NCD):
                fns.append(lambda e, c=c: e.matmul(
                    pS[0:1, 0:NE], ones_cf[:, :], gate_sb[:, c * NE:(c + 1) * NE],
                    start=(c == 0), stop=(c == NCD - 1)))
            mgs = tq(fns, wait=wS + [(sGW, mgw)])
            mge = v(lambda e: e.tensor_copy(out=gsrow[:, :], in_=pS[0:1, 0:NE]),
                    wait=[(PE, mgs)])
            used("S", VE, mge)
            wS = take("S")
            mgb = t(lambda e: e.matmul(
                pS[:, 0:NE], ones_r[:, :], gsrow[:, :], start=True, stop=True),
                wait=wS + [(VE, mge)])
            mgbe = v(lambda e: e.tensor_copy(out=gsumB[:, :], in_=pS[:, 0:NE]),
                     wait=[(PE, mgb)])
            used("S", VE, mgbe)
            gsum4 = ARF[:, 2240:2304]
            iota4 = ARF[:, 2304:2368]
            for k4 in range(4):
                mgbe = v(lambda e, k4=k4: e.tensor_copy(
                    out=gsum4[:, k4 * 16:(k4 + 1) * 16], in_=gsumB[:, :]),
                    wait=[(VE, mgbe)])
                miot = v(lambda e, k4=k4: e.tensor_copy(
                    out=iota4[:, k4 * 16:(k4 + 1) * 16], in_=iotaE[:, :]))
            if DEBUG_RT:
                dq(sDB, lambda e: e.dma_start(out=dbg[4], in_=gsum4[:, :]),
                   wait=[(VE, miot)])

            st = new_st()
            st["xln_dst"] = AR2[:, 0:4096]
            st["xln_slots"] = 4
            m_off = mz0
            mct_prev = 0
            rt_ve = 0
            for bi in range(8):
                pi0 = 4 * bi
                sc_ms = []
                gate_pe = []
                for po in range(4):
                    pi = pi0 + po
                    pb_ = pi % 2
                    ml, mv2, mv6, mv7 = emit_ln_panel(xsrc, pi, pb_, st)
                    sc_ms.append(mv7)
                    mv8 = v(lambda e, pi=pi: e.tensor_copy(
                        out=rrec[:, 2 * pi:2 * pi + 1], in_=lnst[:, 0:1]),
                        wait=[(VE, mv2)])
                    mv9 = v(lambda e, pi=pi: e.tensor_copy(
                        out=rrec[:, 2 * pi + 1:2 * pi + 2], in_=lnst[:, 3:4]),
                        wait=[(VE, mv6)])
                    for rnd in range(2):
                        wB = take("B0")
                        for cc in range(4):
                            c = rnd * 4 + cc
                            mtl = p.emit("tensor", lambda e, c=c, cc=cc, pb_=pb_: e.transpose(
                                psB[:, cc * 128:(cc + 1) * 128],
                                xpan[:, pb_ * D + c * 128:pb_ * D + c * 128 + 128],
                                ident_f32[:, :]),
                                wait=(wB + [(sXP[pb_], ml)]) if cc == 0 else (),
                                inc=[(PE, 1)] if cc == 3 else [])
                        mtr = mtl[0]
                        mxe = v(lambda e, rnd=rnd: e.tensor_copy(
                            out=xfm[:, rnd * 512:(rnd + 1) * 512], in_=psB[:, 0:512]),
                            wait=[(PE, mtr), (VE, mxe_prev[0])])
                        used("B0", VE, mxe)
                    mxe_prev[0] = mxe
                    st["xpan_p"][pb_] = mtr
                    st["xpan_v"][pb_] = max(st["xpan_v"][pb_], mxe)
                    wS = take("S") if po == 0 else []
                    fns = []
                    for c in range(NCD):
                        fns.append(lambda e, c=c, po=po: e.matmul(
                            pS[:, po * 16:(po + 1) * 16],
                            xfm[:, c * 128:(c + 1) * 128],
                            gate_sb[:, c * NE:(c + 1) * NE],
                            start=(c == 0), stop=(c == NCD - 1),
                            skip_group_check=True))
                    gate_pe.append(tq(fns, wait=wS + [(VE, mxe), (VE, mgbe)]))
                if DEBUG_RT and bi == 0:
                    dq(sDB, lambda e: e.dma_start(out=dbg_x, in_=xfm[:, :]),
                       wait=[(VE, mxe)])
                    mdp = v(lambda e: e.tensor_copy(
                        out=ARF[:, 2368:2432], in_=pS[:, 0:64]),
                        wait=[(PE, gate_pe[3])])
                    used("S", VE, mdp)
                    dq(sDB, lambda e: e.dma_start(
                        out=dbg[0], in_=ARF[:, 2368:2432]), wait=[(VE, mdp)])
                def bview(col0, rep=16):
                    return bass.AP(rrec.ap().tensor, 2 * pi0 + col0,
                                   [[2 * NP, 128], [2, 4], [0, rep]])
                mb1 = v(lambda e, bv=bview(0): e.tensor_tensor(
                    out=routg[:, 0:64].rearrange("p (g r) -> p g r", g=4),
                    in0=gsum4.rearrange("p (g r) -> p g r", g=4),
                    in1=bv, op=ALU.mult),
                    wait=[(VE, VE.n), (PE, mct_prev)])
                if DEBUG_RT and bi == 0:
                    mdp1 = v(lambda e: e.tensor_copy(
                        out=ARF[:, 2432:2496], in_=routg[:, 0:64]),
                        wait=[(VE, mb1)])
                    dq(sDB, lambda e: e.dma_start(
                        out=dbg[1], in_=ARF[:, 2432:2496]), wait=[(VE, mdp1)])
                mb2 = v(lambda e: e.tensor_sub(
                    routg[:, 64:128].rearrange("p (g r) -> p g r", g=4),
                    bass.AP(pS.ap().tensor, 0, [[512, 128], [16, 4], [1, 16]]),
                    routg[:, 0:64].rearrange("p (g r) -> p g r", g=4)),
                    wait=[(PE, gate_pe[3]), (VE, mb1)])
                used("S", VE, mb2)
                mb3 = v(lambda e, bv=bview(1): e.tensor_tensor(
                    out=routg[:, 64:128].rearrange("p (g r) -> p g r", g=4),
                    in0=routg[:, 64:128].rearrange("p (g r) -> p g r", g=4),
                    in1=bv, op=ALU.mult), wait=[(VE, mb2)])
                if DEBUG_RT and bi == 0:
                    mdp3 = v(lambda e: e.tensor_copy(
                        out=ARF[:, 2496:2560], in_=routg[:, 64:128]),
                        wait=[(VE, mb3)])
                    dq(sDB, lambda e: e.dma_start(
                        out=dbg[2], in_=ARF[:, 2496:2560]), wait=[(VE, mdp3)])
                mb4 = s_(lambda e: e.activation(
                    out=routg[:, 128:192], in_=routg[:, 64:128], func=AF.Exp),
                    wait=[(VE, mb3), (SE, rt_se[0])])
                rt_se[0] = mb4
                mb5 = v(lambda e: e.reduce_max(
                    out=rout1[:, 0:4], in_=routg[:, 64:128].rearrange(
                        "p (g r) -> p g r", g=4), axis=AX.X), wait=[(VE, mb3)])
                mb6 = v(lambda e: e.reduce_max(
                    out=rout1[:, 4:8], in_=routg[:, 128:192].rearrange(
                        "p (g r) -> p g r", g=4), axis=AX.X), wait=[(SE, mb4)])
                mb7 = v(lambda e: e.reduce_sum(
                    out=rout1[:, 8:12], in_=routg[:, 128:192].rearrange(
                        "p (g r) -> p g r", g=4), axis=AX.X), wait=[(SE, mb4)])
                mb8 = v(lambda e: e.reciprocal(
                    out=rout1[:, 12:16], in_=rout1[:, 8:12]), wait=[(VE, mb7)])
                mb9 = v(lambda e: e.tensor_mul(
                    rout1[:, 16:20], rout1[:, 4:8], rout1[:, 12:16]),
                    wait=[(VE, mb8), (VE, mb6)])
                dbgw12 = []
                if DEBUG_RT and bi == 0:
                    dq(sDB, lambda e: e.dma_start(
                        out=dbg[3, :, 0:20], in_=rout1[:, 0:20]),
                        wait=[(VE, mb9)])
                    dbgw12 = [(sDB, sDB.n)]
                mb10 = v(lambda e: e.tensor_tensor(
                    out=routg[:, 0:64].rearrange("p (g r) -> p g r", g=4),
                    in0=routg[:, 64:128].rearrange("p (g r) -> p g r", g=4),
                    in1=bass.AP(rout1.ap().tensor, 0, [[32, 128], [1, 4], [0, 16]]),
                    op=ALU.is_equal), wait=[(VE, mb5), (VE, mb9)])
                rank_pe = []
                for po in range(4):
                    wS2 = []
                    p.emit("tensor", lambda e, po=po: e.matmul(
                        pS[:, 64 + po * 16:64 + (po + 1) * 16], ustrict[:, :],
                        routg[:, po * 16:(po + 1) * 16],
                        start=True, stop=False, skip_group_check=True),
                        wait=[(VE, mb10), (VE, m_off)] if po == 0 else [(VE, m_off)])
                    mrk = t(lambda e, po=po: e.matmul(
                        pS[:, 64 + po * 16:64 + (po + 1) * 16], ones_r[:, :],
                        offrow[:, :], start=False, stop=True,
                        skip_group_check=True))
                    mct = t(lambda e, po=po: e.matmul(
                        pS[0:1, 128 + po * 16:128 + (po + 1) * 16], ones_cf[:, :],
                        routg[:, po * 16:(po + 1) * 16],
                        start=True, stop=True, skip_group_check=True))
                    m_off = v(lambda e, po=po: e.tensor_add(
                        offrow[:, :], offrow[:, :],
                        pS[0:1, 128 + po * 16:128 + (po + 1) * 16]),
                        wait=[(PE, mct)])
                    used("S", VE, m_off)
                    rank_pe.append(mrk)
                    mct_prev = mct
                mb11 = v(lambda e: e.tensor_mul(
                    routg[:, 128:192].rearrange("p (g r) -> p g r", g=4),
                    routg[:, 0:64].rearrange("p (g r) -> p g r", g=4),
                    pS.ap().rearrange("p (g r) -> p g r", g=4)[:, :, 4:20][:, :, 0:16]
                    if False else
                    bass.AP(pS.ap().tensor, 64, [[512, 128], [16, 4], [1, 16]])),
                    wait=[(PE, rank_pe[3]), (VE, mb9)])
                used("S", VE, mb11)
                mb12 = v(lambda e: e.reduce_sum(
                    out=rout1[:, 4:8], in_=routg[:, 128:192].rearrange(
                        "p (g r) -> p g r", g=4), axis=AX.X),
                    wait=[(VE, mb11)] + dbgw12)
                mb13 = v(lambda e: e.tensor_tensor(
                    out=routg[:, 128:192].rearrange("p (g r) -> p g r", g=4),
                    in0=routg[:, 0:64].rearrange("p (g r) -> p g r", g=4),
                    in1=iota4.rearrange("p (g r) -> p g r", g=4),
                    op=ALU.mult), wait=[(VE, mb12)])
                mb14 = v(lambda e: e.reduce_sum(
                    out=rout1[:, 8:12], in_=routg[:, 128:192].rearrange(
                        "p (g r) -> p g r", g=4), axis=AX.X), wait=[(VE, mb13)])
                mb15 = v(lambda e: e.tensor_scalar(
                    out=rout1[:, 12:16], in0=rout1[:, 4:8],
                    scalar1=float(LOCAL_CAP), scalar2=None, op0=ALU.is_lt),
                    wait=[(VE, mb12)])
                mb16 = v(lambda e, pi0=pi0: e.tensor_mul(
                    wtok[:, pi0:pi0 + 4], rout1[:, 16:20], rout1[:, 12:16]),
                    wait=[(VE, mb15)])
                mb17 = v(lambda e: e.scalar_tensor_tensor(
                    out=rout1[:, 20:24], in0=rout1[:, 8:12],
                    scalar=float(LOCAL_CAP), in1=rout1[:, 4:8],
                    op0=ALU.mult, op1=ALU.add), wait=[(VE, mb14), (VE, mb15)])
                mb18 = v(lambda e: e.scalar_tensor_tensor(
                    out=rout1[:, 20:24], in0=rout1[:, 20:24],
                    scalar=-float(NSLOT), in1=rout1[:, 12:16],
                    op0=ALU.add, op1=ALU.mult), wait=[(VE, mb17)])
                mb19 = v(lambda e: e.tensor_scalar_add(
                    rout1[:, 20:24], rout1[:, 20:24], float(NSLOT)),
                    wait=[(VE, mb18)])
                mb20 = v(lambda e, pi0=pi0: e.tensor_copy(
                    out=sloti[:, pi0:pi0 + 4], in_=rout1[:, 20:24]),
                    wait=[(VE, mb19)])
                rt_ve = mb20
                if DEBUG_RT and bi == 0:
                    rt_ve = v(lambda e: e.tensor_copy(
                        out=rout1[0:1, 24:25], in_=rout1[0:1, 0:1]),
                        wait=[(sDB, sDB.n), (VE, mb20)])
                    mb20 = rt_ve
                for po in range(4):
                    pi = pi0 + po
                    st["xlt_g"][po] = gq(sSC[po], lambda e, pi=pi, po=po: e.indirect_dma_start(
                        out=disp,
                        out_offset=IndirectOffsetOnAxis(sloti[:, pi:pi + 1], 0),
                        in_=st["xln_dst"][:, po * D:(po + 1) * D], in_offset=None),
                        wait=[(VE, mb20), (VE, sc_ms[po]), (sZR, m_zr)])
            m_router_ve = VE.n
            m_scat = [(s, s.n) for s in sSC]

            m_w2_pe = 0
            m_xg_pe = 0
            m_yfm_pe = 0
            m_ytk_ld = 0
            w1_pe = [0, 0]
            mm = 0
            for e_i in range(NE):
                for k3 in range(3):
                    dq(sXG, lambda e, e_i=e_i, k3=k3: e.dma_start(
                        out=xg[:, k3 * D:(k3 + 1) * D],
                        in_=disp[e_i * LOCAL_CAP + k3 * 128:
                                 e_i * LOCAL_CAP + (k3 + 1) * 128, :]),
                        wait=m_scat + [(PE, m_xg_pe)])
                mgx = sXG.n
                mxt = 0
                mtr = 0
                for c in range(NCD):
                    wT = take("T")
                    for sp in range(3):
                        mtl = p.emit("tensor", lambda e, c=c, sp=sp: e.transpose(
                            pT[:, sp * 128:(sp + 1) * 128],
                            xg[:, sp * D + c * 128:sp * D + c * 128 + 128],
                            ident_bf[:, :]),
                            wait=(wT + [(sXG, mgx)]) if sp == 0 else (),
                            inc=[(PE, 1)] if sp == 2 else [])
                    mtr = mtl[0]
                    mxt = v(lambda e, c=c: e.tensor_copy(
                        out=xgT[:, c * LOCAL_CAP:(c + 1) * LOCAL_CAP],
                        in_=pT[:, 0:LOCAL_CAP]),
                        wait=[(PE, mtr)])
                    used("T", VE, mxt)
                m_xg_pe = mtr
                if bfw:
                    mw2 = dq(sW2, lambda e, e_i=e_i: e.dma_start(
                        out=w2sb.rearrange("p (c dd) -> p c dd", c=NFC),
                        in_=w2b[e_i].rearrange("(c p) dd -> p c dd", p=128)),
                        wait=[(PE, m_w2_pe), (sZR, m_zr),
                              (sCC[e_i % 4], 16 * 16 * (e_i // 4 + 1))])
                else:
                    mw2 = gq(sW2c, lambda e, e_i=e_i: e.dma_start(
                        out=w2sb.rearrange("p (c dd) -> p c dd", c=NFC),
                        in_=w2[e_i].rearrange("(c p) dd -> p c dd", p=128)),
                        wait=[(PE, m_w2_pe), (sZR, m_zr)])
                mh_done = 0
                for q4 in range(4):
                    wbq = q4 % 2
                    if bfw:
                        mw1 = dq(sW1[wbq], lambda e, e_i=e_i, q4=q4, wbq=wbq: e.dma_start(
                            out=w1sb[wbq].rearrange("p (c f) -> p c f", c=NCD),
                            in_=w1b[e_i, :, q4 * 1024:(q4 + 1) * 1024].rearrange(
                                "(c p) f -> p c f", p=128)),
                            wait=[(PE, w1_pe[wbq]), (sZR, m_zr),
                                  (sCC[e_i % 4], 16 * 16 * (e_i // 4 + 1))]
                            + (m_scat if e_i == 0 and q4 < 2 else []))
                    else:
                        mw1 = gq(sW1c[wbq], lambda e, e_i=e_i, q4=q4, wbq=wbq: e.dma_start(
                            out=w1sb[wbq].rearrange("p (c f) -> p c f", c=NCD),
                            in_=w1[e_i, :, q4 * 1024:(q4 + 1) * 1024].rearrange(
                                "(c p) f -> p c f", p=128)),
                            wait=[(PE, w1_pe[wbq]), (sZR, m_zr)]
                            + (m_scat if e_i == 0 and q4 < 2 else []))
                    for fc in range(8):
                        reg = ["A0", "A1"][fc % 2]
                        wA = take(reg)
                        fns = []
                        for c in range(NCD):
                            fns.append(lambda e, c=c, fc=fc, wbq=wbq, reg=reg: e.matmul(
                                reg_ap(reg)[:, 0:LOCAL_CAP],
                                w1sb[wbq][:, c * 1024 + fc * 128:c * 1024 + fc * 128 + 128],
                                xgT[:, c * LOCAL_CAP:(c + 1) * LOCAL_CAP],
                                start=(c == 0), stop=(c == NCD - 1)))
                        mm = tq(fns, wait=wA + [(fW1[wbq], mw1), (VE, mxt)])
                        fci = q4 * 8 + fc
                        mge2 = s_(lambda e, fci=fci, reg=reg: e.activation(
                            out=hsb[:, fci * LOCAL_CAP:(fci + 1) * LOCAL_CAP],
                            in_=reg_ap(reg)[:, 0:LOCAL_CAP],
                            func=AF.Gelu_apprx_tanh),
                            wait=[(PE, mm)])
                        used(reg, SE, mge2)
                        mh_done = mge2
                    w1_pe[wbq] = mm
                my_done = 0
                for dc in range(NCD):
                    reg = ["A2", "A3"][dc % 2]
                    wA = take(reg)
                    fns = []
                    for fc in range(NFC):
                        fns.append(lambda e, fc=fc, dc=dc, reg=reg: e.matmul(
                            reg_ap(reg)[:, 0:LOCAL_CAP],
                            w2sb[:, fc * 1024 + dc * 128:fc * 1024 + dc * 128 + 128],
                            hsb[:, fc * LOCAL_CAP:(fc + 1) * LOCAL_CAP],
                            start=(fc == 0), stop=(fc == NFC - 1)))
                    mm = tq(fns, wait=wA + [(SE, mh_done), (fW2, mw2), (PE, m_yfm_pe)])
                    mye = v(lambda e, dc=dc, reg=reg: e.tensor_copy(
                        out=yfm[:, dc * LOCAL_CAP:(dc + 1) * LOCAL_CAP],
                        in_=reg_ap(reg)[:, 0:LOCAL_CAP]),
                        wait=[(PE, mm)])
                    used(reg, VE, mye)
                    my_done = mye
                m_w2_pe = mm
                for sp in range(3):
                    wB = take("B0") + take("B1")
                    for c in range(NCD):
                        mtl = p.emit("tensor", lambda e, c=c, sp=sp: e.transpose(
                            psB[:, c * 128:(c + 1) * 128],
                            yfm[:, c * LOCAL_CAP + sp * 128:c * LOCAL_CAP + sp * 128 + 128],
                            ident_f32[:, :]),
                            wait=(wB + [(VE, my_done)]) if c == 0 else (),
                            inc=[(PE, 1)] if c == NCD - 1 else [])
                    mtr = mtl[0]
                    m_yfm_pe = mtr
                    mty = v(lambda e, sp=sp: e.tensor_copy(
                        out=ytk[:, sp * D:(sp + 1) * D], in_=psB[:, :]),
                        wait=[(PE, mtr), (sYS, m_ytk_ld)])
                    used("B0", VE, mty)
                    used("B1", VE, mty)
                    m_ytk_ld = dq(sYS, lambda e, e_i=e_i, sp=sp: e.dma_start(
                        out=yexp[e_i * LOCAL_CAP + sp * 128:
                                 e_i * LOCAL_CAP + (sp + 1) * 128, :],
                        in_=ytk[:, sp * D:(sp + 1) * D]),
                        wait=[(VE, mty), (sZR, m_zr)])
            m_yexp_done = m_ytk_ld

            comb = dict(xres=[0, 0], outst=[0, 0])
            ygab = [yga, AR2[:, 34816:35840]]
            sYGb = [sYG, sYG2]
            yg_v = [m_router_ve, m_router_ve]
            mad_prev = m_router_ve
            for pi in range(NP):
                tb = pi % 2
                mlr = dq(sXR[tb], lambda e, pi=pi, tb=tb: e.dma_start(
                    out=xres[tb], in_=xsrc[pi * 128:(pi + 1) * 128, :]),
                    wait=[(VE, comb["xres"][tb]), (PE, m_yfm_pe)])
                mga = gq(sYGb[tb], lambda e, pi=pi, tb=tb: e.indirect_dma_start(
                    out=ygab[tb][:, :], out_offset=None,
                    in_=yexp, in_offset=IndirectOffsetOnAxis(sloti[:, pi:pi + 1], 0)),
                    wait=[(sYS, m_yexp_done), (VE, yg_v[tb])])
                msc = v(lambda e, pi=pi, tb=tb: e.tensor_scalar(
                    out=ysc[:, :], in0=ygab[tb][:, :], scalar1=wtok[:, pi:pi + 1],
                    scalar2=None, op0=ALU.mult),
                    wait=[(sYGb[tb], mga), (VE, mad_prev)])
                yg_v[tb] = msc
                mad = v(lambda e, tb=tb: e.tensor_add(
                    outst[tb], ysc[:, :], xres[tb]),
                    wait=[(VE, msc), (sXR[tb], mlr), (sOS[tb], comb["outst"][tb])])
                comb["xres"][tb] = mad
                mad_prev = mad
                comb["outst"][tb] = dq(sOS[tb], lambda e, pi=pi, tb=tb: e.dma_start(
                    out=xdst[pi * 128:(pi + 1) * 128, :], in_=outst[tb]),
                    wait=[(VE, mad)])

        subs = [("attn", "a"), ("ffn", None), ("attn", "b"), ("ffn", None)]
        for si in range(n_sub):
            kind = subs[si]
            if kind[0] == "attn":
                emit_attn(si, kind[1])
            else:
                emit_ffn(si)
            p.barrier(skip=("sCC0", "sCC1", "sCC2", "sCC3"))

        block = stack.enter_context(nc.Block())
        p.finalize(block)
    return nc


def _layout_inputs(inputs):
    x = np.ascontiguousarray(np.asarray(inputs["L0_attn_a_x"], dtype=np.float32))
    common = dict(
        wq_a=np.ascontiguousarray(np.asarray(inputs["L0_attn_a_Wq"]).T),
        wk_a=np.ascontiguousarray(np.asarray(inputs["L0_attn_a_Wk"]).T),
        wv_a=np.ascontiguousarray(np.asarray(inputs["L0_attn_a_Wv"]).T),
        wq_b=np.ascontiguousarray(np.asarray(inputs["L0_attn_b_W_Q"]).reshape(D, D)),
        wk_b=np.ascontiguousarray(np.asarray(inputs["L0_attn_b_W_K"]).reshape(D, D)),
        wv_b=np.ascontiguousarray(np.asarray(inputs["L0_attn_b_W_V"]).reshape(D, D)),
        wo_b=np.ascontiguousarray(
            np.transpose(np.asarray(inputs["L0_attn_b_W_out"]), (1, 0, 2)).reshape(D, D)),
        bo_b=np.ascontiguousarray(np.asarray(inputs["L0_attn_b_b_out"]).reshape(D, 1)),
        gate_td=np.ascontiguousarray(np.asarray(inputs["L0_ffn_gate_w"]).T),
        w1=np.ascontiguousarray(np.asarray(inputs["L0_ffn_w1"])),
        w2=np.ascontiguousarray(np.asarray(inputs["L0_ffn_w2"])),
    )
    in_maps = []
    for c in range(8):
        m = dict(common)
        m["x_in"] = np.ascontiguousarray(x[c])
        in_maps.append(m)
    return in_maps


def kernel(**inputs):
    in_maps = _layout_inputs(inputs)
    nc = build_program(4)
    res = run_bass_kernel_spmd(nc, in_maps, core_ids=list(range(8)))
    return np.stack([res.results[c]["y_out"] for c in range(8)], axis=0)


# version 15
# speedup vs baseline: 1.0117x; 1.0117x over previous
import contextlib

DEBUG_RT = False
import numpy as np

import concourse.bass as bass
import concourse.mybir as mybir
from concourse.bass import IndirectOffsetOnAxis
from concourse.bass_utils import run_bass_kernel_spmd

F32 = mybir.dt.float32
BF16 = mybir.dt.bfloat16
I32 = mybir.dt.int32
AF = mybir.ActivationFunctionType
ALU = mybir.AluOpType
AX = mybir.AxisListType

D = 1024
S = 4096
NP = S // 128
NCD = D // 128
DFF = 4096
NFC = DFF // 128
NE = 16
LOCAL_CAP = 384
NSLOT = NE * LOCAL_CAP
QBLK = 512
NBLK = S // QBLK
MASK_NEG = -1.0e6

ENGINES = ("sync", "tensor", "vector", "scalar", "gpsimd")


class Sem:
    def __init__(self, handle, name):
        self.h = handle
        self.name = name
        self.n = 0


class Prog:
    def __init__(self, nc, stack):
        self.nc = nc
        self.q = {e: [] for e in ENGINES}
        self._stack = stack
        self._last_wait = {}
        self.sems = []

    def sem(self, name):
        h = self._stack.enter_context(self.nc.semaphore(name))
        s = Sem(h, name)
        self.sems.append(s)
        return s

    def emit(self, eng, fn, wait=(), inc=()):
        wait = [w for w in wait if w is not None]
        inc = list(inc)
        for s, v in wait:
            assert v <= s.n, (
                f"[{eng}] wait {s.name}>={v} but only {s.n} recorded (deadlock)")
        ms = []
        for s, k in inc:
            s.n += k
            ms.append(s.n)
        eff = []
        for s, v in wait:
            key = (eng, s.name)
            if self._last_wait.get(key, -1) >= v:
                continue
            self._last_wait[key] = v
            eff.append((s, v))

        def run(engine, _fn=fn, _w=eff, _i=inc):
            for s, v in _w:
                engine.wait_ge(s.h, v)
            inst = _fn(engine)
            if inst is not None:
                for s, k in _i:
                    inst.then_inc(s.h, k)
            return inst

        self.q[eng].append(run)
        return ms

    def barrier(self, skip=()):
        snap = [(s, s.n) for s in self.sems if s.name not in skip]
        for eng in ENGINES:
            self.emit(eng, lambda e: None, wait=snap)

    def finalize(self, block):
        @block.sync
        def _(e):
            for f in self.q["sync"]:
                f(e)

        @block.tensor
        def _(e):
            for f in self.q["tensor"]:
                f(e)

        @block.vector
        def _(e):
            for f in self.q["vector"]:
                f(e)

        @block.scalar
        def _(e):
            for f in self.q["scalar"]:
                f(e)

        @block.gpsimd
        def _(e):
            for f in self.q["gpsimd"]:
                f(e)


def build_program(n_sub=4):
    nc = bass.Bass("TRN2", target_bir_lowering=False, debug=False)

    x_in = nc.dram_tensor("x_in", [S, D], F32, kind="ExternalInput").ap()
    wq_a = nc.dram_tensor("wq_a", [D, D], F32, kind="ExternalInput").ap()
    wk_a = nc.dram_tensor("wk_a", [D, D], F32, kind="ExternalInput").ap()
    wv_a = nc.dram_tensor("wv_a", [D, D], F32, kind="ExternalInput").ap()
    wq_b = nc.dram_tensor("wq_b", [D, D], F32, kind="ExternalInput").ap()
    wk_b = nc.dram_tensor("wk_b", [D, D], F32, kind="ExternalInput").ap()
    wv_b = nc.dram_tensor("wv_b", [D, D], F32, kind="ExternalInput").ap()
    wo_b = nc.dram_tensor("wo_b", [D, D], F32, kind="ExternalInput").ap()
    bo_b = nc.dram_tensor("bo_b", [D, 1], F32, kind="ExternalInput").ap()
    gate_td = nc.dram_tensor("gate_td", [D, NE], F32, kind="ExternalInput").ap()
    w1 = nc.dram_tensor("w1", [NE, D, DFF], F32, kind="ExternalInput").ap()
    w2 = nc.dram_tensor("w2", [NE, DFF, D], F32, kind="ExternalInput").ap()
    y_out = nc.dram_tensor("y_out", [S, D], F32, kind="ExternalOutput").ap()

    resid = [x_in]
    for i in range(1, n_sub):
        resid.append(nc.dram_tensor(f"resid{i}", [S, D], F32, kind="Internal").ap())
    resid.append(y_out)

    qbuf = nc.dram_tensor("qbuf", [NCD, 128, S], BF16, kind="Internal").ap()
    kbufA = nc.dram_tensor("kbufA", [NP, 128, D], BF16, kind="Internal").ap()
    kbufB = nc.dram_tensor("kbufB", [16, 64, S], BF16, kind="Internal").ap()
    vbufA = nc.dram_tensor("vbufA", [NP, 128, 1040], BF16, kind="Internal").ap()
    vbufB = nc.dram_tensor("vbufB", [8, S, 130], BF16, kind="Internal").ap()
    w1b = nc.dram_tensor("w1b", [NE, D, DFF], BF16, kind="Internal").ap()
    w2b = nc.dram_tensor("w2b", [NE, DFF, D], BF16, kind="Internal").ap()
    disp = nc.dram_tensor("disp", [NSLOT + 1, D], BF16, kind="Internal").ap()
    dbg = nc.dram_tensor("dbg", [5, 128, 64], F32, kind="Internal").ap()
    dbg_x = nc.dram_tensor("dbg_x", [128, 1024], F32, kind="Internal").ap()
    yexp = nc.dram_tensor("yexp", [NSLOT + 1, D], BF16, kind="Internal").ap()

    stack = contextlib.ExitStack()
    with stack:
        p = Prog(nc, stack)
        PE = p.sem("pe")
        VE = p.sem("ve")
        SE = p.sem("se")
        GC = p.sem("gc")
        sXP = [p.sem("sXP0"), p.sem("sXP1")]
        sXR = [p.sem("sXR0"), p.sem("sXR1")]
        sOS = [p.sem("sOS0"), p.sem("sOS1")]
        sQS = [p.sem("sQS0"), p.sem("sQS1")]
        sKS = [p.sem("sKS0"), p.sem("sKS1")]
        sVS = p.sem("sVS")
        sQB = p.sem("sQB")
        sKT = [p.sem(f"sKT{i}") for i in range(4)]
        sVT = [p.sem(f"sVT{i}") for i in range(4)]
        sKP2 = [p.sem("sKP0"), p.sem("sKP1")]
        sVP2 = [p.sem("sVP0"), p.sem("sVP1")]
        sWP = p.sem("sWP")
        sBO = p.sem("sBO")
        sGW = p.sem("sGW")
        sSC = [p.sem(f"sSC{i}") for i in range(4)]
        sXG = p.sem("sXG")
        sW1 = [p.sem("sW10"), p.sem("sW11")]
        sW2 = p.sem("sW2")
        sW1c = [p.sem("sW10c"), p.sem("sW11c")]
        sW2c = p.sem("sW2c")
        sYS = p.sem("sYS")
        sYG = p.sem("sYG")
        sZR = p.sem("sZR")
        sWB = [p.sem("sWB0"), p.sem("sWB1"), p.sem("sWB2")]
        sCA = [p.sem(f"sCA{i}") for i in range(4)]
        sCC = [p.sem(f"sCC{i}") for i in range(4)]
        sYG2 = p.sem("sYG2")
        sDB = p.sem("sDB")
        cast_go = [0]
        sb = stack.enter_context

        ident_bf = sb(nc.sbuf_tensor("ident_bf", [128, 128], BF16))
        ident_f32 = sb(nc.sbuf_tensor("ident_f32", [128, 128], F32))
        ustrict = sb(nc.sbuf_tensor("ustrict", [128, 128], F32))
        cmask = sb(nc.sbuf_tensor("cmask", [128, 4 * QBLK], BF16))
        iotaE = sb(nc.sbuf_tensor("iotaE", [128, NE], F32))
        ones_r = sb(nc.sbuf_tensor("ones_r", [1, 128], F32))
        ones_cb = sb(nc.sbuf_tensor("ones_cb", [128, 1], BF16))
        ones_cf = sb(nc.sbuf_tensor("ones_cf", [128, 1], F32))

        AR1 = sb(nc.sbuf_tensor("AR1", [128, 32768], BF16))
        AR2 = sb(nc.sbuf_tensor("AR2", [128, 39040], BF16))
        wproj = AR1[:, 0:24576]
        expT = AR2[:, 0:16384]
        zst = AR2[:, 16384:20480]
        zst2 = AR2[:, 20480:24576]
        vpair2 = [AR2[:, 24576:28736], AR2[:, 6144:10304]]
        kpair2 = [AR2[:, 34928:39024], AR2[:, 10304:14400]]
        vtile = [AR2[:, 28736 + i * 1040:28736 + (i + 1) * 1040] for i in range(4)]
        vst = AR2[:, 30816:31856]
        kst = AR2[:, 31856:32880]
        qst = AR2[:, 32880:33904]
        ktile = [AR2[:, 34928 + i * 1024:34928 + (i + 1) * 1024] for i in range(4)]
        w2sb = AR1[:, 0:32768]
        w1sb = [AR2[:, 0:8192], AR2[:, 8192:16384]]
        hsb = AR2[:, 16384:28672]
        xg = AR2[:, 28672:31744]
        xgT = AR2[:, 31744:34816]
        ytk = AR2[:, 34816:37888]
        yga = AR2[:, 37888:38912]

        ARF = sb(nc.sbuf_tensor("ARF", [128, 3072], F32))
        xres = [ARF[:, 0:1024], ARF[:, 1024:2048]]
        sqs = ARF[:, 0:1024]
        normB = ARF[:, 2048:2560]
        fq = ARF[0:1, 2560:3072]
        yfm = ARF[:, 0:3072]

        qgrp = sb(nc.sbuf_tensor("qgrp", [128, NCD * QBLK], BF16))
        xln_tok = sb(nc.sbuf_tensor("xln_tok", [128, 2 * D], BF16))
        xln_fm = sb(nc.sbuf_tensor("xln_fm", [128, 2 * NCD * 512], BF16))
        xpan = sb(nc.sbuf_tensor("xpan", [128, 2 * D], F32))
        outs2 = sb(nc.sbuf_tensor("outs2", [128, 2 * D], F32))
        outst = [outs2[:, 0:1024], outs2[:, 1024:2048]]
        xfm = outs2[:, 0:1024]
        ysc = xpan[:, 0:1024]
        lnsq = sb(nc.sbuf_tensor("lnsq", [128, D], F32))
        lnst = sb(nc.sbuf_tensor("lnst", [128, 8], F32))
        mrun = sb(nc.sbuf_tensor("mrun", [128, 2], F32))
        csc = sb(nc.sbuf_tensor("csc", [1, 8], F32))
        negC = sb(nc.sbuf_tensor("negC", [128, 1], F32))
        bo_sb = sb(nc.sbuf_tensor("bo_sb", [128, NCD], F32))

        gate_sb = sb(nc.sbuf_tensor("gate_sb", [128, NCD * NE], F32))
        gsumB = sb(nc.sbuf_tensor("gsumB", [128, NE], F32))
        gsrow = sb(nc.sbuf_tensor("gsrow", [1, NE], F32))
        offrow = sb(nc.sbuf_tensor("offrow", [1, NE], F32))
        wtok = sb(nc.sbuf_tensor("wtok", [128, NP], F32))
        sloti = sb(nc.sbuf_tensor("sloti", [128, NP], I32))
        rrec = sb(nc.sbuf_tensor("rrec", [128, 2 * NP], F32))
        routg = ARF[:, 2048:2240]
        rout1 = sb(nc.sbuf_tensor("rout1", [128, 32], F32))

        psA = sb(nc.psum_tensor("psA", [128, 2048], F32))
        psB = sb(nc.psum_tensor("psB", [128, 1024], F32))
        pS = sb(nc.psum_tensor("pS", [128, 512], F32))
        pT = sb(nc.psum_tensor("pT", [128, 1024], BF16))

        guards = {r: [] for r in ("A0", "A1", "A2", "A3", "B0", "B1", "S", "T")}

        def reg_ap(r):
            if r[0] == "A":
                i = int(r[1])
                return psA[:, i * 512:(i + 1) * 512]
            if r[0] == "B":
                i = int(r[1])
                return psB[:, i * 512:(i + 1) * 512]
            if r == "S":
                return pS[:, :]
            return pT[:, :]

        def take(r):
            w = guards[r]
            guards[r] = []
            return w

        def used(r, sem, ms):
            guards[r].append((sem, ms))

        def t(fn, wait=()):
            return p.emit("tensor", fn, wait, [(PE, 1)])[0]

        def tq(fns, wait=()):
            for i, fn in enumerate(fns):
                if i == len(fns) - 1:
                    return p.emit("tensor", fn, wait if i == 0 else (), [(PE, 1)])[0]
                p.emit("tensor", fn, wait if i == 0 else ())

        def v(fn, wait=()):
            return p.emit("vector", fn, wait, [(VE, 1)])[0]

        def s_(fn, wait=()):
            return p.emit("scalar", fn, wait, [(SE, 1)])[0]

        def g(fn, wait=()):
            return p.emit("gpsimd", fn, wait, [(GC, 1)])[0]

        def dq(sem, fn, wait=()):
            return p.emit("sync", fn, wait, [(sem, 16)])[0]

        def gq(sem, fn, wait=()):
            return p.emit("gpsimd", fn, wait, [(sem, 16)])[0]

        m = g(lambda e: e.memset(ident_bf[:, :], 0.0))
        g(lambda e: e.affine_select(out=ident_bf[:, :], in_=ident_bf[:, :],
                                    compare_op=ALU.not_equal, fill=1.0, base=0,
                                    pattern=[[-1, 128]], channel_multiplier=1),
          wait=[(GC, m)])
        m = g(lambda e: e.memset(ident_f32[:, :], 0.0))
        g(lambda e: e.affine_select(out=ident_f32[:, :], in_=ident_f32[:, :],
                                    compare_op=ALU.not_equal, fill=1.0, base=0,
                                    pattern=[[-1, 128]], channel_multiplier=1),
          wait=[(GC, m)])
        m = g(lambda e: e.memset(ustrict[:, :], 0.0))
        g(lambda e: e.affine_select(out=ustrict[:, :], in_=ustrict[:, :],
                                    compare_op=ALU.is_ge, fill=1.0, base=0,
                                    pattern=[[-1, 128]], channel_multiplier=1),
          wait=[(GC, m)])
        m = g(lambda e: e.memset(cmask[:, :], 0.0))
        for tt in range(4):
            g(lambda e, tt=tt: e.affine_select(
                out=cmask[:, tt * QBLK:(tt + 1) * QBLK],
                in_=cmask[:, tt * QBLK:(tt + 1) * QBLK],
                compare_op=ALU.is_ge, fill=-1.0e9,
                base=-128 * tt, pattern=[[1, QBLK]], channel_multiplier=-1),
                wait=[(GC, m)])
        g(lambda e: e.iota(iotaE[:, :], pattern=[[1, NE]], base=0,
                           channel_multiplier=0,
                           allow_small_or_imprecise_dtypes=True))
        g(lambda e: e.memset(ones_r[:, :], 1.0))
        g(lambda e: e.memset(ones_cb[:, :], 1.0))
        g(lambda e: e.memset(ones_cf[:, :], 1.0))
        dq(sBO, lambda e: e.dma_start(
            out=bo_sb[:, :], in_=bo_b.rearrange("(c p) o -> p (c o)", p=128),
            allow_slow_non_contiguous=True))
        mzr = g(lambda e: e.memset(AR1[:, 24576:32768], 0.0))
        for buf in (disp, yexp):
            dq(sZR, lambda e, buf=buf: e.dma_start(
                out=buf[NSLOT:NSLOT + 1, :], in_=AR1[0:1, 24576:25600]),
                wait=[(GC, mzr)])
            for r in range(6):
                rows = min(1024, NSLOT - r * 1024)
                dq(sZR, lambda e, buf=buf, r=r, rows=rows: e.dma_start(
                    out=buf[r * 1024:r * 1024 + rows, :].rearrange(
                        "(a p) d -> p a d", p=128) if rows % 128 == 0 else
                    buf[r * 1024:r * 1024 + rows, :],
                    in_=AR1.ap().rearrange("p (a d) -> p a d", d=1024)[:, 24:24 + rows // 128]
                    if rows % 128 == 0 else AR1[0:rows, 24576:24576 + 1024]),
                    wait=[(GC, mzr)])
        m_zr = sZR.n
        p.barrier(skip=("sZR",))

        def emit_ln_panel(xsrc, pi, pb_, st):
            ml = dq(sXP[pb_], lambda e, pi=pi, pb_=pb_: e.dma_start(
                out=xpan[:, pb_ * D:(pb_ + 1) * D],
                in_=xsrc[pi * 128:(pi + 1) * 128, :]),
                wait=[(VE, st["xpan_v"][pb_]), (SE, st["xpan_s"][pb_]),
                      (PE, st["xpan_p"][pb_])])
            mv1 = v(lambda e, pb_=pb_: e.reduce_sum(
                out=lnst[:, 0:1], in_=xpan[:, pb_ * D:(pb_ + 1) * D], axis=AX.X),
                wait=[(sXP[pb_], ml), (VE, st["lnst_v"]), (SE, st["lnst_s"])])
            mv2 = v(lambda e: e.tensor_scalar_mul(
                lnst[:, 0:1], lnst[:, 0:1], 1.0 / D), wait=[(VE, mv1)])
            ms1 = s_(lambda e, pb_=pb_: e.activation(
                out=lnsq[:, :], in_=xpan[:, pb_ * D:(pb_ + 1) * D], func=AF.Square),
                wait=[(sXP[pb_], ml), (VE, st["sq_v"])])
            st["xpan_s"][pb_] = ms1
            mv3 = v(lambda e: e.reduce_sum(
                out=lnst[:, 1:2], in_=lnsq[:, :], axis=AX.X), wait=[(SE, ms1)])
            st["sq_v"] = mv3
            mv4 = v(lambda e: e.tensor_scalar(
                out=lnst[:, 2:3], in0=lnst[:, 0:1], scalar1=lnst[:, 0:1],
                scalar2=1e-5, op0=ALU.mult, op1=ALU.subtract), wait=[(VE, mv2)])
            mv5 = v(lambda e: e.scalar_tensor_tensor(
                out=lnst[:, 2:3], in0=lnst[:, 1:2], scalar=1.0 / D,
                in1=lnst[:, 2:3], op0=ALU.mult, op1=ALU.subtract),
                wait=[(VE, mv4), (VE, mv3)])
            ms2 = s_(lambda e: e.activation(
                out=lnst[:, 3:4], in_=lnst[:, 2:3], func=AF.Sqrt),
                wait=[(VE, mv5)])
            st["lnst_s"] = ms2
            mv6 = v(lambda e: e.reciprocal(
                out=lnst[:, 3:4], in_=lnst[:, 3:4]), wait=[(SE, ms2)])
            xld = st.get("xln_dst", xln_tok)
            nsl = st.get("xln_slots", 2)
            sl2 = pi % nsl
            mv7 = v(lambda e, pb_=pb_, sl2=sl2, xld=xld: e.tensor_scalar(
                out=xld[:, sl2 * D:(sl2 + 1) * D],
                in0=xpan[:, pb_ * D:(pb_ + 1) * D],
                scalar1=lnst[:, 0:1], scalar2=lnst[:, 3:4],
                op0=ALU.subtract, op1=ALU.mult),
                wait=[(VE, mv6), (PE, st["xlt_p"][sl2]),
                      (sSC[sl2], st["xlt_g"][sl2])])
            st["xpan_v"][pb_] = mv7
            st["lnst_v"] = mv7
            return ml, mv2, mv6, mv7

        def new_st():
            return dict(xpan_v=[0, 0], xpan_s=[0, 0], xpan_p=[0, 0],
                        xlt_p=[0, 0, 0, 0], xlt_g=[0, 0, 0, 0],
                        lnst_v=0, lnst_s=0, sq_v=0)

        def emit_attn(sub, kind):
            xsrc = resid[sub]
            xdst = resid[sub + 1]
            sc = 1.0 / np.sqrt(1024.0) if kind == "a" else 1.0 / np.sqrt(64.0)
            wq, wk, wv = (wq_a, wk_a, wv_a) if kind == "a" else (wq_b, wk_b, wv_b)

            for wi, w in enumerate((wq, wk, wv)):
                gq(sWP, lambda e, wi=wi, w=w: e.dma_start(
                    out=wproj.rearrange("p (i c o) -> p i c o", i=3, c=NCD)[:, wi],
                    in_=w.rearrange("(c p) o -> p c o", p=128)))
            m_wp = sWP.n
            if kind == "b":
                mvo = g(lambda e: e.memset(
                    vst.rearrange("p (pr u) -> p pr u", u=130)[:, :, 64:65], 1.0))
                mvo2 = g(lambda e: e.memset(
                    vst.rearrange("p (pr u) -> p pr u", u=130)[:, :, 129:130], 1.0),
                    wait=[(GC, mvo)])
            else:
                mvo2 = g(lambda e: e.memset(
                    vst.rearrange("p (pr u) -> p pr u", u=130)[:, :, 128:130], 0.0))

            st = new_st()
            qst_use = [0, 0]
            kst_use = [0, 0]
            vst_use = 0
            mev = 0
            xln_fm_p = [0, 0]
            nrm_pe = [0, 0]
            knrm_pe = [0, 0]
            csc_ve = [0]
            qpend = []
            kpend = []
            mm = 0
            for grp in range(8):
                gb = grp % 2
                if grp == 4:
                    cast_go[0] = PE.n
                for po in range(4):
                    pi = grp * 4 + po
                    pb_ = pi % 2
                    ml, _, _, mv7 = emit_ln_panel(xsrc, pi, pb_, st)
                    w0 = take("T") + [(VE, mv7)]
                    for c in range(NCD):
                        mtl = p.emit("tensor", lambda e, c=c, pb_=pb_: e.transpose(
                            pT[:, c * 128:(c + 1) * 128],
                            xln_tok[:, pb_ * D + c * 128:pb_ * D + c * 128 + 128],
                            ident_bf[:, :]),
                            wait=w0 if c == 0 else (),
                            inc=[(PE, 1)] if c == NCD - 1 else [])
                    mtr = mtl[0]
                    st["xlt_p"][pb_] = mtr
                    mev = v(lambda e, gb=gb, po=po: e.tensor_copy(
                        out=xln_fm.rearrange(
                            "p (b c u) -> p b c u", b=2, c=NCD)[:, gb, :, po * 128:(po + 1) * 128],
                        in_=pT.ap().rearrange("p (c u) -> p c u", c=NCD)),
                        wait=[(PE, mtr), (PE, xln_fm_p[gb])])
                    used("T", VE, mev)
                for wi in range(2):
                    for oc in range(NCD):
                        reg = ["A0", "A1"][oc % 2]
                        ww = take(reg) + [(VE, mev), (sWP, m_wp)]
                        fns = []
                        for c in range(NCD):
                            fns.append(lambda e, c=c, oc=oc, wi=wi, gb=gb, reg=reg: e.matmul(
                                reg_ap(reg),
                                wproj[:, (wi * NCD + c) * 1024 + oc * 128:
                                      (wi * NCD + c) * 1024 + oc * 128 + 128],
                                xln_fm[:, gb * 4096 + c * 512:gb * 4096 + (c + 1) * 512],
                                start=(c == 0), stop=(c == NCD - 1)))
                        mm = tq(fns, wait=ww)
                        if wi == 0:
                            qs = oc % 2
                            mevq = v(lambda e, qs=qs, reg=reg: e.tensor_copy(
                                out=qst[:, qs * 512:(qs + 1) * 512], in_=reg_ap(reg)),
                                wait=[(PE, mm), (sQS[qs], qst_use[qs])])
                            used(reg, VE, mevq)
                            qst_use[qs] = dq(sQS[qs], lambda e, oc=oc, grp=grp, qs=qs: e.dma_start(
                                out=qbuf[oc, :, grp * 512:(grp + 1) * 512],
                                in_=qst[:, qs * 512:(qs + 1) * 512]),
                                wait=[(VE, mevq)])
                            msq = s_(lambda e, qs=qs: e.activation(
                                out=sqs[:, qs * 512:(qs + 1) * 512],
                                in_=qst[:, qs * 512:(qs + 1) * 512], func=AF.Square),
                                wait=[(VE, mevq), (PE, nrm_pe[qs])])
                            qpend.append((qs, msq, oc))
                            if oc == NCD - 1:
                                flushn = list(qpend)
                                qpend.clear()
                            elif len(qpend) >= 2:
                                flushn = [qpend.pop(0)]
                            else:
                                flushn = []
                            for qs_f, msq_f, oc_f in flushn:
                                if kind == "a":
                                    wN = take("S") if oc_f == 0 else []
                                    nrm_pe[qs_f] = t(lambda e, qs=qs_f, oc2=oc_f: e.matmul(
                                        pS[0:1, 0:512], ones_cf[:, :],
                                        sqs[:, qs * 512:(qs + 1) * 512],
                                        start=(oc2 == 0), stop=(oc2 == NCD - 1),
                                        skip_group_check=True),
                                        wait=wN + [(SE, msq_f)])
                                    if oc_f == NCD - 1:
                                        mrq = v(lambda e: e.reduce_max(
                                            out=csc[0:1, 2:3], in_=pS[0:1, 0:512],
                                            axis=AX.X),
                                            wait=[(PE, nrm_pe[qs_f]), (VE, csc_ve[0])])
                                        used("S", VE, mrq)
                                        if grp == 0:
                                            csc_ve[0] = v(lambda e: e.tensor_copy(
                                                out=csc[0:1, 0:1], in_=csc[0:1, 2:3]),
                                              wait=[(VE, mrq)])
                                        else:
                                            csc_ve[0] = v(lambda e: e.tensor_max(
                                                csc[0:1, 0:1], csc[0:1, 0:1],
                                                csc[0:1, 2:3]), wait=[(VE, mrq)])
                                else:
                                    for hh in range(2):
                                        r = hh * 64
                                        wN = take("S")
                                        nrm_pe[qs_f] = t(lambda e, qs=qs_f, r=r: e.matmul(
                                            pS[0:1, 0:512], ones_cf[r:r + 64, :],
                                            sqs[r:r + 64, qs * 512:(qs + 1) * 512],
                                            start=True, stop=True,
                                            tile_position=(r, 0)),
                                            wait=wN + [(SE, msq_f)])
                                        mrq = v(lambda e: e.reduce_max(
                                            out=csc[0:1, 2:3], in_=pS[0:1, 0:512],
                                            axis=AX.X),
                                            wait=[(PE, nrm_pe[qs_f]), (VE, csc_ve[0])])
                                        used("S", VE, mrq)
                                        if grp == 0 and oc_f == 0 and hh == 0:
                                            csc_ve[0] = v(lambda e: e.tensor_copy(
                                                out=csc[0:1, 0:1], in_=csc[0:1, 2:3]),
                                              wait=[(VE, mrq)])
                                        else:
                                            csc_ve[0] = v(lambda e: e.tensor_max(
                                                csc[0:1, 0:1], csc[0:1, 0:1],
                                                csc[0:1, 2:3]), wait=[(VE, mrq)])
                        else:
                            ks = oc % 2
                            mevk = v(lambda e, ks=ks, reg=reg: e.tensor_copy(
                                out=kst[:, ks * 512:(ks + 1) * 512], in_=reg_ap(reg)),
                                wait=[(PE, mm), (sKS[ks], kst_use[ks])])
                            used(reg, VE, mevk)
                            msk = s_(lambda e, ks=ks: e.activation(
                                out=sqs[:, ks * 512:(ks + 1) * 512],
                                in_=kst[:, ks * 512:(ks + 1) * 512], func=AF.Square),
                                wait=[(VE, mevk), (PE, knrm_pe[ks])])
                            kpend.append((ks, msk, oc))
                            if oc == NCD - 1:
                                flushn = list(kpend)
                                kpend.clear()
                            elif len(kpend) >= 2:
                                flushn = [kpend.pop(0)]
                            else:
                                flushn = []
                            for ks_f, msk_f, oc_f in flushn:
                                if kind == "a":
                                    wN = take("B0") if oc_f == 0 else []
                                    knrm_pe[ks_f] = t(lambda e, ks=ks_f, oc2=oc_f: e.matmul(
                                        psB[0:1, 0:512], ones_cf[:, :],
                                        sqs[:, ks * 512:(ks + 1) * 512],
                                        start=(oc2 == 0), stop=(oc2 == NCD - 1),
                                        skip_group_check=True),
                                        wait=wN + [(SE, msk_f)])
                                    if oc_f == NCD - 1:
                                        mrk2 = v(lambda e: e.reduce_max(
                                            out=csc[0:1, 2:3], in_=psB[0:1, 0:512],
                                            axis=AX.X),
                                            wait=[(PE, knrm_pe[ks_f]), (VE, csc_ve[0])])
                                        used("B0", VE, mrk2)
                                        if grp == 0:
                                            csc_ve[0] = v(lambda e: e.tensor_copy(
                                                out=csc[0:1, 1:2], in_=csc[0:1, 2:3]),
                                              wait=[(VE, mrk2)])
                                        else:
                                            csc_ve[0] = v(lambda e: e.tensor_max(
                                                csc[0:1, 1:2], csc[0:1, 1:2],
                                                csc[0:1, 2:3]), wait=[(VE, mrk2)])
                                else:
                                    for hh in range(2):
                                        r = hh * 64
                                        wN = take("B0")
                                        knrm_pe[ks_f] = t(lambda e, ks=ks_f, r=r: e.matmul(
                                            psB[0:1, 0:512], ones_cf[r:r + 64, :],
                                            sqs[r:r + 64, ks * 512:(ks + 1) * 512],
                                            start=True, stop=True,
                                            tile_position=(r, 0)),
                                            wait=wN + [(SE, msk_f)])
                                        mrk2 = v(lambda e: e.reduce_max(
                                            out=csc[0:1, 2:3], in_=psB[0:1, 0:512],
                                            axis=AX.X),
                                            wait=[(PE, knrm_pe[ks_f]), (VE, csc_ve[0])])
                                        used("B0", VE, mrk2)
                                        if grp == 0 and oc_f == 0 and hh == 0:
                                            csc_ve[0] = v(lambda e: e.tensor_copy(
                                                out=csc[0:1, 1:2], in_=csc[0:1, 2:3]),
                                              wait=[(VE, mrk2)])
                                        else:
                                            csc_ve[0] = v(lambda e: e.tensor_max(
                                                csc[0:1, 1:2], csc[0:1, 1:2],
                                                csc[0:1, 2:3]), wait=[(VE, mrk2)])
                            if kind == "a":
                                kst_use[ks] = dq(sKS[ks], lambda e, oc=oc, grp=grp, ks=ks: e.dma_start(
                                    out=kbufA[grp * 4:(grp + 1) * 4, :,
                                              oc * 128:(oc + 1) * 128].rearrange(
                                                  "j u c -> u j c"),
                                    in_=kst[:, ks * 512:(ks + 1) * 512].rearrange(
                                        "u (j c) -> u j c", j=4)),
                                    wait=[(VE, mevk)])
                            else:
                                dq(sKS[ks], lambda e, oc=oc, grp=grp, ks=ks: e.dma_start(
                                    out=kbufB[2 * oc, :, grp * 512:(grp + 1) * 512],
                                    in_=kst[0:64, ks * 512:(ks + 1) * 512]),
                                    wait=[(VE, mevk)])
                                kst_use[ks] = dq(sKS[ks], lambda e, oc=oc, grp=grp, ks=ks: e.dma_start(
                                    out=kbufB[2 * oc + 1, :, grp * 512:(grp + 1) * 512],
                                    in_=kst[64:128, ks * 512:(ks + 1) * 512]))
                for st4 in range(4):
                    tile = grp * 4 + st4
                    mvv = 0
                    for half in range(2):
                        reg = ["A2", "A3"][half]
                        ww = take(reg) + [(VE, mev), (sWP, m_wp),
                                          (sVS, vst_use), (GC, mvo2)]
                        fns = []
                        for c in range(NCD):
                            fns.append(lambda e, c=c, st4=st4, half=half, gb=gb, reg=reg: e.matmul(
                                reg_ap(reg),
                                xln_fm[:, gb * 4096 + c * 512 + st4 * 128:
                                       gb * 4096 + c * 512 + st4 * 128 + 128],
                                wproj[:, (2 * NCD + c) * 1024 + half * 512:
                                      (2 * NCD + c) * 1024 + half * 512 + 512],
                                start=(c == 0), stop=(c == NCD - 1)))
                        mm = tq(fns, wait=ww)
                        if kind == "a":
                            for cc in range(4):
                                c = half * 4 + cc
                                mvv = v(lambda e, c=c, cc=cc, reg=reg: e.tensor_copy(
                                    out=vst[:, c * 130:c * 130 + 128],
                                    in_=reg_ap(reg)[:, cc * 128:(cc + 1) * 128]),
                                    wait=[(PE, mm)] if cc == 0 else ())
                        else:
                            for hh in range(8):
                                h = half * 8 + hh
                                pr, u = h // 2, (h % 2) * 65
                                mvv = v(lambda e, pr=pr, u=u, hh=hh, reg=reg: e.tensor_copy(
                                    out=vst[:, pr * 130 + u:pr * 130 + u + 64],
                                    in_=reg_ap(reg)[:, hh * 64:(hh + 1) * 64]),
                                    wait=[(PE, mm)] if hh == 0 else ())
                        used(reg, VE, mvv)
                    if kind == "a":
                        vst_use = dq(sVS, lambda e, tile=tile: e.dma_start(
                            out=vbufA[tile], in_=vst[:, :]), wait=[(VE, mvv)])
                    else:
                        vst_use = dq(sVS, lambda e, tile=tile: e.dma_start(
                            out=vbufB[:, tile * 128:(tile + 1) * 128, :].rearrange(
                                "pr u c -> u pr c"),
                            in_=vst.rearrange("u (pr c) -> u pr c", c=130)),
                            wait=[(VE, mvv)])
                xln_fm_p[gb] = mm
                if grp == 1:
                    p._early_ve = VE.n
            msq2 = s_(lambda e: e.activation(out=csc[0:1, 0:2], in_=csc[0:1, 0:2],
                                             func=AF.Sqrt),
                      wait=[(VE, VE.n)])
            mcm = v(lambda e: e.tensor_mul(csc[0:1, 3:4], csc[0:1, 0:1],
                                           csc[0:1, 1:2]), wait=[(SE, msq2)])
            mcm = v(lambda e: e.tensor_scalar_mul(csc[0:1, 3:4], csc[0:1, 3:4],
                                                  -sc), wait=[(VE, mcm)])
            wS = take("S")
            mt2 = t(lambda e: e.matmul(
                pS[:, 0:1], ones_r[:, :], csc[0:1, 3:4], start=True, stop=True),
                wait=wS + [(VE, mcm)])
            mng = v(lambda e: e.tensor_copy(out=negC[:, :], in_=pS[:, 0:1]),
                    wait=[(PE, mt2)])
            used("S", VE, mng)
            m_proj_pe = PE.n
            m_proj_done_v = VE.n
            mq_all = [(s, s.n) for s in sQS]
            mk_all = [(s, s.n) for s in sKS]
            mv_all = [(sVS, sVS.n)]
            if kind == "b":
                mwo = gq(sWP, lambda e: e.dma_start(
                    out=wproj.rearrange("p (i c o) -> p i c o", i=3, c=NCD)[:, 0],
                    in_=wo_b.rearrange("(c p) o -> p c o", p=128)),
                    wait=[(PE, m_proj_pe)])
            else:
                mwo = sWP.n

            tail = dict(xres=[0, 0], outst=[0, 0])
            kt_pe = [0, 0, 0, 0]
            vt_pe = [0, 0, 0, 0]
            kp_pe = 0
            vp_pe = 0
            kp_use = [0, 0]
            vp_use = [0, 0]
            zev_use = [0, 0]
            expT_pe = 0
            expT_slots = [0, 0, 0, 0, 0, 0]
            qb_prev = 0
            for b in range(NBLK):
                nj = 4 * (b + 1)
                mqb = dq(sQB, lambda e, b=b: e.dma_start(
                    out=qgrp.ap().rearrange("p (c u) -> p c u", c=NCD),
                    in_=qbuf[:, :, b * 512:(b + 1) * 512].rearrange(
                        "c p u -> p c u")),
                    wait=mq_all + [(PE, qb_prev)])

                if kind == "a":
                    mex_last = 0
                    msum = 0
                    for j in range(nj):
                        kb = j % 4
                        mkl = dq(sKT[kb], lambda e, j=j, kb=kb: e.dma_start(
                            out=ktile[kb], in_=kbufA[j]),
                            wait=mk_all + [(PE, kt_pe[kb])])
                        reg = ["B0", "B1"][j % 2]
                        wB = take(reg)
                        fns = []
                        for c in range(NCD):
                            fns.append(lambda e, c=c, kb=kb, reg=reg: e.matmul(
                                reg_ap(reg),
                                ktile[kb][:, c * 128:(c + 1) * 128],
                                qgrp[:, c * 512:(c + 1) * 512],
                                start=(c == 0), stop=(c == NCD - 1)))
                        mm = tq(fns, wait=wB + [(sKT[kb], mkl), (sQB, mqb),
                                                (VE, m_proj_done_v)])
                        kt_pe[kb] = mm
                        exw = [(PE, mm), (VE, mng), (PE, expT_pe), (sZR, m_zr)]
                        if j >= nj - 4:
                            tt = j - (nj - 4)
                            mmk = v(lambda e, reg=reg, tt=tt: e.tensor_add(
                                reg_ap(reg), reg_ap(reg),
                                cmask[:, tt * QBLK:(tt + 1) * QBLK]),
                                wait=[(PE, mm)])
                            mex = s_(lambda e, j=j, reg=reg: e.activation(
                                out=expT[:, j * 512:(j + 1) * 512], in_=reg_ap(reg),
                                func=AF.Exp, bias=negC[:, :], scale=sc),
                                wait=exw + [(VE, mmk)])
                        else:
                            mex = s_(lambda e, j=j, reg=reg: e.activation(
                                out=expT[:, j * 512:(j + 1) * 512], in_=reg_ap(reg),
                                func=AF.Exp, bias=negC[:, :], scale=sc),
                                wait=exw)
                        used(reg, SE, mex)
                        mex_last = mex
                    wS = take("S")
                    fns = []
                    for j in range(nj):
                        fns.append(lambda e, j=j, nj=nj: e.matmul(
                            pS[0:1, 0:512], ones_cb[:, :],
                            expT[:, j * 512:(j + 1) * 512],
                            start=(j == 0), stop=(j == nj - 1)))
                    msum = tq(fns, wait=wS + [(SE, mex_last)])
                    mfq = v(lambda e: e.reciprocal(out=fq[:, :], in_=pS[0:1, 0:512]),
                            wait=[(PE, msum)])
                    used("S", VE, mfq)
                    wB = take("B0")
                    mnb = t(lambda e: e.matmul(
                        psB[:, 0:512], ones_r[:, :], fq[:, :], start=True, stop=True),
                        wait=wB + [(VE, mfq)])
                    mnbe = v(lambda e: e.tensor_copy(out=normB[:, :], in_=psB[:, 0:512]),
                             wait=[(PE, mnb)])
                    used("B0", VE, mnbe)
                    m_z = 0
                    for hf in range(2):
                        wA = take("A0") + take("A1") + take("A2") + take("A3")
                        mm = None
                        for j in range(nj):
                            vb = j % 4
                            mvl = dq(sVT[vb], lambda e, j=j, vb=vb: e.dma_start(
                                out=vtile[vb], in_=vbufA[j]),
                                wait=mv_all + [(PE, vt_pe[vb])])
                            fns = []
                            for q_ in range(4):
                                oc = hf * 4 + q_
                                fns.append(lambda e, q_=q_, oc=oc, vb=vb, j=j, nj=nj: e.matmul(
                                    psA[:, q_ * 512:(q_ + 1) * 512],
                                    vtile[vb][:, oc * 130:oc * 130 + 128],
                                    expT[:, j * 512:(j + 1) * 512],
                                    start=(j == 0), stop=(j == nj - 1),
                                    skip_group_check=True))
                            mm = tq(fns, wait=(wA + [(SE, mex_last), (sVT[vb], mvl)])
                                    if j == 0 else [(sVT[vb], mvl)])
                            vt_pe[vb] = mm
                        expT_pe = mm
                        for q_ in range(4):
                            oc = hf * 4 + q_
                            mz = v(lambda e, q_=q_, oc=oc: e.tensor_mul(
                                zst[:, oc * 512:(oc + 1) * 512],
                                psA[:, q_ * 512:(q_ + 1) * 512], normB[:, :]),
                                wait=[(PE, mm), (VE, mnbe)])
                            used(f"A{q_}", VE, mz)
                            m_z = mz
                    zsrc = zst
                else:
                    m_z = 0

                    def pair_load(hp):
                        sl = hp % 2
                        dq(sKP2[sl], lambda e, hp=hp, nj=nj, sl=sl: e.dma_start(
                            out=kpair2[sl][0:64, 0:nj * 128],
                            in_=kbufB[2 * hp, :, 0:nj * 128]),
                            wait=mk_all + [(PE, kp_use[sl])])
                        dq(sKP2[sl], lambda e, hp=hp, nj=nj, sl=sl: e.dma_start(
                            out=kpair2[sl][64:128, 0:nj * 128],
                            in_=kbufB[2 * hp + 1, :, 0:nj * 128]))
                        mvp = dq(sVP2[sl], lambda e, hp=hp, nj=nj, sl=sl: e.dma_start(
                            out=vpair2[sl][:, 0:nj * 130],
                            in_=vbufB[hp, 0:nj * 128, :].rearrange(
                                "(j u) c -> u j c", u=128),
                            allow_slow_non_contiguous=True),
                            wait=mv_all + [(PE, vp_use[sl])])
                        return sKP2[sl].n, mvp

                    pl = {0: pair_load(0)}
                    pending_norm = [None]
                    for hp in range(8):
                        sl = hp % 2
                        if hp + 1 < 8:
                            pl[hp + 1] = pair_load(hp + 1)
                        mkp, mvp = pl[hp]
                        kpair = kpair2[sl]
                        vpair = vpair2[sl]
                        av_done = [0, 0]
                        pgs = [("B0", "B1", psB, 0), ("A2", "A3", psA, 1024)]
                        sc_mm = [0] * nj

                        def b_scores(j, hp=hp, mkp=mkp, mvp=mvp, kp=kpair):
                            g0, g1, pst, off = pgs[j % 2]
                            wB = take(g0) + take(g1)
                            fns = []
                            for hh in range(2):
                                r = hh * 64
                                fns.append(lambda e, j=j, r=r, hp=hp, pst=pst,
                                           off=off, kp=kp: e.matmul(
                                    pst[:, off + (r // 64) * 512:
                                        off + (r // 64) * 512 + 512],
                                    kp[r:r + 64, j * 128:(j + 1) * 128],
                                    qgrp[r:r + 64, hp * 512:(hp + 1) * 512],
                                    start=True, stop=True, tile_position=(r, 0)))
                            sc_mm[j] = tq(fns, wait=wB + [(sKP2[hp % 2], mkp),
                                                          (sVP2[hp % 2], mvp),
                                                          (sQB, mqb)])

                        def b_expav(j, hp=hp, nj=nj, vp=vpair):
                            nonlocal kp_pe, m_exp_last
                            g0, g1, pst, off = pgs[j % 2]
                            slot = j % 6
                            exw = [(PE, sc_mm[j]), (VE, mng),
                                   (PE, expT_slots[slot]), (sZR, m_zr)]
                            if j >= nj - 4:
                                tt = j - (nj - 4)
                                mmk = 0
                                for hh in range(2):
                                    mmk = v(lambda e, pst=pst, off=off, hh=hh,
                                            tt=tt: e.tensor_add(
                                        pst[:, off + hh * 512:off + hh * 512 + 512],
                                        pst[:, off + hh * 512:off + hh * 512 + 512],
                                        cmask[:, tt * QBLK:(tt + 1) * QBLK]),
                                        wait=[(PE, sc_mm[j])] if hh == 0 else [(VE, mmk)])
                                mex = s_(lambda e, slot=slot, pst=pst, off=off: e.activation(
                                    out=expT[:, slot * 1024:slot * 1024 + 1024],
                                    in_=pst[:, off:off + 1024], func=AF.Exp,
                                    bias=negC[:, :], scale=sc),
                                    wait=exw + [(VE, mmk)])
                            else:
                                mex = s_(lambda e, slot=slot, pst=pst, off=off: e.activation(
                                    out=expT[:, slot * 1024:slot * 1024 + 1024],
                                    in_=pst[:, off:off + 1024], func=AF.Exp,
                                    bias=negC[:, :], scale=sc),
                                    wait=exw)
                            used(g0, SE, mex)
                            used(g1, SE, mex)
                            for hh in range(2):
                                zreg = ["A0", "A1"][hh]
                                wz = take(zreg) if j == 0 else []
                                mav = p.emit("tensor", lambda e, j=j, hh=hh,
                                             slot=slot, zreg=zreg, nj=nj,
                                             vp=vp: e.matmul(
                                    reg_ap(zreg)[0:65, :],
                                    vp[:, j * 130 + hh * 65:j * 130 + hh * 65 + 65],
                                    expT[:, slot * 1024 + hh * 512:
                                         slot * 1024 + hh * 512 + 512],
                                    start=(j == 0), stop=(j == nj - 1),
                                    skip_group_check=True),
                                    wait=(wz + [(SE, mex)]),
                                    inc=[(PE, 1)])[0]
                                av_done[hh] = mav
                            expT_slots[slot] = av_done[1]
                            kp_pe = av_done[1]

                        m_exp_last = 0
                        for j in range(nj):
                            b_scores(j)
                            if j == 1 and pending_norm[0] is not None:
                                pending_norm[0]()
                                pending_norm[0] = None
                            if j >= 1:
                                b_expav(j - 1)
                        b_expav(nj - 1)
                        kp_use[sl] = av_done[1]
                        vp_use[sl] = av_done[1]
                        zsl = hp % 2
                        mze_h = [0, 0]
                        for hh in range(2):
                            zreg = ["A0", "A1"][hh]
                            mze = v(lambda e, zsl=zsl, hh=hh, zreg=zreg: e.tensor_copy(
                                out=xpan[0:65, zsl * 1024 + hh * 512:
                                         zsl * 1024 + hh * 512 + 512],
                                in_=reg_ap(zreg)[0:65, :]),
                                wait=[(PE, av_done[hh]), (VE, zev_use[zsl])])
                            used(zreg, VE, mze)
                            mze_h[hh] = mze

                        def do_norm(hp=hp, zsl=zsl, mze_h=mze_h):
                            mzl = 0
                            for hh in range(2):
                                h = hp * 2 + hh
                                c, r = h // 2, (h % 2) * 64
                                mfq = v(lambda e, zsl=zsl, hh=hh: e.reciprocal(
                                    out=fq[:, :],
                                    in_=xpan[64:65, zsl * 1024 + hh * 512:
                                             zsl * 1024 + hh * 512 + 512]),
                                    wait=[(VE, mze_h[hh])])
                                wB = take("S")
                                mnb = t(lambda e: e.matmul(
                                    pS[0:64, 0:512], ones_r[:, 0:64], fq[:, :],
                                    start=True, stop=True),
                                    wait=wB + [(VE, mfq)])
                                mnbe = v(lambda e: e.tensor_copy(
                                    out=normB[0:64, :], in_=pS[0:64, 0:512]),
                                    wait=[(PE, mnb)])
                                used("S", VE, mnbe)
                                mzl = v(lambda e, c=c, r=r, zsl=zsl, hh=hh: e.tensor_mul(
                                    zst[r:r + 64, c * 512:(c + 1) * 512],
                                    xpan[0:64, zsl * 1024 + hh * 512:
                                         zsl * 1024 + hh * 512 + 512],
                                    normB[0:64, :]),
                                    wait=[(VE, mnbe)])
                            zev_use[zsl] = mzl
                            return mzl

                        if hp < 7:
                            pending_norm[0] = do_norm
                        else:
                            m_z = do_norm()
                    mzo = 0
                    for dc in range(NCD):
                        reg = ["B0", "B1"][dc % 2]
                        wB = take(reg)
                        fns = []
                        for c in range(NCD):
                            fns.append(lambda e, c=c, dc=dc, reg=reg: e.matmul(
                                reg_ap(reg),
                                wproj[:, c * 1024 + dc * 128:c * 1024 + dc * 128 + 128],
                                zst[:, c * 512:(c + 1) * 512],
                                start=(c == 0), stop=(c == NCD - 1)))
                        mm = tq(fns, wait=wB + [(VE, m_z), (sWP, mwo)])
                        mzo = v(lambda e, dc=dc, reg=reg: e.tensor_scalar_add(
                            zst2[:, dc * 512:(dc + 1) * 512], reg_ap(reg),
                            bo_sb[:, dc:dc + 1]),
                            wait=[(PE, mm), (sBO, 16)])
                        used(reg, VE, mzo)
                    m_z = mzo
                    zsrc = zst2

                for po in range(4):
                    pi = b * 4 + po
                    tb = po % 2
                    mlr = dq(sXR[tb], lambda e, pi=pi, tb=tb: e.dma_start(
                        out=xres[tb], in_=xsrc[pi * 128:(pi + 1) * 128, :]),
                        wait=[(VE, tail["xres"][tb])])
                    wT = take("T")
                    for c in range(NCD):
                        mtl = p.emit("tensor", lambda e, c=c, po=po, zsrc=zsrc: e.transpose(
                            pT[:, c * 128:(c + 1) * 128],
                            zsrc[:, c * 512 + po * 128:c * 512 + po * 128 + 128],
                            ident_bf[:, :]),
                            wait=(wT + [(VE, m_z)]) if c == 0 else (),
                            inc=[(PE, 1)] if c == NCD - 1 else [])
                    mtr = mtl[0]
                    mad = v(lambda e, tb=tb: e.tensor_add(
                        outst[tb], pT[:, :], xres[tb]),
                        wait=[(PE, mtr), (sXR[tb], mlr), (sOS[tb], tail["outst"][tb])])
                    used("T", VE, mad)
                    tail["xres"][tb] = mad
                    tail["outst"][tb] = dq(sOS[tb], lambda e, pi=pi, tb=tb: e.dma_start(
                        out=xdst[pi * 128:(pi + 1) * 128, :], in_=outst[tb]),
                        wait=[(VE, mad)])
                qb_prev = PE.n

        def emit_ffn(sub):
            xsrc = resid[sub]
            xdst = resid[sub + 1]
            bfw = (sub == 3)
            wb = (sub == 1)
            mxe_prev = [0]
            rt_se = [0]
            wb1_use = [0, 0]
            wb2_use = 0
            v(lambda e: e.memset(rrec[:, :], 0.0))
            fW1 = sW1 if bfw else sW1c
            fW2 = sW2 if bfw else sW2c
            mgw = dq(sGW, lambda e: e.dma_start(
                out=gate_sb.ap().rearrange("p (c e2) -> p c e2", c=NCD),
                in_=gate_td.rearrange("(c p) e2 -> p c e2", p=128)))
            mz0 = v(lambda e: e.memset(offrow[:, :], 0.0))
            wS = take("S")
            fns = []
            for c in range(NCD):
                fns.append(lambda e, c=c: e.matmul(
                    pS[0:1, 0:NE], ones_cf[:, :], gate_sb[:, c * NE:(c + 1) * NE],
                    start=(c == 0), stop=(c == NCD - 1)))
            mgs = tq(fns, wait=wS + [(sGW, mgw)])
            mge = v(lambda e: e.tensor_copy(out=gsrow[:, :], in_=pS[0:1, 0:NE]),
                    wait=[(PE, mgs)])
            used("S", VE, mge)
            wS = take("S")
            mgb = t(lambda e: e.matmul(
                pS[:, 0:NE], ones_r[:, :], gsrow[:, :], start=True, stop=True),
                wait=wS + [(VE, mge)])
            mgbe = v(lambda e: e.tensor_copy(out=gsumB[:, :], in_=pS[:, 0:NE]),
                     wait=[(PE, mgb)])
            used("S", VE, mgbe)
            gsum4 = ARF[:, 2240:2304]
            iota4 = ARF[:, 2304:2368]
            for k4 in range(4):
                mgbe = v(lambda e, k4=k4: e.tensor_copy(
                    out=gsum4[:, k4 * 16:(k4 + 1) * 16], in_=gsumB[:, :]),
                    wait=[(VE, mgbe)])
                miot = v(lambda e, k4=k4: e.tensor_copy(
                    out=iota4[:, k4 * 16:(k4 + 1) * 16], in_=iotaE[:, :]))
            if DEBUG_RT:
                dq(sDB, lambda e: e.dma_start(out=dbg[4], in_=gsum4[:, :]),
                   wait=[(VE, miot)])

            st = new_st()
            st["xln_dst"] = AR2[:, 0:4096]
            st["xln_slots"] = 4
            m_off = mz0
            mct_prev = 0
            rt_ve = 0
            for bi in range(8):
                pi0 = 4 * bi
                sc_ms = []
                gate_pe = []
                for po in range(4):
                    pi = pi0 + po
                    pb_ = pi % 2
                    ml, mv2, mv6, mv7 = emit_ln_panel(xsrc, pi, pb_, st)
                    sc_ms.append(mv7)
                    mv8 = v(lambda e, pi=pi: e.tensor_copy(
                        out=rrec[:, 2 * pi:2 * pi + 1], in_=lnst[:, 0:1]),
                        wait=[(VE, mv2)])
                    mv9 = v(lambda e, pi=pi: e.tensor_copy(
                        out=rrec[:, 2 * pi + 1:2 * pi + 2], in_=lnst[:, 3:4]),
                        wait=[(VE, mv6)])
                    for rnd in range(2):
                        wB = take("B0")
                        for cc in range(4):
                            c = rnd * 4 + cc
                            mtl = p.emit("tensor", lambda e, c=c, cc=cc, pb_=pb_: e.transpose(
                                psB[:, cc * 128:(cc + 1) * 128],
                                xpan[:, pb_ * D + c * 128:pb_ * D + c * 128 + 128],
                                ident_f32[:, :]),
                                wait=(wB + [(sXP[pb_], ml)]) if cc == 0 else (),
                                inc=[(PE, 1)] if cc == 3 else [])
                        mtr = mtl[0]
                        mxe = v(lambda e, rnd=rnd: e.tensor_copy(
                            out=xfm[:, rnd * 512:(rnd + 1) * 512], in_=psB[:, 0:512]),
                            wait=[(PE, mtr), (VE, mxe_prev[0])])
                        used("B0", VE, mxe)
                    mxe_prev[0] = mxe
                    st["xpan_p"][pb_] = mtr
                    st["xpan_v"][pb_] = max(st["xpan_v"][pb_], mxe)
                    wS = take("S") if po == 0 else []
                    fns = []
                    for c in range(NCD):
                        fns.append(lambda e, c=c, po=po: e.matmul(
                            pS[:, po * 16:(po + 1) * 16],
                            xfm[:, c * 128:(c + 1) * 128],
                            gate_sb[:, c * NE:(c + 1) * NE],
                            start=(c == 0), stop=(c == NCD - 1),
                            skip_group_check=True))
                    gate_pe.append(tq(fns, wait=wS + [(VE, mxe), (VE, mgbe)]))
                if DEBUG_RT and bi == 0:
                    dq(sDB, lambda e: e.dma_start(out=dbg_x, in_=xfm[:, :]),
                       wait=[(VE, mxe)])
                    mdp = v(lambda e: e.tensor_copy(
                        out=ARF[:, 2368:2432], in_=pS[:, 0:64]),
                        wait=[(PE, gate_pe[3])])
                    used("S", VE, mdp)
                    dq(sDB, lambda e: e.dma_start(
                        out=dbg[0], in_=ARF[:, 2368:2432]), wait=[(VE, mdp)])
                def bview(col0, rep=16):
                    return bass.AP(rrec.ap().tensor, 2 * pi0 + col0,
                                   [[2 * NP, 128], [2, 4], [0, rep]])
                mb1 = v(lambda e, bv=bview(0): e.tensor_tensor(
                    out=routg[:, 0:64].rearrange("p (g r) -> p g r", g=4),
                    in0=gsum4.rearrange("p (g r) -> p g r", g=4),
                    in1=bv, op=ALU.mult),
                    wait=[(VE, VE.n), (PE, mct_prev)])
                if DEBUG_RT and bi == 0:
                    mdp1 = v(lambda e: e.tensor_copy(
                        out=ARF[:, 2432:2496], in_=routg[:, 0:64]),
                        wait=[(VE, mb1)])
                    dq(sDB, lambda e: e.dma_start(
                        out=dbg[1], in_=ARF[:, 2432:2496]), wait=[(VE, mdp1)])
                mb2 = v(lambda e: e.tensor_sub(
                    routg[:, 64:128].rearrange("p (g r) -> p g r", g=4),
                    bass.AP(pS.ap().tensor, 0, [[512, 128], [16, 4], [1, 16]]),
                    routg[:, 0:64].rearrange("p (g r) -> p g r", g=4)),
                    wait=[(PE, gate_pe[3]), (VE, mb1)])
                used("S", VE, mb2)
                mb3 = v(lambda e, bv=bview(1): e.tensor_tensor(
                    out=routg[:, 64:128].rearrange("p (g r) -> p g r", g=4),
                    in0=routg[:, 64:128].rearrange("p (g r) -> p g r", g=4),
                    in1=bv, op=ALU.mult), wait=[(VE, mb2)])
                if DEBUG_RT and bi == 0:
                    mdp3 = v(lambda e: e.tensor_copy(
                        out=ARF[:, 2496:2560], in_=routg[:, 64:128]),
                        wait=[(VE, mb3)])
                    dq(sDB, lambda e: e.dma_start(
                        out=dbg[2], in_=ARF[:, 2496:2560]), wait=[(VE, mdp3)])
                mb4 = s_(lambda e: e.activation(
                    out=routg[:, 128:192], in_=routg[:, 64:128], func=AF.Exp),
                    wait=[(VE, mb3), (SE, rt_se[0])])
                rt_se[0] = mb4
                mb5 = v(lambda e: e.reduce_max(
                    out=rout1[:, 0:4], in_=routg[:, 64:128].rearrange(
                        "p (g r) -> p g r", g=4), axis=AX.X), wait=[(VE, mb3)])
                mb6 = v(lambda e: e.reduce_max(
                    out=rout1[:, 4:8], in_=routg[:, 128:192].rearrange(
                        "p (g r) -> p g r", g=4), axis=AX.X), wait=[(SE, mb4)])
                mb7 = v(lambda e: e.reduce_sum(
                    out=rout1[:, 8:12], in_=routg[:, 128:192].rearrange(
                        "p (g r) -> p g r", g=4), axis=AX.X), wait=[(SE, mb4)])
                mb8 = v(lambda e: e.reciprocal(
                    out=rout1[:, 12:16], in_=rout1[:, 8:12]), wait=[(VE, mb7)])
                mb9 = v(lambda e: e.tensor_mul(
                    rout1[:, 16:20], rout1[:, 4:8], rout1[:, 12:16]),
                    wait=[(VE, mb8), (VE, mb6)])
                dbgw12 = []
                if DEBUG_RT and bi == 0:
                    dq(sDB, lambda e: e.dma_start(
                        out=dbg[3, :, 0:20], in_=rout1[:, 0:20]),
                        wait=[(VE, mb9)])
                    dbgw12 = [(sDB, sDB.n)]
                mb10 = v(lambda e: e.tensor_tensor(
                    out=routg[:, 0:64].rearrange("p (g r) -> p g r", g=4),
                    in0=routg[:, 64:128].rearrange("p (g r) -> p g r", g=4),
                    in1=bass.AP(rout1.ap().tensor, 0, [[32, 128], [1, 4], [0, 16]]),
                    op=ALU.is_equal), wait=[(VE, mb5), (VE, mb9)])
                rank_pe = []
                for po in range(4):
                    wS2 = []
                    p.emit("tensor", lambda e, po=po: e.matmul(
                        pS[:, 64 + po * 16:64 + (po + 1) * 16], ustrict[:, :],
                        routg[:, po * 16:(po + 1) * 16],
                        start=True, stop=False, skip_group_check=True),
                        wait=[(VE, mb10), (VE, m_off)] if po == 0 else [(VE, m_off)])
                    mrk = t(lambda e, po=po: e.matmul(
                        pS[:, 64 + po * 16:64 + (po + 1) * 16], ones_r[:, :],
                        offrow[:, :], start=False, stop=True,
                        skip_group_check=True))
                    mct = t(lambda e, po=po: e.matmul(
                        pS[0:1, 128 + po * 16:128 + (po + 1) * 16], ones_cf[:, :],
                        routg[:, po * 16:(po + 1) * 16],
                        start=True, stop=True, skip_group_check=True))
                    m_off = v(lambda e, po=po: e.tensor_add(
                        offrow[:, :], offrow[:, :],
                        pS[0:1, 128 + po * 16:128 + (po + 1) * 16]),
                        wait=[(PE, mct)])
                    used("S", VE, m_off)
                    rank_pe.append(mrk)
                    mct_prev = mct
                mb11 = v(lambda e: e.tensor_mul(
                    routg[:, 128:192].rearrange("p (g r) -> p g r", g=4),
                    routg[:, 0:64].rearrange("p (g r) -> p g r", g=4),
                    pS.ap().rearrange("p (g r) -> p g r", g=4)[:, :, 4:20][:, :, 0:16]
                    if False else
                    bass.AP(pS.ap().tensor, 64, [[512, 128], [16, 4], [1, 16]])),
                    wait=[(PE, rank_pe[3]), (VE, mb9)])
                used("S", VE, mb11)
                mb12 = v(lambda e: e.reduce_sum(
                    out=rout1[:, 4:8], in_=routg[:, 128:192].rearrange(
                        "p (g r) -> p g r", g=4), axis=AX.X),
                    wait=[(VE, mb11)] + dbgw12)
                mb13 = v(lambda e: e.tensor_tensor(
                    out=routg[:, 128:192].rearrange("p (g r) -> p g r", g=4),
                    in0=routg[:, 0:64].rearrange("p (g r) -> p g r", g=4),
                    in1=iota4.rearrange("p (g r) -> p g r", g=4),
                    op=ALU.mult), wait=[(VE, mb12)])
                mb14 = v(lambda e: e.reduce_sum(
                    out=rout1[:, 8:12], in_=routg[:, 128:192].rearrange(
                        "p (g r) -> p g r", g=4), axis=AX.X), wait=[(VE, mb13)])
                mb15 = v(lambda e: e.tensor_scalar(
                    out=rout1[:, 12:16], in0=rout1[:, 4:8],
                    scalar1=float(LOCAL_CAP), scalar2=None, op0=ALU.is_lt),
                    wait=[(VE, mb12)])
                mb16 = v(lambda e, pi0=pi0: e.tensor_mul(
                    wtok[:, pi0:pi0 + 4], rout1[:, 16:20], rout1[:, 12:16]),
                    wait=[(VE, mb15)])
                mb17 = v(lambda e: e.scalar_tensor_tensor(
                    out=rout1[:, 20:24], in0=rout1[:, 8:12],
                    scalar=float(LOCAL_CAP), in1=rout1[:, 4:8],
                    op0=ALU.mult, op1=ALU.add), wait=[(VE, mb14), (VE, mb15)])
                mb18 = v(lambda e: e.scalar_tensor_tensor(
                    out=rout1[:, 20:24], in0=rout1[:, 20:24],
                    scalar=-float(NSLOT), in1=rout1[:, 12:16],
                    op0=ALU.add, op1=ALU.mult), wait=[(VE, mb17)])
                mb19 = v(lambda e: e.tensor_scalar_add(
                    rout1[:, 20:24], rout1[:, 20:24], float(NSLOT)),
                    wait=[(VE, mb18)])
                mb20 = v(lambda e, pi0=pi0: e.tensor_copy(
                    out=sloti[:, pi0:pi0 + 4], in_=rout1[:, 20:24]),
                    wait=[(VE, mb19)])
                rt_ve = mb20
                if DEBUG_RT and bi == 0:
                    rt_ve = v(lambda e: e.tensor_copy(
                        out=rout1[0:1, 24:25], in_=rout1[0:1, 0:1]),
                        wait=[(sDB, sDB.n), (VE, mb20)])
                    mb20 = rt_ve
                for po in range(4):
                    pi = pi0 + po
                    st["xlt_g"][po] = gq(sSC[po], lambda e, pi=pi, po=po: e.indirect_dma_start(
                        out=disp,
                        out_offset=IndirectOffsetOnAxis(sloti[:, pi:pi + 1], 0),
                        in_=st["xln_dst"][:, po * D:(po + 1) * D], in_offset=None),
                        wait=[(VE, mb20), (VE, sc_ms[po]), (sZR, m_zr)])
            m_router_ve = VE.n
            m_scat = [(s, s.n) for s in sSC]

            m_w2_pe = 0
            m_xg_pe = 0
            m_yfm_pe = 0
            m_ytk_ld = 0
            w1_pe = [0, 0]
            mm = 0
            for e_i in range(NE):
                for k3 in range(3):
                    dq(sXG, lambda e, e_i=e_i, k3=k3: e.dma_start(
                        out=xg[:, k3 * D:(k3 + 1) * D],
                        in_=disp[e_i * LOCAL_CAP + k3 * 128:
                                 e_i * LOCAL_CAP + (k3 + 1) * 128, :]),
                        wait=m_scat + [(PE, m_xg_pe)])
                mgx = sXG.n
                mxt = 0
                mtr = 0
                for c in range(NCD):
                    wT = take("T")
                    for sp in range(3):
                        mtl = p.emit("tensor", lambda e, c=c, sp=sp: e.transpose(
                            pT[:, sp * 128:(sp + 1) * 128],
                            xg[:, sp * D + c * 128:sp * D + c * 128 + 128],
                            ident_bf[:, :]),
                            wait=(wT + [(sXG, mgx)]) if sp == 0 else (),
                            inc=[(PE, 1)] if sp == 2 else [])
                    mtr = mtl[0]
                    mxt = v(lambda e, c=c: e.tensor_copy(
                        out=xgT[:, c * LOCAL_CAP:(c + 1) * LOCAL_CAP],
                        in_=pT[:, 0:LOCAL_CAP]),
                        wait=[(PE, mtr)])
                    used("T", VE, mxt)
                m_xg_pe = mtr
                if bfw:
                    mw2 = dq(sW2, lambda e, e_i=e_i: e.dma_start(
                        out=w2sb.rearrange("p (c dd) -> p c dd", c=NFC),
                        in_=w2b[e_i].rearrange("(c p) dd -> p c dd", p=128)),
                        wait=[(PE, m_w2_pe), (sZR, m_zr)])
                else:
                    mw2 = gq(sW2c, lambda e, e_i=e_i: e.dma_start(
                        out=w2sb.rearrange("p (c dd) -> p c dd", c=NFC),
                        in_=w2[e_i].rearrange("(c p) dd -> p c dd", p=128)),
                        wait=[(PE, m_w2_pe), (sZR, m_zr)]
                        + ([(sWB[2], wb2_use)] if wb else []))
                mh_done = 0
                for q4 in range(4):
                    wbq = q4 % 2
                    if bfw:
                        mw1 = dq(sW1[wbq], lambda e, e_i=e_i, q4=q4, wbq=wbq: e.dma_start(
                            out=w1sb[wbq].rearrange("p (c f) -> p c f", c=NCD),
                            in_=w1b[e_i, :, q4 * 1024:(q4 + 1) * 1024].rearrange(
                                "(c p) f -> p c f", p=128)),
                            wait=[(PE, w1_pe[wbq]), (sZR, m_zr)]
                            + (m_scat if e_i == 0 and q4 < 2 else []))
                    else:
                        mw1 = gq(sW1c[wbq], lambda e, e_i=e_i, q4=q4, wbq=wbq: e.dma_start(
                            out=w1sb[wbq].rearrange("p (c f) -> p c f", c=NCD),
                            in_=w1[e_i, :, q4 * 1024:(q4 + 1) * 1024].rearrange(
                                "(c p) f -> p c f", p=128)),
                            wait=[(PE, w1_pe[wbq]), (sZR, m_zr)]
                            + ([(sWB[wbq], wb1_use[wbq])] if wb else [])
                            + (m_scat if e_i == 0 and q4 < 2 else []))
                    for fc in range(8):
                        reg = ["A0", "A1"][fc % 2]
                        wA = take(reg)
                        fns = []
                        for c in range(NCD):
                            fns.append(lambda e, c=c, fc=fc, wbq=wbq, reg=reg: e.matmul(
                                reg_ap(reg)[:, 0:LOCAL_CAP],
                                w1sb[wbq][:, c * 1024 + fc * 128:c * 1024 + fc * 128 + 128],
                                xgT[:, c * LOCAL_CAP:(c + 1) * LOCAL_CAP],
                                start=(c == 0), stop=(c == NCD - 1)))
                        mm = tq(fns, wait=wA + [(fW1[wbq], mw1), (VE, mxt)])
                        fci = q4 * 8 + fc
                        mge2 = s_(lambda e, fci=fci, reg=reg: e.activation(
                            out=hsb[:, fci * LOCAL_CAP:(fci + 1) * LOCAL_CAP],
                            in_=reg_ap(reg)[:, 0:LOCAL_CAP],
                            func=AF.Gelu_apprx_tanh),
                            wait=[(PE, mm)])
                        used(reg, SE, mge2)
                        mh_done = mge2
                    w1_pe[wbq] = mm
                    if wb:
                        wb1_use[wbq] = p.emit(
                            "scalar", lambda e, e_i=e_i, q4=q4, wbq=wbq: e.dma_start(
                                out=w1b[e_i, :, q4 * 1024:(q4 + 1) * 1024].rearrange(
                                    "(c p) f -> p c f", p=128),
                                in_=w1sb[wbq].rearrange("p (c f) -> p c f", c=NCD)),
                            wait=[(sW1c[wbq], mw1)], inc=[(sWB[wbq], 16)])[0]
                if wb:
                    wb2_use = p.emit(
                        "scalar", lambda e, e_i=e_i: e.dma_start(
                            out=w2b[e_i].rearrange("(c p) dd -> p c dd", p=128),
                            in_=w2sb.rearrange("p (c dd) -> p c dd", c=NFC)),
                        wait=[(sW2c, mw2)], inc=[(sWB[2], 16)])[0]
                my_done = 0
                for dc in range(NCD):
                    reg = ["A2", "A3"][dc % 2]
                    wA = take(reg)
                    fns = []
                    for fc in range(NFC):
                        fns.append(lambda e, fc=fc, dc=dc, reg=reg: e.matmul(
                            reg_ap(reg)[:, 0:LOCAL_CAP],
                            w2sb[:, fc * 1024 + dc * 128:fc * 1024 + dc * 128 + 128],
                            hsb[:, fc * LOCAL_CAP:(fc + 1) * LOCAL_CAP],
                            start=(fc == 0), stop=(fc == NFC - 1)))
                    mm = tq(fns, wait=wA + [(SE, mh_done), (fW2, mw2), (PE, m_yfm_pe)])
                    mye = v(lambda e, dc=dc, reg=reg: e.tensor_copy(
                        out=yfm[:, dc * LOCAL_CAP:(dc + 1) * LOCAL_CAP],
                        in_=reg_ap(reg)[:, 0:LOCAL_CAP]),
                        wait=[(PE, mm)])
                    used(reg, VE, mye)
                    my_done = mye
                m_w2_pe = mm
                for sp in range(3):
                    wB = take("B0") + take("B1")
                    for c in range(NCD):
                        mtl = p.emit("tensor", lambda e, c=c, sp=sp: e.transpose(
                            psB[:, c * 128:(c + 1) * 128],
                            yfm[:, c * LOCAL_CAP + sp * 128:c * LOCAL_CAP + sp * 128 + 128],
                            ident_f32[:, :]),
                            wait=(wB + [(VE, my_done)]) if c == 0 else (),
                            inc=[(PE, 1)] if c == NCD - 1 else [])
                    mtr = mtl[0]
                    m_yfm_pe = mtr
                    mty = v(lambda e, sp=sp: e.tensor_copy(
                        out=ytk[:, sp * D:(sp + 1) * D], in_=psB[:, :]),
                        wait=[(PE, mtr), (sYS, m_ytk_ld)])
                    used("B0", VE, mty)
                    used("B1", VE, mty)
                    m_ytk_ld = dq(sYS, lambda e, e_i=e_i, sp=sp: e.dma_start(
                        out=yexp[e_i * LOCAL_CAP + sp * 128:
                                 e_i * LOCAL_CAP + (sp + 1) * 128, :],
                        in_=ytk[:, sp * D:(sp + 1) * D]),
                        wait=[(VE, mty), (sZR, m_zr)])
            m_yexp_done = m_ytk_ld

            comb = dict(xres=[0, 0], outst=[0, 0])
            ygab = [yga, AR2[:, 34816:35840]]
            sYGb = [sYG, sYG2]
            yg_v = [m_router_ve, m_router_ve]
            mad_prev = m_router_ve
            for pi in range(NP):
                tb = pi % 2
                mlr = dq(sXR[tb], lambda e, pi=pi, tb=tb: e.dma_start(
                    out=xres[tb], in_=xsrc[pi * 128:(pi + 1) * 128, :]),
                    wait=[(VE, comb["xres"][tb]), (PE, m_yfm_pe)])
                mga = gq(sYGb[tb], lambda e, pi=pi, tb=tb: e.indirect_dma_start(
                    out=ygab[tb][:, :], out_offset=None,
                    in_=yexp, in_offset=IndirectOffsetOnAxis(sloti[:, pi:pi + 1], 0)),
                    wait=[(sYS, m_yexp_done), (VE, yg_v[tb])])
                msc = v(lambda e, pi=pi, tb=tb: e.tensor_scalar(
                    out=ysc[:, :], in0=ygab[tb][:, :], scalar1=wtok[:, pi:pi + 1],
                    scalar2=None, op0=ALU.mult),
                    wait=[(sYGb[tb], mga), (VE, mad_prev)])
                yg_v[tb] = msc
                mad = v(lambda e, tb=tb: e.tensor_add(
                    outst[tb], ysc[:, :], xres[tb]),
                    wait=[(VE, msc), (sXR[tb], mlr), (sOS[tb], comb["outst"][tb])])
                comb["xres"][tb] = mad
                mad_prev = mad
                comb["outst"][tb] = dq(sOS[tb], lambda e, pi=pi, tb=tb: e.dma_start(
                    out=xdst[pi * 128:(pi + 1) * 128, :], in_=outst[tb]),
                    wait=[(VE, mad)])

        subs = [("attn", "a"), ("ffn", None), ("attn", "b"), ("ffn", None)]
        for si in range(n_sub):
            kind = subs[si]
            if kind[0] == "attn":
                emit_attn(si, kind[1])
            else:
                emit_ffn(si)
            p.barrier(skip=("sCC0", "sCC1", "sCC2", "sCC3"))

        block = stack.enter_context(nc.Block())
        p.finalize(block)
    return nc


def _layout_inputs(inputs):
    x = np.ascontiguousarray(np.asarray(inputs["L0_attn_a_x"], dtype=np.float32))
    common = dict(
        wq_a=np.ascontiguousarray(np.asarray(inputs["L0_attn_a_Wq"]).T),
        wk_a=np.ascontiguousarray(np.asarray(inputs["L0_attn_a_Wk"]).T),
        wv_a=np.ascontiguousarray(np.asarray(inputs["L0_attn_a_Wv"]).T),
        wq_b=np.ascontiguousarray(np.asarray(inputs["L0_attn_b_W_Q"]).reshape(D, D)),
        wk_b=np.ascontiguousarray(np.asarray(inputs["L0_attn_b_W_K"]).reshape(D, D)),
        wv_b=np.ascontiguousarray(np.asarray(inputs["L0_attn_b_W_V"]).reshape(D, D)),
        wo_b=np.ascontiguousarray(
            np.transpose(np.asarray(inputs["L0_attn_b_W_out"]), (1, 0, 2)).reshape(D, D)),
        bo_b=np.ascontiguousarray(np.asarray(inputs["L0_attn_b_b_out"]).reshape(D, 1)),
        gate_td=np.ascontiguousarray(np.asarray(inputs["L0_ffn_gate_w"]).T),
        w1=np.ascontiguousarray(np.asarray(inputs["L0_ffn_w1"])),
        w2=np.ascontiguousarray(np.asarray(inputs["L0_ffn_w2"])),
    )
    in_maps = []
    for c in range(8):
        m = dict(common)
        m["x_in"] = np.ascontiguousarray(x[c])
        in_maps.append(m)
    return in_maps


def kernel(**inputs):
    in_maps = _layout_inputs(inputs)
    nc = build_program(4)
    res = run_bass_kernel_spmd(nc, in_maps, core_ids=list(range(8)))
    return np.stack([res.results[c]["y_out"] for c in range(8)], axis=0)


# version 19
# speedup vs baseline: 1.0686x; 1.0562x over previous
import contextlib

DEBUG_RT = False
import numpy as np

import concourse.bass as bass
import concourse.mybir as mybir
from concourse.bass import IndirectOffsetOnAxis
from concourse.bass_utils import run_bass_kernel_spmd

F32 = mybir.dt.float32
BF16 = mybir.dt.bfloat16
I32 = mybir.dt.int32
AF = mybir.ActivationFunctionType
ALU = mybir.AluOpType
AX = mybir.AxisListType

D = 1024
S = 4096
NP = S // 128
NCD = D // 128
DFF = 4096
NFC = DFF // 128
NE = 16
LOCAL_CAP = 384
NSLOT = NE * LOCAL_CAP
QBLK = 512
NBLK = S // QBLK
MASK_NEG = -1.0e6

ENGINES = ("sync", "tensor", "vector", "scalar", "gpsimd")


class Sem:
    def __init__(self, handle, name):
        self.h = handle
        self.name = name
        self.n = 0


class Prog:
    def __init__(self, nc, stack):
        self.nc = nc
        self.q = {e: [] for e in ENGINES}
        self._stack = stack
        self._last_wait = {}
        self.sems = []

    def sem(self, name):
        h = self._stack.enter_context(self.nc.semaphore(name))
        s = Sem(h, name)
        self.sems.append(s)
        return s

    def emit(self, eng, fn, wait=(), inc=()):
        wait = [w for w in wait if w is not None]
        inc = list(inc)
        for s, v in wait:
            assert v <= s.n, (
                f"[{eng}] wait {s.name}>={v} but only {s.n} recorded (deadlock)")
        ms = []
        for s, k in inc:
            s.n += k
            ms.append(s.n)
        eff = []
        for s, v in wait:
            key = (eng, s.name)
            if self._last_wait.get(key, -1) >= v:
                continue
            self._last_wait[key] = v
            eff.append((s, v))

        def run(engine, _fn=fn, _w=eff, _i=inc):
            for s, v in _w:
                engine.wait_ge(s.h, v)
            inst = _fn(engine)
            if inst is not None:
                for s, k in _i:
                    inst.then_inc(s.h, k)
            return inst

        self.q[eng].append(run)
        return ms

    def barrier(self, skip=()):
        snap = [(s, s.n) for s in self.sems if s.name not in skip]
        for eng in ENGINES:
            self.emit(eng, lambda e: None, wait=snap)

    def finalize(self, block):
        @block.sync
        def _(e):
            for f in self.q["sync"]:
                f(e)

        @block.tensor
        def _(e):
            for f in self.q["tensor"]:
                f(e)

        @block.vector
        def _(e):
            for f in self.q["vector"]:
                f(e)

        @block.scalar
        def _(e):
            for f in self.q["scalar"]:
                f(e)

        @block.gpsimd
        def _(e):
            for f in self.q["gpsimd"]:
                f(e)


def build_program(n_sub=4):
    nc = bass.Bass("TRN2", target_bir_lowering=False, debug=False)

    x_in = nc.dram_tensor("x_in", [S, D], F32, kind="ExternalInput").ap()
    wq_a = nc.dram_tensor("wq_a", [128, NCD, D], F32, kind="ExternalInput").ap()
    wk_a = nc.dram_tensor("wk_a", [128, NCD, D], F32, kind="ExternalInput").ap()
    wv_a = nc.dram_tensor("wv_a", [128, NCD, D], F32, kind="ExternalInput").ap()
    wq_b = nc.dram_tensor("wq_b", [128, NCD, D], F32, kind="ExternalInput").ap()
    wk_b = nc.dram_tensor("wk_b", [128, NCD, D], F32, kind="ExternalInput").ap()
    wv_b = nc.dram_tensor("wv_b", [128, NCD, D], F32, kind="ExternalInput").ap()
    wo_b = nc.dram_tensor("wo_b", [128, NCD, D], F32, kind="ExternalInput").ap()
    bo_b = nc.dram_tensor("bo_b", [D, 1], F32, kind="ExternalInput").ap()
    gate_td = nc.dram_tensor("gate_td", [D, NE], F32, kind="ExternalInput").ap()
    w1 = nc.dram_tensor("w1", [NE, 4, 128, NCD * 1024], F32, kind="ExternalInput").ap()
    w2 = nc.dram_tensor("w2", [NE, 128, NFC * 1024], F32, kind="ExternalInput").ap()
    y_out = nc.dram_tensor("y_out", [S, D], F32, kind="ExternalOutput").ap()

    resid = [x_in]
    for i in range(1, n_sub):
        resid.append(nc.dram_tensor(f"resid{i}", [S, D], F32, kind="Internal").ap())
    resid.append(y_out)

    qbuf = nc.dram_tensor("qbuf", [NCD, 128, S], BF16, kind="Internal").ap()
    kbufA = nc.dram_tensor("kbufA", [NP, 128, D], BF16, kind="Internal").ap()
    kbufB = nc.dram_tensor("kbufB", [16, 64, S], BF16, kind="Internal").ap()
    vbufA = nc.dram_tensor("vbufA", [NP, 128, 1040], BF16, kind="Internal").ap()
    vbufB = nc.dram_tensor("vbufB", [8, S, 130], BF16, kind="Internal").ap()
    w1b = nc.dram_tensor("w1b", [NE, 4, 128, NCD * 1024], BF16, kind="Internal").ap()
    w2b = nc.dram_tensor("w2b", [NE, 128, NFC * 1024], BF16, kind="Internal").ap()
    disp = nc.dram_tensor("disp", [NSLOT + 1, D], BF16, kind="Internal").ap()
    dbg = nc.dram_tensor("dbg", [5, 128, 64], F32, kind="Internal").ap()
    dbg_x = nc.dram_tensor("dbg_x", [128, 1024], F32, kind="Internal").ap()
    yexp = nc.dram_tensor("yexp", [NSLOT + 1, D], BF16, kind="Internal").ap()

    stack = contextlib.ExitStack()
    with stack:
        p = Prog(nc, stack)
        PE = p.sem("pe")
        VE = p.sem("ve")
        SE = p.sem("se")
        GC = p.sem("gc")
        sXP = [p.sem("sXP0"), p.sem("sXP1")]
        sXR = [p.sem("sXR0"), p.sem("sXR1")]
        sOS = [p.sem("sOS0"), p.sem("sOS1")]
        sQS = [p.sem("sQS0"), p.sem("sQS1")]
        sKS = [p.sem("sKS0"), p.sem("sKS1")]
        sVS = p.sem("sVS")
        sQB = p.sem("sQB")
        sKT = [p.sem(f"sKT{i}") for i in range(4)]
        sVT = [p.sem(f"sVT{i}") for i in range(4)]
        sKP2 = [p.sem("sKP0"), p.sem("sKP1")]
        sVP2 = [p.sem("sVP0"), p.sem("sVP1")]
        sWP = p.sem("sWP")
        sBO = p.sem("sBO")
        sGW = p.sem("sGW")
        sSC = [p.sem(f"sSC{i}") for i in range(4)]
        sXG = p.sem("sXG")
        sW1 = [p.sem("sW10"), p.sem("sW11")]
        sW2 = p.sem("sW2")
        sW1c = [p.sem("sW10c"), p.sem("sW11c")]
        sW2c = p.sem("sW2c")
        sYS = p.sem("sYS")
        sYG = p.sem("sYG")
        sZR = p.sem("sZR")
        sWB = [p.sem("sWB0"), p.sem("sWB1"), p.sem("sWB2")]
        sCA = [p.sem(f"sCA{i}") for i in range(4)]
        sCC = [p.sem(f"sCC{i}") for i in range(4)]
        sYG2 = p.sem("sYG2")
        sDB = p.sem("sDB")
        cast_go = [0]
        sb = stack.enter_context

        ident_bf = sb(nc.sbuf_tensor("ident_bf", [128, 128], BF16))
        ident_f32 = sb(nc.sbuf_tensor("ident_f32", [128, 128], F32))
        ustrict = sb(nc.sbuf_tensor("ustrict", [128, 128], F32))
        cmask = sb(nc.sbuf_tensor("cmask", [128, 4 * QBLK], BF16))
        iotaE = sb(nc.sbuf_tensor("iotaE", [128, NE], F32))
        ones_r = sb(nc.sbuf_tensor("ones_r", [1, 128], F32))
        ones_cb = sb(nc.sbuf_tensor("ones_cb", [128, 1], BF16))
        ones_cf = sb(nc.sbuf_tensor("ones_cf", [128, 1], F32))

        AR1 = sb(nc.sbuf_tensor("AR1", [128, 32768], BF16))
        AR2 = sb(nc.sbuf_tensor("AR2", [128, 39040], BF16))
        wproj = AR1[:, 0:24576]
        expT = AR2[:, 0:16384]
        zst = AR2[:, 16384:20480]
        zst2 = AR2[:, 20480:24576]
        vpair2 = [AR2[:, 24576:28736], AR2[:, 6144:10304]]
        kpair2 = [AR2[:, 34928:39024], AR2[:, 10304:14400]]
        vtile = [AR2[:, 28736 + i * 1040:28736 + (i + 1) * 1040] for i in range(4)]
        vst = AR2[:, 30816:31856]
        kst = AR2[:, 31856:32880]
        qst = AR2[:, 32880:33904]
        ktile = [AR2[:, 34928 + i * 1024:34928 + (i + 1) * 1024] for i in range(4)]
        w2sb = AR1[:, 0:32768]
        w1sb = [AR2[:, 0:8192], AR2[:, 8192:16384]]
        hsb = AR2[:, 16384:28672]
        xg = AR2[:, 28672:31744]
        xgT = AR2[:, 31744:34816]
        ytk = AR2[:, 34816:37888]
        yga = AR2[:, 37888:38912]

        ARF = sb(nc.sbuf_tensor("ARF", [128, 3072], F32))
        xres = [ARF[:, 0:1024], ARF[:, 1024:2048]]
        sqs = ARF[:, 0:1024]
        normB = ARF[:, 2048:2560]
        fq = ARF[0:1, 2560:3072]
        yfm = ARF[:, 0:3072]

        qgrp = sb(nc.sbuf_tensor("qgrp", [128, NCD * QBLK], BF16))
        xln_tok = sb(nc.sbuf_tensor("xln_tok", [128, 2 * D], BF16))
        xln_fm = sb(nc.sbuf_tensor("xln_fm", [128, 2 * NCD * 512], BF16))
        xpan = sb(nc.sbuf_tensor("xpan", [128, 2 * D], F32))
        outs2 = sb(nc.sbuf_tensor("outs2", [128, 2 * D], F32))
        outst = [outs2[:, 0:1024], outs2[:, 1024:2048]]
        xfm = outs2[:, 0:1024]
        ysc = xpan[:, 0:1024]
        lnsq = sb(nc.sbuf_tensor("lnsq", [128, D], F32))
        lnst = sb(nc.sbuf_tensor("lnst", [128, 8], F32))
        mrun = sb(nc.sbuf_tensor("mrun", [128, 2], F32))
        csc = sb(nc.sbuf_tensor("csc", [1, 8], F32))
        negC = sb(nc.sbuf_tensor("negC", [128, 1], F32))
        bo_sb = sb(nc.sbuf_tensor("bo_sb", [128, NCD], F32))

        gate_sb = sb(nc.sbuf_tensor("gate_sb", [128, NCD * NE], F32))
        gsumB = sb(nc.sbuf_tensor("gsumB", [128, NE], F32))
        gsrow = sb(nc.sbuf_tensor("gsrow", [1, NE], F32))
        offrow = sb(nc.sbuf_tensor("offrow", [1, NE], F32))
        wtok = sb(nc.sbuf_tensor("wtok", [128, NP], F32))
        sloti = sb(nc.sbuf_tensor("sloti", [128, NP], I32))
        rrec = sb(nc.sbuf_tensor("rrec", [128, 2 * NP], F32))
        routg = ARF[:, 2048:2240]
        rout1 = sb(nc.sbuf_tensor("rout1", [128, 32], F32))

        psA = sb(nc.psum_tensor("psA", [128, 2048], F32))
        psB = sb(nc.psum_tensor("psB", [128, 1024], F32))
        pS = sb(nc.psum_tensor("pS", [128, 512], F32))
        pT = sb(nc.psum_tensor("pT", [128, 1024], BF16))

        guards = {r: [] for r in ("A0", "A1", "A2", "A3", "B0", "B1", "S", "T")}

        def reg_ap(r):
            if r[0] == "A":
                i = int(r[1])
                return psA[:, i * 512:(i + 1) * 512]
            if r[0] == "B":
                i = int(r[1])
                return psB[:, i * 512:(i + 1) * 512]
            if r == "S":
                return pS[:, :]
            return pT[:, :]

        def take(r):
            w = guards[r]
            guards[r] = []
            return w

        def used(r, sem, ms):
            guards[r].append((sem, ms))

        def t(fn, wait=()):
            return p.emit("tensor", fn, wait, [(PE, 1)])[0]

        def tq(fns, wait=()):
            for i, fn in enumerate(fns):
                if i == len(fns) - 1:
                    return p.emit("tensor", fn, wait if i == 0 else (), [(PE, 1)])[0]
                p.emit("tensor", fn, wait if i == 0 else ())

        def v(fn, wait=()):
            return p.emit("vector", fn, wait, [(VE, 1)])[0]

        def s_(fn, wait=()):
            return p.emit("scalar", fn, wait, [(SE, 1)])[0]

        def g(fn, wait=()):
            return p.emit("gpsimd", fn, wait, [(GC, 1)])[0]

        def dq(sem, fn, wait=()):
            return p.emit("sync", fn, wait, [(sem, 16)])[0]

        def gq(sem, fn, wait=()):
            return p.emit("gpsimd", fn, wait, [(sem, 16)])[0]

        m = g(lambda e: e.memset(ident_bf[:, :], 0.0))
        g(lambda e: e.affine_select(out=ident_bf[:, :], in_=ident_bf[:, :],
                                    compare_op=ALU.not_equal, fill=1.0, base=0,
                                    pattern=[[-1, 128]], channel_multiplier=1),
          wait=[(GC, m)])
        m = g(lambda e: e.memset(ident_f32[:, :], 0.0))
        g(lambda e: e.affine_select(out=ident_f32[:, :], in_=ident_f32[:, :],
                                    compare_op=ALU.not_equal, fill=1.0, base=0,
                                    pattern=[[-1, 128]], channel_multiplier=1),
          wait=[(GC, m)])
        m = g(lambda e: e.memset(ustrict[:, :], 0.0))
        g(lambda e: e.affine_select(out=ustrict[:, :], in_=ustrict[:, :],
                                    compare_op=ALU.is_ge, fill=1.0, base=0,
                                    pattern=[[-1, 128]], channel_multiplier=1),
          wait=[(GC, m)])
        m = g(lambda e: e.memset(cmask[:, :], 0.0))
        for tt in range(4):
            g(lambda e, tt=tt: e.affine_select(
                out=cmask[:, tt * QBLK:(tt + 1) * QBLK],
                in_=cmask[:, tt * QBLK:(tt + 1) * QBLK],
                compare_op=ALU.is_ge, fill=-1.0e9,
                base=-128 * tt, pattern=[[1, QBLK]], channel_multiplier=-1),
                wait=[(GC, m)])
        g(lambda e: e.iota(iotaE[:, :], pattern=[[1, NE]], base=0,
                           channel_multiplier=0,
                           allow_small_or_imprecise_dtypes=True))
        g(lambda e: e.memset(ones_r[:, :], 1.0))
        g(lambda e: e.memset(ones_cb[:, :], 1.0))
        g(lambda e: e.memset(ones_cf[:, :], 1.0))
        dq(sBO, lambda e: e.dma_start(
            out=bo_sb[:, :], in_=bo_b.rearrange("(c p) o -> p (c o)", p=128),
            allow_slow_non_contiguous=True))
        mzr = g(lambda e: e.memset(AR1[:, 24576:32768], 0.0))
        for buf in (disp, yexp):
            dq(sZR, lambda e, buf=buf: e.dma_start(
                out=buf[NSLOT:NSLOT + 1, :], in_=AR1[0:1, 24576:25600]),
                wait=[(GC, mzr)])
            for r in range(6):
                rows = min(1024, NSLOT - r * 1024)
                dq(sZR, lambda e, buf=buf, r=r, rows=rows: e.dma_start(
                    out=buf[r * 1024:r * 1024 + rows, :].rearrange(
                        "(a p) d -> p a d", p=128) if rows % 128 == 0 else
                    buf[r * 1024:r * 1024 + rows, :],
                    in_=AR1.ap().rearrange("p (a d) -> p a d", d=1024)[:, 24:24 + rows // 128]
                    if rows % 128 == 0 else AR1[0:rows, 24576:24576 + 1024]),
                    wait=[(GC, mzr)])
        m_zr = sZR.n
        p.barrier(skip=("sZR",))

        def emit_ln_panel(xsrc, pi, pb_, st):
            ml = dq(sXP[pb_], lambda e, pi=pi, pb_=pb_: e.dma_start(
                out=xpan[:, pb_ * D:(pb_ + 1) * D],
                in_=xsrc[pi * 128:(pi + 1) * 128, :]),
                wait=[(VE, st["xpan_v"][pb_]), (SE, st["xpan_s"][pb_]),
                      (PE, st["xpan_p"][pb_])])
            mv1 = v(lambda e, pb_=pb_: e.reduce_sum(
                out=lnst[:, 0:1], in_=xpan[:, pb_ * D:(pb_ + 1) * D], axis=AX.X),
                wait=[(sXP[pb_], ml), (VE, st["lnst_v"]), (SE, st["lnst_s"])])
            mv2 = v(lambda e: e.tensor_scalar_mul(
                lnst[:, 0:1], lnst[:, 0:1], 1.0 / D), wait=[(VE, mv1)])
            ms1 = s_(lambda e, pb_=pb_: e.activation(
                out=lnsq[:, :], in_=xpan[:, pb_ * D:(pb_ + 1) * D], func=AF.Square),
                wait=[(sXP[pb_], ml), (VE, st["sq_v"])])
            st["xpan_s"][pb_] = ms1
            mv3 = v(lambda e: e.reduce_sum(
                out=lnst[:, 1:2], in_=lnsq[:, :], axis=AX.X), wait=[(SE, ms1)])
            st["sq_v"] = mv3
            mv4 = v(lambda e: e.tensor_scalar(
                out=lnst[:, 2:3], in0=lnst[:, 0:1], scalar1=lnst[:, 0:1],
                scalar2=1e-5, op0=ALU.mult, op1=ALU.subtract), wait=[(VE, mv2)])
            mv5 = v(lambda e: e.scalar_tensor_tensor(
                out=lnst[:, 2:3], in0=lnst[:, 1:2], scalar=1.0 / D,
                in1=lnst[:, 2:3], op0=ALU.mult, op1=ALU.subtract),
                wait=[(VE, mv4), (VE, mv3)])
            ms2 = s_(lambda e: e.activation(
                out=lnst[:, 3:4], in_=lnst[:, 2:3], func=AF.Sqrt),
                wait=[(VE, mv5)])
            st["lnst_s"] = ms2
            mv6 = v(lambda e: e.reciprocal(
                out=lnst[:, 3:4], in_=lnst[:, 3:4]), wait=[(SE, ms2)])
            xld = st.get("xln_dst", xln_tok)
            nsl = st.get("xln_slots", 2)
            sl2 = pi % nsl
            mv7 = v(lambda e, pb_=pb_, sl2=sl2, xld=xld: e.tensor_scalar(
                out=xld[:, sl2 * D:(sl2 + 1) * D],
                in0=xpan[:, pb_ * D:(pb_ + 1) * D],
                scalar1=lnst[:, 0:1], scalar2=lnst[:, 3:4],
                op0=ALU.subtract, op1=ALU.mult),
                wait=[(VE, mv6), (PE, st["xlt_p"][sl2]),
                      (sSC[sl2], st["xlt_g"][sl2])])
            st["xpan_v"][pb_] = mv7
            st["lnst_v"] = mv7
            return ml, mv2, mv6, mv7

        def new_st():
            return dict(xpan_v=[0, 0], xpan_s=[0, 0], xpan_p=[0, 0],
                        xlt_p=[0, 0, 0, 0], xlt_g=[0, 0, 0, 0],
                        lnst_v=0, lnst_s=0, sq_v=0)

        def emit_attn(sub, kind):
            xsrc = resid[sub]
            xdst = resid[sub + 1]
            sc = 1.0 / np.sqrt(1024.0) if kind == "a" else 1.0 / np.sqrt(64.0)
            wq, wk, wv = (wq_a, wk_a, wv_a) if kind == "a" else (wq_b, wk_b, wv_b)

            for wi, w in enumerate((wq, wk, wv)):
                gq(sWP, lambda e, wi=wi, w=w: e.dma_start(
                    out=wproj.rearrange("p (i c o) -> p i c o", i=3, c=NCD)[:, wi],
                    in_=w))
            m_wp = sWP.n
            if kind == "b":
                mvo = g(lambda e: e.memset(
                    vst.rearrange("p (pr u) -> p pr u", u=130)[:, :, 64:65], 1.0))
                mvo2 = g(lambda e: e.memset(
                    vst.rearrange("p (pr u) -> p pr u", u=130)[:, :, 129:130], 1.0),
                    wait=[(GC, mvo)])
            else:
                mvo2 = g(lambda e: e.memset(
                    vst.rearrange("p (pr u) -> p pr u", u=130)[:, :, 128:130], 0.0))

            st = new_st()
            qst_use = [0, 0]
            kst_use = [0, 0]
            vst_use = 0
            mev = 0
            xln_fm_p = [0, 0]
            nrm_pe = [0, 0]
            knrm_pe = [0, 0]
            csc_ve = [0]
            qpend = []
            kpend = []
            mm = 0
            for grp in range(8):
                gb = grp % 2
                if grp == 4:
                    cast_go[0] = PE.n
                for po in range(4):
                    pi = grp * 4 + po
                    pb_ = pi % 2
                    ml, _, _, mv7 = emit_ln_panel(xsrc, pi, pb_, st)
                    w0 = take("T") + [(VE, mv7)]
                    for c in range(NCD):
                        mtl = p.emit("tensor", lambda e, c=c, pb_=pb_: e.transpose(
                            pT[:, c * 128:(c + 1) * 128],
                            xln_tok[:, pb_ * D + c * 128:pb_ * D + c * 128 + 128],
                            ident_bf[:, :]),
                            wait=w0 if c == 0 else (),
                            inc=[(PE, 1)] if c == NCD - 1 else [])
                    mtr = mtl[0]
                    st["xlt_p"][pb_] = mtr
                    mev = v(lambda e, gb=gb, po=po: e.tensor_copy(
                        out=xln_fm.rearrange(
                            "p (b c u) -> p b c u", b=2, c=NCD)[:, gb, :, po * 128:(po + 1) * 128],
                        in_=pT.ap().rearrange("p (c u) -> p c u", c=NCD)),
                        wait=[(PE, mtr), (PE, xln_fm_p[gb])])
                    used("T", VE, mev)
                for wi in range(2):
                    for oc in range(NCD):
                        reg = ["A0", "A1"][oc % 2]
                        ww = take(reg) + [(VE, mev), (sWP, m_wp)]
                        fns = []
                        for c in range(NCD):
                            fns.append(lambda e, c=c, oc=oc, wi=wi, gb=gb, reg=reg: e.matmul(
                                reg_ap(reg),
                                wproj[:, (wi * NCD + c) * 1024 + oc * 128:
                                      (wi * NCD + c) * 1024 + oc * 128 + 128],
                                xln_fm[:, gb * 4096 + c * 512:gb * 4096 + (c + 1) * 512],
                                start=(c == 0), stop=(c == NCD - 1)))
                        mm = tq(fns, wait=ww)
                        if wi == 0:
                            qs = oc % 2
                            mevq = v(lambda e, qs=qs, reg=reg: e.tensor_copy(
                                out=qst[:, qs * 512:(qs + 1) * 512], in_=reg_ap(reg)),
                                wait=[(PE, mm), (sQS[qs], qst_use[qs])])
                            used(reg, VE, mevq)
                            qst_use[qs] = dq(sQS[qs], lambda e, oc=oc, grp=grp, qs=qs: e.dma_start(
                                out=qbuf[oc, :, grp * 512:(grp + 1) * 512],
                                in_=qst[:, qs * 512:(qs + 1) * 512]),
                                wait=[(VE, mevq)])
                            msq = s_(lambda e, qs=qs: e.activation(
                                out=sqs[:, qs * 512:(qs + 1) * 512],
                                in_=qst[:, qs * 512:(qs + 1) * 512], func=AF.Square),
                                wait=[(VE, mevq), (PE, nrm_pe[qs])])
                            qpend.append((qs, msq, oc))
                            if oc == NCD - 1:
                                flushn = list(qpend)
                                qpend.clear()
                            elif len(qpend) >= 2:
                                flushn = [qpend.pop(0)]
                            else:
                                flushn = []
                            for qs_f, msq_f, oc_f in flushn:
                                if kind == "a":
                                    wN = take("S") if oc_f == 0 else []
                                    nrm_pe[qs_f] = t(lambda e, qs=qs_f, oc2=oc_f: e.matmul(
                                        pS[0:1, 0:512], ones_cf[:, :],
                                        sqs[:, qs * 512:(qs + 1) * 512],
                                        start=(oc2 == 0), stop=(oc2 == NCD - 1),
                                        skip_group_check=True),
                                        wait=wN + [(SE, msq_f)])
                                    if oc_f == NCD - 1:
                                        mrq = v(lambda e: e.reduce_max(
                                            out=csc[0:1, 2:3], in_=pS[0:1, 0:512],
                                            axis=AX.X),
                                            wait=[(PE, nrm_pe[qs_f]), (VE, csc_ve[0])])
                                        used("S", VE, mrq)
                                        if grp == 0:
                                            csc_ve[0] = v(lambda e: e.tensor_copy(
                                                out=csc[0:1, 0:1], in_=csc[0:1, 2:3]),
                                              wait=[(VE, mrq)])
                                        else:
                                            csc_ve[0] = v(lambda e: e.tensor_max(
                                                csc[0:1, 0:1], csc[0:1, 0:1],
                                                csc[0:1, 2:3]), wait=[(VE, mrq)])
                                else:
                                    for hh in range(2):
                                        r = hh * 64
                                        wN = take("S")
                                        nrm_pe[qs_f] = t(lambda e, qs=qs_f, r=r: e.matmul(
                                            pS[0:1, 0:512], ones_cf[r:r + 64, :],
                                            sqs[r:r + 64, qs * 512:(qs + 1) * 512],
                                            start=True, stop=True,
                                            tile_position=(r, 0)),
                                            wait=wN + [(SE, msq_f)])
                                        mrq = v(lambda e: e.reduce_max(
                                            out=csc[0:1, 2:3], in_=pS[0:1, 0:512],
                                            axis=AX.X),
                                            wait=[(PE, nrm_pe[qs_f]), (VE, csc_ve[0])])
                                        used("S", VE, mrq)
                                        if grp == 0 and oc_f == 0 and hh == 0:
                                            csc_ve[0] = v(lambda e: e.tensor_copy(
                                                out=csc[0:1, 0:1], in_=csc[0:1, 2:3]),
                                              wait=[(VE, mrq)])
                                        else:
                                            csc_ve[0] = v(lambda e: e.tensor_max(
                                                csc[0:1, 0:1], csc[0:1, 0:1],
                                                csc[0:1, 2:3]), wait=[(VE, mrq)])
                        else:
                            ks = oc % 2
                            mevk = v(lambda e, ks=ks, reg=reg: e.tensor_copy(
                                out=kst[:, ks * 512:(ks + 1) * 512], in_=reg_ap(reg)),
                                wait=[(PE, mm), (sKS[ks], kst_use[ks])])
                            used(reg, VE, mevk)
                            msk = s_(lambda e, ks=ks: e.activation(
                                out=sqs[:, ks * 512:(ks + 1) * 512],
                                in_=kst[:, ks * 512:(ks + 1) * 512], func=AF.Square),
                                wait=[(VE, mevk), (PE, knrm_pe[ks])])
                            kpend.append((ks, msk, oc))
                            if oc == NCD - 1:
                                flushn = list(kpend)
                                kpend.clear()
                            elif len(kpend) >= 2:
                                flushn = [kpend.pop(0)]
                            else:
                                flushn = []
                            for ks_f, msk_f, oc_f in flushn:
                                if kind == "a":
                                    wN = take("B0") if oc_f == 0 else []
                                    knrm_pe[ks_f] = t(lambda e, ks=ks_f, oc2=oc_f: e.matmul(
                                        psB[0:1, 0:512], ones_cf[:, :],
                                        sqs[:, ks * 512:(ks + 1) * 512],
                                        start=(oc2 == 0), stop=(oc2 == NCD - 1),
                                        skip_group_check=True),
                                        wait=wN + [(SE, msk_f)])
                                    if oc_f == NCD - 1:
                                        mrk2 = v(lambda e: e.reduce_max(
                                            out=csc[0:1, 2:3], in_=psB[0:1, 0:512],
                                            axis=AX.X),
                                            wait=[(PE, knrm_pe[ks_f]), (VE, csc_ve[0])])
                                        used("B0", VE, mrk2)
                                        if grp == 0:
                                            csc_ve[0] = v(lambda e: e.tensor_copy(
                                                out=csc[0:1, 1:2], in_=csc[0:1, 2:3]),
                                              wait=[(VE, mrk2)])
                                        else:
                                            csc_ve[0] = v(lambda e: e.tensor_max(
                                                csc[0:1, 1:2], csc[0:1, 1:2],
                                                csc[0:1, 2:3]), wait=[(VE, mrk2)])
                                else:
                                    for hh in range(2):
                                        r = hh * 64
                                        wN = take("B0")
                                        knrm_pe[ks_f] = t(lambda e, ks=ks_f, r=r: e.matmul(
                                            psB[0:1, 0:512], ones_cf[r:r + 64, :],
                                            sqs[r:r + 64, ks * 512:(ks + 1) * 512],
                                            start=True, stop=True,
                                            tile_position=(r, 0)),
                                            wait=wN + [(SE, msk_f)])
                                        mrk2 = v(lambda e: e.reduce_max(
                                            out=csc[0:1, 2:3], in_=psB[0:1, 0:512],
                                            axis=AX.X),
                                            wait=[(PE, knrm_pe[ks_f]), (VE, csc_ve[0])])
                                        used("B0", VE, mrk2)
                                        if grp == 0 and oc_f == 0 and hh == 0:
                                            csc_ve[0] = v(lambda e: e.tensor_copy(
                                                out=csc[0:1, 1:2], in_=csc[0:1, 2:3]),
                                              wait=[(VE, mrk2)])
                                        else:
                                            csc_ve[0] = v(lambda e: e.tensor_max(
                                                csc[0:1, 1:2], csc[0:1, 1:2],
                                                csc[0:1, 2:3]), wait=[(VE, mrk2)])
                            if kind == "a":
                                kst_use[ks] = dq(sKS[ks], lambda e, oc=oc, grp=grp, ks=ks: e.dma_start(
                                    out=kbufA[grp * 4:(grp + 1) * 4, :,
                                              oc * 128:(oc + 1) * 128].rearrange(
                                                  "j u c -> u j c"),
                                    in_=kst[:, ks * 512:(ks + 1) * 512].rearrange(
                                        "u (j c) -> u j c", j=4)),
                                    wait=[(VE, mevk)])
                            else:
                                dq(sKS[ks], lambda e, oc=oc, grp=grp, ks=ks: e.dma_start(
                                    out=kbufB[2 * oc, :, grp * 512:(grp + 1) * 512],
                                    in_=kst[0:64, ks * 512:(ks + 1) * 512]),
                                    wait=[(VE, mevk)])
                                kst_use[ks] = dq(sKS[ks], lambda e, oc=oc, grp=grp, ks=ks: e.dma_start(
                                    out=kbufB[2 * oc + 1, :, grp * 512:(grp + 1) * 512],
                                    in_=kst[64:128, ks * 512:(ks + 1) * 512]))
                for st4 in range(4):
                    tile = grp * 4 + st4
                    mvv = 0
                    for half in range(2):
                        reg = ["A2", "A3"][half]
                        ww = take(reg) + [(VE, mev), (sWP, m_wp),
                                          (sVS, vst_use), (GC, mvo2)]
                        fns = []
                        for c in range(NCD):
                            fns.append(lambda e, c=c, st4=st4, half=half, gb=gb, reg=reg: e.matmul(
                                reg_ap(reg),
                                xln_fm[:, gb * 4096 + c * 512 + st4 * 128:
                                       gb * 4096 + c * 512 + st4 * 128 + 128],
                                wproj[:, (2 * NCD + c) * 1024 + half * 512:
                                      (2 * NCD + c) * 1024 + half * 512 + 512],
                                start=(c == 0), stop=(c == NCD - 1)))
                        mm = tq(fns, wait=ww)
                        if kind == "a":
                            for cc in range(4):
                                c = half * 4 + cc
                                mvv = v(lambda e, c=c, cc=cc, reg=reg: e.tensor_copy(
                                    out=vst[:, c * 130:c * 130 + 128],
                                    in_=reg_ap(reg)[:, cc * 128:(cc + 1) * 128]),
                                    wait=[(PE, mm)] if cc == 0 else ())
                        else:
                            for hh in range(8):
                                h = half * 8 + hh
                                pr, u = h // 2, (h % 2) * 65
                                mvv = v(lambda e, pr=pr, u=u, hh=hh, reg=reg: e.tensor_copy(
                                    out=vst[:, pr * 130 + u:pr * 130 + u + 64],
                                    in_=reg_ap(reg)[:, hh * 64:(hh + 1) * 64]),
                                    wait=[(PE, mm)] if hh == 0 else ())
                        used(reg, VE, mvv)
                    if kind == "a":
                        vst_use = dq(sVS, lambda e, tile=tile: e.dma_start(
                            out=vbufA[tile], in_=vst[:, :]), wait=[(VE, mvv)])
                    else:
                        vst_use = dq(sVS, lambda e, tile=tile: e.dma_start(
                            out=vbufB[:, tile * 128:(tile + 1) * 128, :].rearrange(
                                "pr u c -> u pr c"),
                            in_=vst.rearrange("u (pr c) -> u pr c", c=130)),
                            wait=[(VE, mvv)])
                xln_fm_p[gb] = mm
                if grp == 1:
                    p._early_ve = VE.n
            msq2 = s_(lambda e: e.activation(out=csc[0:1, 0:2], in_=csc[0:1, 0:2],
                                             func=AF.Sqrt),
                      wait=[(VE, VE.n)])
            mcm = v(lambda e: e.tensor_mul(csc[0:1, 3:4], csc[0:1, 0:1],
                                           csc[0:1, 1:2]), wait=[(SE, msq2)])
            mcm = v(lambda e: e.tensor_scalar_mul(csc[0:1, 3:4], csc[0:1, 3:4],
                                                  -sc), wait=[(VE, mcm)])
            wS = take("S")
            mt2 = t(lambda e: e.matmul(
                pS[:, 0:1], ones_r[:, :], csc[0:1, 3:4], start=True, stop=True),
                wait=wS + [(VE, mcm)])
            mng = v(lambda e: e.tensor_copy(out=negC[:, :], in_=pS[:, 0:1]),
                    wait=[(PE, mt2)])
            used("S", VE, mng)
            m_proj_pe = PE.n
            m_proj_done_v = VE.n
            mq_all = [(s, s.n) for s in sQS]
            mk_all = [(s, s.n) for s in sKS]
            mv_all = [(sVS, sVS.n)]
            if kind == "b":
                mwo = gq(sWP, lambda e: e.dma_start(
                    out=wproj.rearrange("p (i c o) -> p i c o", i=3, c=NCD)[:, 0],
                    in_=wo_b),
                    wait=[(PE, m_proj_pe)])
            else:
                mwo = sWP.n

            tail = dict(xres=[0, 0], outst=[0, 0])
            kt_pe = [0, 0, 0, 0]
            vt_pe = [0, 0, 0, 0]
            kp_pe = 0
            vp_pe = 0
            kp_use = [0, 0]
            vp_use = [0, 0]
            zev_use = [0, 0]
            expT_pe = 0
            expT_slots = [0, 0, 0, 0, 0, 0]
            qb_prev = 0
            for b in range(NBLK):
                nj = 4 * (b + 1)
                mqb = dq(sQB, lambda e, b=b: e.dma_start(
                    out=qgrp.ap().rearrange("p (c u) -> p c u", c=NCD),
                    in_=qbuf[:, :, b * 512:(b + 1) * 512].rearrange(
                        "c p u -> p c u")),
                    wait=mq_all + [(PE, qb_prev)])

                if kind == "a":
                    mex_last = 0
                    msum = 0
                    for j in range(nj):
                        kb = j % 4
                        mkl = dq(sKT[kb], lambda e, j=j, kb=kb: e.dma_start(
                            out=ktile[kb], in_=kbufA[j]),
                            wait=mk_all + [(PE, kt_pe[kb])])
                        reg = ["B0", "B1"][j % 2]
                        wB = take(reg)
                        fns = []
                        for c in range(NCD):
                            fns.append(lambda e, c=c, kb=kb, reg=reg: e.matmul(
                                reg_ap(reg),
                                ktile[kb][:, c * 128:(c + 1) * 128],
                                qgrp[:, c * 512:(c + 1) * 512],
                                start=(c == 0), stop=(c == NCD - 1)))
                        mm = tq(fns, wait=wB + [(sKT[kb], mkl), (sQB, mqb),
                                                (VE, m_proj_done_v)])
                        kt_pe[kb] = mm
                        exw = [(PE, mm), (VE, mng), (PE, expT_pe), (sZR, m_zr)]
                        if j >= nj - 4:
                            tt = j - (nj - 4)
                            mmk = v(lambda e, reg=reg, tt=tt: e.tensor_add(
                                reg_ap(reg), reg_ap(reg),
                                cmask[:, tt * QBLK:(tt + 1) * QBLK]),
                                wait=[(PE, mm)])
                            mex = s_(lambda e, j=j, reg=reg: e.activation(
                                out=expT[:, j * 512:(j + 1) * 512], in_=reg_ap(reg),
                                func=AF.Exp, bias=negC[:, :], scale=sc),
                                wait=exw + [(VE, mmk)])
                        else:
                            mex = s_(lambda e, j=j, reg=reg: e.activation(
                                out=expT[:, j * 512:(j + 1) * 512], in_=reg_ap(reg),
                                func=AF.Exp, bias=negC[:, :], scale=sc),
                                wait=exw)
                        used(reg, SE, mex)
                        mex_last = mex
                    wS = take("S")
                    fns = []
                    for j in range(nj):
                        fns.append(lambda e, j=j, nj=nj: e.matmul(
                            pS[0:1, 0:512], ones_cb[:, :],
                            expT[:, j * 512:(j + 1) * 512],
                            start=(j == 0), stop=(j == nj - 1)))
                    msum = tq(fns, wait=wS + [(SE, mex_last)])
                    mfq = v(lambda e: e.reciprocal(out=fq[:, :], in_=pS[0:1, 0:512]),
                            wait=[(PE, msum)])
                    used("S", VE, mfq)
                    wB = take("B0")
                    mnb = t(lambda e: e.matmul(
                        psB[:, 0:512], ones_r[:, :], fq[:, :], start=True, stop=True),
                        wait=wB + [(VE, mfq)])
                    mnbe = v(lambda e: e.tensor_copy(out=normB[:, :], in_=psB[:, 0:512]),
                             wait=[(PE, mnb)])
                    used("B0", VE, mnbe)
                    m_z = 0
                    for hf in range(2):
                        wA = take("A0") + take("A1") + take("A2") + take("A3")
                        mm = None
                        for j in range(nj):
                            vb = j % 4
                            mvl = dq(sVT[vb], lambda e, j=j, vb=vb: e.dma_start(
                                out=vtile[vb], in_=vbufA[j]),
                                wait=mv_all + [(PE, vt_pe[vb])])
                            fns = []
                            for q_ in range(4):
                                oc = hf * 4 + q_
                                fns.append(lambda e, q_=q_, oc=oc, vb=vb, j=j, nj=nj: e.matmul(
                                    psA[:, q_ * 512:(q_ + 1) * 512],
                                    vtile[vb][:, oc * 130:oc * 130 + 128],
                                    expT[:, j * 512:(j + 1) * 512],
                                    start=(j == 0), stop=(j == nj - 1),
                                    skip_group_check=True))
                            mm = tq(fns, wait=(wA + [(SE, mex_last), (sVT[vb], mvl)])
                                    if j == 0 else [(sVT[vb], mvl)])
                            vt_pe[vb] = mm
                        expT_pe = mm
                        for q_ in range(4):
                            oc = hf * 4 + q_
                            mz = v(lambda e, q_=q_, oc=oc: e.tensor_mul(
                                zst[:, oc * 512:(oc + 1) * 512],
                                psA[:, q_ * 512:(q_ + 1) * 512], normB[:, :]),
                                wait=[(PE, mm), (VE, mnbe)])
                            used(f"A{q_}", VE, mz)
                            m_z = mz
                    zsrc = zst
                else:
                    m_z = 0

                    def pair_load(hp):
                        sl = hp % 2
                        dq(sKP2[sl], lambda e, hp=hp, nj=nj, sl=sl: e.dma_start(
                            out=kpair2[sl][0:64, 0:nj * 128],
                            in_=kbufB[2 * hp, :, 0:nj * 128]),
                            wait=mk_all + [(PE, kp_use[sl])])
                        dq(sKP2[sl], lambda e, hp=hp, nj=nj, sl=sl: e.dma_start(
                            out=kpair2[sl][64:128, 0:nj * 128],
                            in_=kbufB[2 * hp + 1, :, 0:nj * 128]))
                        mvp = dq(sVP2[sl], lambda e, hp=hp, nj=nj, sl=sl: e.dma_start(
                            out=vpair2[sl][:, 0:nj * 130],
                            in_=vbufB[hp, 0:nj * 128, :].rearrange(
                                "(j u) c -> u j c", u=128),
                            allow_slow_non_contiguous=True),
                            wait=mv_all + [(PE, vp_use[sl])])
                        return sKP2[sl].n, mvp

                    pl = {0: pair_load(0)}
                    pending_norm = [None]
                    for hp in range(8):
                        sl = hp % 2
                        if hp + 1 < 8:
                            pl[hp + 1] = pair_load(hp + 1)
                        mkp, mvp = pl[hp]
                        kpair = kpair2[sl]
                        vpair = vpair2[sl]
                        av_done = [0, 0]
                        pgs = [("B0", "B1", psB, 0), ("A2", "A3", psA, 1024)]
                        sc_mm = [0] * nj

                        def b_scores(j, hp=hp, mkp=mkp, mvp=mvp, kp=kpair):
                            g0, g1, pst, off = pgs[j % 2]
                            wB = take(g0) + take(g1)
                            fns = []
                            for hh in range(2):
                                r = hh * 64
                                fns.append(lambda e, j=j, r=r, hp=hp, pst=pst,
                                           off=off, kp=kp: e.matmul(
                                    pst[:, off + (r // 64) * 512:
                                        off + (r // 64) * 512 + 512],
                                    kp[r:r + 64, j * 128:(j + 1) * 128],
                                    qgrp[r:r + 64, hp * 512:(hp + 1) * 512],
                                    start=True, stop=True, tile_position=(r, 0)))
                            sc_mm[j] = tq(fns, wait=wB + [(sKP2[hp % 2], mkp),
                                                          (sVP2[hp % 2], mvp),
                                                          (sQB, mqb)])

                        def b_expav(j, hp=hp, nj=nj, vp=vpair):
                            nonlocal kp_pe, m_exp_last
                            g0, g1, pst, off = pgs[j % 2]
                            slot = j % 6
                            exw = [(PE, sc_mm[j]), (VE, mng),
                                   (PE, expT_slots[slot]), (sZR, m_zr)]
                            if j >= nj - 4:
                                tt = j - (nj - 4)
                                mmk = 0
                                for hh in range(2):
                                    mmk = v(lambda e, pst=pst, off=off, hh=hh,
                                            tt=tt: e.tensor_add(
                                        pst[:, off + hh * 512:off + hh * 512 + 512],
                                        pst[:, off + hh * 512:off + hh * 512 + 512],
                                        cmask[:, tt * QBLK:(tt + 1) * QBLK]),
                                        wait=[(PE, sc_mm[j])] if hh == 0 else [(VE, mmk)])
                                mex = s_(lambda e, slot=slot, pst=pst, off=off: e.activation(
                                    out=expT[:, slot * 1024:slot * 1024 + 1024],
                                    in_=pst[:, off:off + 1024], func=AF.Exp,
                                    bias=negC[:, :], scale=sc),
                                    wait=exw + [(VE, mmk)])
                            else:
                                mex = s_(lambda e, slot=slot, pst=pst, off=off: e.activation(
                                    out=expT[:, slot * 1024:slot * 1024 + 1024],
                                    in_=pst[:, off:off + 1024], func=AF.Exp,
                                    bias=negC[:, :], scale=sc),
                                    wait=exw)
                            used(g0, SE, mex)
                            used(g1, SE, mex)
                            for hh in range(2):
                                zreg = ["A0", "A1"][hh]
                                wz = take(zreg) if j == 0 else []
                                mav = p.emit("tensor", lambda e, j=j, hh=hh,
                                             slot=slot, zreg=zreg, nj=nj,
                                             vp=vp: e.matmul(
                                    reg_ap(zreg)[0:65, :],
                                    vp[:, j * 130 + hh * 65:j * 130 + hh * 65 + 65],
                                    expT[:, slot * 1024 + hh * 512:
                                         slot * 1024 + hh * 512 + 512],
                                    start=(j == 0), stop=(j == nj - 1),
                                    skip_group_check=True),
                                    wait=(wz + [(SE, mex)]),
                                    inc=[(PE, 1)])[0]
                                av_done[hh] = mav
                            expT_slots[slot] = av_done[1]
                            kp_pe = av_done[1]

                        m_exp_last = 0
                        for j in range(nj):
                            b_scores(j)
                            if j == 1 and pending_norm[0] is not None:
                                pending_norm[0]()
                                pending_norm[0] = None
                            if j >= 1:
                                b_expav(j - 1)
                        b_expav(nj - 1)
                        kp_use[sl] = av_done[1]
                        vp_use[sl] = av_done[1]
                        zsl = hp % 2
                        mze_h = [0, 0]
                        for hh in range(2):
                            zreg = ["A0", "A1"][hh]
                            mze = v(lambda e, zsl=zsl, hh=hh, zreg=zreg: e.tensor_copy(
                                out=xpan[0:65, zsl * 1024 + hh * 512:
                                         zsl * 1024 + hh * 512 + 512],
                                in_=reg_ap(zreg)[0:65, :]),
                                wait=[(PE, av_done[hh]), (VE, zev_use[zsl])])
                            used(zreg, VE, mze)
                            mze_h[hh] = mze

                        def do_norm(hp=hp, zsl=zsl, mze_h=mze_h):
                            mzl = 0
                            for hh in range(2):
                                h = hp * 2 + hh
                                c, r = h // 2, (h % 2) * 64
                                mfq = v(lambda e, zsl=zsl, hh=hh: e.reciprocal(
                                    out=fq[:, :],
                                    in_=xpan[64:65, zsl * 1024 + hh * 512:
                                             zsl * 1024 + hh * 512 + 512]),
                                    wait=[(VE, mze_h[hh])])
                                wB = take("S")
                                mnb = t(lambda e: e.matmul(
                                    pS[0:64, 0:512], ones_r[:, 0:64], fq[:, :],
                                    start=True, stop=True),
                                    wait=wB + [(VE, mfq)])
                                mnbe = v(lambda e: e.tensor_copy(
                                    out=normB[0:64, :], in_=pS[0:64, 0:512]),
                                    wait=[(PE, mnb)])
                                used("S", VE, mnbe)
                                mzl = v(lambda e, c=c, r=r, zsl=zsl, hh=hh: e.tensor_mul(
                                    zst[r:r + 64, c * 512:(c + 1) * 512],
                                    xpan[0:64, zsl * 1024 + hh * 512:
                                         zsl * 1024 + hh * 512 + 512],
                                    normB[0:64, :]),
                                    wait=[(VE, mnbe)])
                            zev_use[zsl] = mzl
                            return mzl

                        if hp < 7:
                            pending_norm[0] = do_norm
                        else:
                            m_z = do_norm()
                    mzo = 0
                    for dc in range(NCD):
                        reg = ["B0", "B1"][dc % 2]
                        wB = take(reg)
                        fns = []
                        for c in range(NCD):
                            fns.append(lambda e, c=c, dc=dc, reg=reg: e.matmul(
                                reg_ap(reg),
                                wproj[:, c * 1024 + dc * 128:c * 1024 + dc * 128 + 128],
                                zst[:, c * 512:(c + 1) * 512],
                                start=(c == 0), stop=(c == NCD - 1)))
                        mm = tq(fns, wait=wB + [(VE, m_z), (sWP, mwo)])
                        mzo = v(lambda e, dc=dc, reg=reg: e.tensor_scalar_add(
                            zst2[:, dc * 512:(dc + 1) * 512], reg_ap(reg),
                            bo_sb[:, dc:dc + 1]),
                            wait=[(PE, mm), (sBO, 16)])
                        used(reg, VE, mzo)
                    m_z = mzo
                    zsrc = zst2

                for po in range(4):
                    pi = b * 4 + po
                    tb = po % 2
                    mlr = dq(sXR[tb], lambda e, pi=pi, tb=tb: e.dma_start(
                        out=xres[tb], in_=xsrc[pi * 128:(pi + 1) * 128, :]),
                        wait=[(VE, tail["xres"][tb])])
                    wT = take("T")
                    for c in range(NCD):
                        mtl = p.emit("tensor", lambda e, c=c, po=po, zsrc=zsrc: e.transpose(
                            pT[:, c * 128:(c + 1) * 128],
                            zsrc[:, c * 512 + po * 128:c * 512 + po * 128 + 128],
                            ident_bf[:, :]),
                            wait=(wT + [(VE, m_z)]) if c == 0 else (),
                            inc=[(PE, 1)] if c == NCD - 1 else [])
                    mtr = mtl[0]
                    mad = v(lambda e, tb=tb: e.tensor_add(
                        outst[tb], pT[:, :], xres[tb]),
                        wait=[(PE, mtr), (sXR[tb], mlr), (sOS[tb], tail["outst"][tb])])
                    used("T", VE, mad)
                    tail["xres"][tb] = mad
                    tail["outst"][tb] = dq(sOS[tb], lambda e, pi=pi, tb=tb: e.dma_start(
                        out=xdst[pi * 128:(pi + 1) * 128, :], in_=outst[tb]),
                        wait=[(VE, mad)])
                qb_prev = PE.n

        def emit_ffn(sub):
            xsrc = resid[sub]
            xdst = resid[sub + 1]
            bfw = (sub == 3)
            wb = (sub == 1)
            mxe_prev = [0]
            rt_se = [0]
            wb1_use = [0, 0]
            wb2_use = 0
            v(lambda e: e.memset(rrec[:, :], 0.0))
            fW1 = sW1 if bfw else sW1c
            fW2 = sW2 if bfw else sW2c
            mgw = dq(sGW, lambda e: e.dma_start(
                out=gate_sb.ap().rearrange("p (c e2) -> p c e2", c=NCD),
                in_=gate_td.rearrange("(c p) e2 -> p c e2", p=128)))
            mz0 = v(lambda e: e.memset(offrow[:, :], 0.0))
            wS = take("S")
            fns = []
            for c in range(NCD):
                fns.append(lambda e, c=c: e.matmul(
                    pS[0:1, 0:NE], ones_cf[:, :], gate_sb[:, c * NE:(c + 1) * NE],
                    start=(c == 0), stop=(c == NCD - 1)))
            mgs = tq(fns, wait=wS + [(sGW, mgw)])
            mge = v(lambda e: e.tensor_copy(out=gsrow[:, :], in_=pS[0:1, 0:NE]),
                    wait=[(PE, mgs)])
            used("S", VE, mge)
            wS = take("S")
            mgb = t(lambda e: e.matmul(
                pS[:, 0:NE], ones_r[:, :], gsrow[:, :], start=True, stop=True),
                wait=wS + [(VE, mge)])
            mgbe = v(lambda e: e.tensor_copy(out=gsumB[:, :], in_=pS[:, 0:NE]),
                     wait=[(PE, mgb)])
            used("S", VE, mgbe)
            gsum4 = ARF[:, 2240:2304]
            iota4 = ARF[:, 2304:2368]
            for k4 in range(4):
                mgbe = v(lambda e, k4=k4: e.tensor_copy(
                    out=gsum4[:, k4 * 16:(k4 + 1) * 16], in_=gsumB[:, :]),
                    wait=[(VE, mgbe)])
                miot = v(lambda e, k4=k4: e.tensor_copy(
                    out=iota4[:, k4 * 16:(k4 + 1) * 16], in_=iotaE[:, :]))
            if DEBUG_RT:
                dq(sDB, lambda e: e.dma_start(out=dbg[4], in_=gsum4[:, :]),
                   wait=[(VE, miot)])

            st = new_st()
            st["xln_dst"] = AR2[:, 0:4096]
            st["xln_slots"] = 4
            m_off = mz0
            mct_prev = 0
            rt_ve = 0
            for bi in range(8):
                pi0 = 4 * bi
                sc_ms = []
                gate_pe = []
                for po in range(4):
                    pi = pi0 + po
                    pb_ = pi % 2
                    ml, mv2, mv6, mv7 = emit_ln_panel(xsrc, pi, pb_, st)
                    sc_ms.append(mv7)
                    mv8 = v(lambda e, pi=pi: e.tensor_copy(
                        out=rrec[:, 2 * pi:2 * pi + 1], in_=lnst[:, 0:1]),
                        wait=[(VE, mv2)])
                    mv9 = v(lambda e, pi=pi: e.tensor_copy(
                        out=rrec[:, 2 * pi + 1:2 * pi + 2], in_=lnst[:, 3:4]),
                        wait=[(VE, mv6)])
                    for rnd in range(2):
                        wB = take("B0")
                        for cc in range(4):
                            c = rnd * 4 + cc
                            mtl = p.emit("tensor", lambda e, c=c, cc=cc, pb_=pb_: e.transpose(
                                psB[:, cc * 128:(cc + 1) * 128],
                                xpan[:, pb_ * D + c * 128:pb_ * D + c * 128 + 128],
                                ident_f32[:, :]),
                                wait=(wB + [(sXP[pb_], ml)]) if cc == 0 else (),
                                inc=[(PE, 1)] if cc == 3 else [])
                        mtr = mtl[0]
                        mxe = v(lambda e, rnd=rnd: e.tensor_copy(
                            out=xfm[:, rnd * 512:(rnd + 1) * 512], in_=psB[:, 0:512]),
                            wait=[(PE, mtr), (VE, mxe_prev[0])])
                        used("B0", VE, mxe)
                    mxe_prev[0] = mxe
                    st["xpan_p"][pb_] = mtr
                    st["xpan_v"][pb_] = max(st["xpan_v"][pb_], mxe)
                    wS = take("S") if po == 0 else []
                    fns = []
                    for c in range(NCD):
                        fns.append(lambda e, c=c, po=po: e.matmul(
                            pS[:, po * 16:(po + 1) * 16],
                            xfm[:, c * 128:(c + 1) * 128],
                            gate_sb[:, c * NE:(c + 1) * NE],
                            start=(c == 0), stop=(c == NCD - 1),
                            skip_group_check=True))
                    gate_pe.append(tq(fns, wait=wS + [(VE, mxe), (VE, mgbe)]))
                if DEBUG_RT and bi == 0:
                    dq(sDB, lambda e: e.dma_start(out=dbg_x, in_=xfm[:, :]),
                       wait=[(VE, mxe)])
                    mdp = v(lambda e: e.tensor_copy(
                        out=ARF[:, 2368:2432], in_=pS[:, 0:64]),
                        wait=[(PE, gate_pe[3])])
                    used("S", VE, mdp)
                    dq(sDB, lambda e: e.dma_start(
                        out=dbg[0], in_=ARF[:, 2368:2432]), wait=[(VE, mdp)])
                def bview(col0, rep=16):
                    return bass.AP(rrec.ap().tensor, 2 * pi0 + col0,
                                   [[2 * NP, 128], [2, 4], [0, rep]])
                mb1 = v(lambda e, bv=bview(0): e.tensor_tensor(
                    out=routg[:, 0:64].rearrange("p (g r) -> p g r", g=4),
                    in0=gsum4.rearrange("p (g r) -> p g r", g=4),
                    in1=bv, op=ALU.mult),
                    wait=[(VE, VE.n), (PE, mct_prev)])
                if DEBUG_RT and bi == 0:
                    mdp1 = v(lambda e: e.tensor_copy(
                        out=ARF[:, 2432:2496], in_=routg[:, 0:64]),
                        wait=[(VE, mb1)])
                    dq(sDB, lambda e: e.dma_start(
                        out=dbg[1], in_=ARF[:, 2432:2496]), wait=[(VE, mdp1)])
                mb2 = v(lambda e: e.tensor_sub(
                    routg[:, 64:128].rearrange("p (g r) -> p g r", g=4),
                    bass.AP(pS.ap().tensor, 0, [[512, 128], [16, 4], [1, 16]]),
                    routg[:, 0:64].rearrange("p (g r) -> p g r", g=4)),
                    wait=[(PE, gate_pe[3]), (VE, mb1)])
                used("S", VE, mb2)
                mb3 = v(lambda e, bv=bview(1): e.tensor_tensor(
                    out=routg[:, 64:128].rearrange("p (g r) -> p g r", g=4),
                    in0=routg[:, 64:128].rearrange("p (g r) -> p g r", g=4),
                    in1=bv, op=ALU.mult), wait=[(VE, mb2)])
                if DEBUG_RT and bi == 0:
                    mdp3 = v(lambda e: e.tensor_copy(
                        out=ARF[:, 2496:2560], in_=routg[:, 64:128]),
                        wait=[(VE, mb3)])
                    dq(sDB, lambda e: e.dma_start(
                        out=dbg[2], in_=ARF[:, 2496:2560]), wait=[(VE, mdp3)])
                mb4 = s_(lambda e: e.activation(
                    out=routg[:, 128:192], in_=routg[:, 64:128], func=AF.Exp),
                    wait=[(VE, mb3), (SE, rt_se[0])])
                rt_se[0] = mb4
                mb5 = v(lambda e: e.reduce_max(
                    out=rout1[:, 0:4], in_=routg[:, 64:128].rearrange(
                        "p (g r) -> p g r", g=4), axis=AX.X), wait=[(VE, mb3)])
                mb6 = v(lambda e: e.reduce_max(
                    out=rout1[:, 4:8], in_=routg[:, 128:192].rearrange(
                        "p (g r) -> p g r", g=4), axis=AX.X), wait=[(SE, mb4)])
                mb7 = v(lambda e: e.reduce_sum(
                    out=rout1[:, 8:12], in_=routg[:, 128:192].rearrange(
                        "p (g r) -> p g r", g=4), axis=AX.X), wait=[(SE, mb4)])
                mb8 = v(lambda e: e.reciprocal(
                    out=rout1[:, 12:16], in_=rout1[:, 8:12]), wait=[(VE, mb7)])
                mb9 = v(lambda e: e.tensor_mul(
                    rout1[:, 16:20], rout1[:, 4:8], rout1[:, 12:16]),
                    wait=[(VE, mb8), (VE, mb6)])
                dbgw12 = []
                if DEBUG_RT and bi == 0:
                    dq(sDB, lambda e: e.dma_start(
                        out=dbg[3, :, 0:20], in_=rout1[:, 0:20]),
                        wait=[(VE, mb9)])
                    dbgw12 = [(sDB, sDB.n)]
                mb10 = v(lambda e: e.tensor_tensor(
                    out=routg[:, 0:64].rearrange("p (g r) -> p g r", g=4),
                    in0=routg[:, 64:128].rearrange("p (g r) -> p g r", g=4),
                    in1=bass.AP(rout1.ap().tensor, 0, [[32, 128], [1, 4], [0, 16]]),
                    op=ALU.is_equal), wait=[(VE, mb5), (VE, mb9)])
                rank_pe = []
                for po in range(4):
                    wS2 = []
                    p.emit("tensor", lambda e, po=po: e.matmul(
                        pS[:, 64 + po * 16:64 + (po + 1) * 16], ustrict[:, :],
                        routg[:, po * 16:(po + 1) * 16],
                        start=True, stop=False, skip_group_check=True),
                        wait=[(VE, mb10), (VE, m_off)] if po == 0 else [(VE, m_off)])
                    mrk = t(lambda e, po=po: e.matmul(
                        pS[:, 64 + po * 16:64 + (po + 1) * 16], ones_r[:, :],
                        offrow[:, :], start=False, stop=True,
                        skip_group_check=True))
                    mct = t(lambda e, po=po: e.matmul(
                        pS[0:1, 128 + po * 16:128 + (po + 1) * 16], ones_cf[:, :],
                        routg[:, po * 16:(po + 1) * 16],
                        start=True, stop=True, skip_group_check=True))
                    m_off = v(lambda e, po=po: e.tensor_add(
                        offrow[:, :], offrow[:, :],
                        pS[0:1, 128 + po * 16:128 + (po + 1) * 16]),
                        wait=[(PE, mct)])
                    used("S", VE, m_off)
                    rank_pe.append(mrk)
                    mct_prev = mct
                mb11 = v(lambda e: e.tensor_mul(
                    routg[:, 128:192].rearrange("p (g r) -> p g r", g=4),
                    routg[:, 0:64].rearrange("p (g r) -> p g r", g=4),
                    pS.ap().rearrange("p (g r) -> p g r", g=4)[:, :, 4:20][:, :, 0:16]
                    if False else
                    bass.AP(pS.ap().tensor, 64, [[512, 128], [16, 4], [1, 16]])),
                    wait=[(PE, rank_pe[3]), (VE, mb9)])
                used("S", VE, mb11)
                mb12 = v(lambda e: e.reduce_sum(
                    out=rout1[:, 4:8], in_=routg[:, 128:192].rearrange(
                        "p (g r) -> p g r", g=4), axis=AX.X),
                    wait=[(VE, mb11)] + dbgw12)
                mb13 = v(lambda e: e.tensor_tensor(
                    out=routg[:, 128:192].rearrange("p (g r) -> p g r", g=4),
                    in0=routg[:, 0:64].rearrange("p (g r) -> p g r", g=4),
                    in1=iota4.rearrange("p (g r) -> p g r", g=4),
                    op=ALU.mult), wait=[(VE, mb12)])
                mb14 = v(lambda e: e.reduce_sum(
                    out=rout1[:, 8:12], in_=routg[:, 128:192].rearrange(
                        "p (g r) -> p g r", g=4), axis=AX.X), wait=[(VE, mb13)])
                mb15 = v(lambda e: e.tensor_scalar(
                    out=rout1[:, 12:16], in0=rout1[:, 4:8],
                    scalar1=float(LOCAL_CAP), scalar2=None, op0=ALU.is_lt),
                    wait=[(VE, mb12)])
                mb16 = v(lambda e, pi0=pi0: e.tensor_mul(
                    wtok[:, pi0:pi0 + 4], rout1[:, 16:20], rout1[:, 12:16]),
                    wait=[(VE, mb15)])
                mb17 = v(lambda e: e.scalar_tensor_tensor(
                    out=rout1[:, 20:24], in0=rout1[:, 8:12],
                    scalar=float(LOCAL_CAP), in1=rout1[:, 4:8],
                    op0=ALU.mult, op1=ALU.add), wait=[(VE, mb14), (VE, mb15)])
                mb18 = v(lambda e: e.scalar_tensor_tensor(
                    out=rout1[:, 20:24], in0=rout1[:, 20:24],
                    scalar=-float(NSLOT), in1=rout1[:, 12:16],
                    op0=ALU.add, op1=ALU.mult), wait=[(VE, mb17)])
                mb19 = v(lambda e: e.tensor_scalar_add(
                    rout1[:, 20:24], rout1[:, 20:24], float(NSLOT)),
                    wait=[(VE, mb18)])
                mb20 = v(lambda e, pi0=pi0: e.tensor_copy(
                    out=sloti[:, pi0:pi0 + 4], in_=rout1[:, 20:24]),
                    wait=[(VE, mb19)])
                rt_ve = mb20
                if DEBUG_RT and bi == 0:
                    rt_ve = v(lambda e: e.tensor_copy(
                        out=rout1[0:1, 24:25], in_=rout1[0:1, 0:1]),
                        wait=[(sDB, sDB.n), (VE, mb20)])
                    mb20 = rt_ve
                for po in range(4):
                    pi = pi0 + po
                    st["xlt_g"][po] = gq(sSC[po], lambda e, pi=pi, po=po: e.indirect_dma_start(
                        out=disp,
                        out_offset=IndirectOffsetOnAxis(sloti[:, pi:pi + 1], 0),
                        in_=st["xln_dst"][:, po * D:(po + 1) * D], in_offset=None),
                        wait=[(VE, mb20), (VE, sc_ms[po]), (sZR, m_zr)])
            m_router_ve = VE.n
            m_scat = [(s, s.n) for s in sSC]

            m_w2_pe = 0
            m_xg_pe = 0
            m_yfm_pe = 0
            m_ytk_ld = 0
            w1_pe = [0, 0]
            mm = 0
            w1_pipe = [0, 0]

            def w1_load(idx):
                e2, q2_ = idx // 4, idx % 4
                wbq = q2_ % 2
                if bfw:
                    w1_pipe[wbq] = dq(sW1[wbq], lambda e, e2=e2, q2_=q2_: e.dma_start(
                        out=w1sb[wbq], in_=w1b[e2, q2_]),
                        wait=[(PE, w1_pe[wbq]), (sZR, m_zr)]
                        + (m_scat if idx < 2 else []))
                else:
                    w1_pipe[wbq] = gq(sW1c[wbq], lambda e, e2=e2, q2_=q2_: e.dma_start(
                        out=w1sb[wbq], in_=w1[e2, q2_]),
                        wait=[(PE, w1_pe[wbq]), (sZR, m_zr)]
                        + ([(sWB[wbq], wb1_use[wbq])] if wb else [])
                        + (m_scat if idx < 2 else []))

            w1_load(0)
            w1_load(1)
            for e_i in range(NE):
                for k3 in range(3):
                    dq(sXG, lambda e, e_i=e_i, k3=k3: e.dma_start(
                        out=xg[:, k3 * D:(k3 + 1) * D],
                        in_=disp[e_i * LOCAL_CAP + k3 * 128:
                                 e_i * LOCAL_CAP + (k3 + 1) * 128, :]),
                        wait=m_scat + [(PE, m_xg_pe)])
                mgx = sXG.n
                mxt = 0
                mtr = 0
                for c in range(NCD):
                    wT = take("T")
                    for sp in range(3):
                        mtl = p.emit("tensor", lambda e, c=c, sp=sp: e.transpose(
                            pT[:, sp * 128:(sp + 1) * 128],
                            xg[:, sp * D + c * 128:sp * D + c * 128 + 128],
                            ident_bf[:, :]),
                            wait=(wT + [(sXG, mgx)]) if sp == 0 else (),
                            inc=[(PE, 1)] if sp == 2 else [])
                    mtr = mtl[0]
                    mxt = v(lambda e, c=c: e.tensor_copy(
                        out=xgT[:, c * LOCAL_CAP:(c + 1) * LOCAL_CAP],
                        in_=pT[:, 0:LOCAL_CAP]),
                        wait=[(PE, mtr)])
                    used("T", VE, mxt)
                m_xg_pe = mtr
                W2P = 4
                wpc = NFC * 1024 // W2P
                for pc_ in range(W2P):
                    if bfw:
                        mw2 = gq(sW2, lambda e, e_i=e_i, pc_=pc_: e.dma_start(
                            out=w2sb[:, pc_ * wpc:(pc_ + 1) * wpc],
                            in_=w2b[e_i][:, pc_ * wpc:(pc_ + 1) * wpc]),
                            wait=([(PE, m_w2_pe), (sZR, m_zr)]
                                  if pc_ == 0 else []))
                    else:
                        mw2 = gq(sW2c, lambda e, e_i=e_i, pc_=pc_: e.dma_start(
                            out=w2sb[:, pc_ * wpc:(pc_ + 1) * wpc],
                            in_=w2[e_i][:, pc_ * wpc:(pc_ + 1) * wpc]),
                            wait=([(PE, m_w2_pe), (sZR, m_zr)]
                                  + ([(sWB[2], wb2_use)] if wb else []))
                            if pc_ == 0 else [])
                mh_done = 0
                for q4 in range(4):
                    wbq = q4 % 2
                    mw1 = w1_pipe[wbq]
                    for fc in range(8):
                        reg = ["A0", "A1"][fc % 2]
                        wA = take(reg)
                        fns = []
                        for c in range(NCD):
                            fns.append(lambda e, c=c, fc=fc, wbq=wbq, reg=reg: e.matmul(
                                reg_ap(reg)[:, 0:LOCAL_CAP],
                                w1sb[wbq][:, c * 1024 + fc * 128:c * 1024 + fc * 128 + 128],
                                xgT[:, c * LOCAL_CAP:(c + 1) * LOCAL_CAP],
                                start=(c == 0), stop=(c == NCD - 1)))
                        mm = tq(fns, wait=wA + [(fW1[wbq], mw1), (VE, mxt)])
                        fci = q4 * 8 + fc
                        mge2 = s_(lambda e, fci=fci, reg=reg: e.activation(
                            out=hsb[:, fci * LOCAL_CAP:(fci + 1) * LOCAL_CAP],
                            in_=reg_ap(reg)[:, 0:LOCAL_CAP],
                            func=AF.Gelu_apprx_tanh),
                            wait=[(PE, mm)])
                        used(reg, SE, mge2)
                        mh_done = mge2
                    w1_pe[wbq] = mm
                    if wb:
                        wb1_use[wbq] = p.emit(
                            "scalar", lambda e, e_i=e_i, q4=q4, wbq=wbq: e.dma_start(
                                out=w1b[e_i, q4], in_=w1sb[wbq]),
                            wait=[(sW1c[wbq], mw1)], inc=[(sWB[wbq], 16)])[0]
                    nxt = e_i * 4 + q4 + 2
                    if nxt < NE * 4:
                        w1_load(nxt)
                if wb:
                    for pc_ in range(W2P):
                        wb2_use = p.emit(
                            "scalar", lambda e, e_i=e_i, pc_=pc_: e.dma_start(
                                out=w2b[e_i][:, pc_ * wpc:(pc_ + 1) * wpc],
                                in_=w2sb[:, pc_ * wpc:(pc_ + 1) * wpc]),
                            wait=[(sW2c, mw2)] if pc_ == 0 else [],
                            inc=[(sWB[2], 16)])[0]
                my_done = 0
                for dc in range(NCD):
                    reg = ["A2", "A3"][dc % 2]
                    wA = take(reg)
                    fns = []
                    for fc in range(NFC):
                        fns.append(lambda e, fc=fc, dc=dc, reg=reg: e.matmul(
                            reg_ap(reg)[:, 0:LOCAL_CAP],
                            w2sb[:, fc * 1024 + dc * 128:fc * 1024 + dc * 128 + 128],
                            hsb[:, fc * LOCAL_CAP:(fc + 1) * LOCAL_CAP],
                            start=(fc == 0), stop=(fc == NFC - 1)))
                    mm = tq(fns, wait=wA + [(SE, mh_done), (fW2, mw2), (PE, m_yfm_pe)])
                    mye = v(lambda e, dc=dc, reg=reg: e.tensor_copy(
                        out=yfm[:, dc * LOCAL_CAP:(dc + 1) * LOCAL_CAP],
                        in_=reg_ap(reg)[:, 0:LOCAL_CAP]),
                        wait=[(PE, mm)])
                    used(reg, VE, mye)
                    my_done = mye
                m_w2_pe = mm
                for sp in range(3):
                    wB = take("B0") + take("B1")
                    for c in range(NCD):
                        mtl = p.emit("tensor", lambda e, c=c, sp=sp: e.transpose(
                            psB[:, c * 128:(c + 1) * 128],
                            yfm[:, c * LOCAL_CAP + sp * 128:c * LOCAL_CAP + sp * 128 + 128],
                            ident_f32[:, :]),
                            wait=(wB + [(VE, my_done)]) if c == 0 else (),
                            inc=[(PE, 1)] if c == NCD - 1 else [])
                    mtr = mtl[0]
                    m_yfm_pe = mtr
                    mty = v(lambda e, sp=sp: e.tensor_copy(
                        out=ytk[:, sp * D:(sp + 1) * D], in_=psB[:, :]),
                        wait=[(PE, mtr), (sYS, m_ytk_ld)])
                    used("B0", VE, mty)
                    used("B1", VE, mty)
                    m_ytk_ld = p.emit(
                        "scalar", lambda e, e_i=e_i, sp=sp: e.dma_start(
                            out=yexp[e_i * LOCAL_CAP + sp * 128:
                                     e_i * LOCAL_CAP + (sp + 1) * 128, :],
                            in_=ytk[:, sp * D:(sp + 1) * D]),
                        wait=[(VE, mty), (sZR, m_zr)], inc=[(sYS, 16)])[0]
            m_yexp_done = m_ytk_ld

            comb = dict(xres=[0, 0], outst=[0, 0])
            ygab = [yga, AR2[:, 34816:35840]]
            sYGb = [sYG, sYG2]
            yg_v = [m_router_ve, m_router_ve]
            mad_prev = m_router_ve
            for pi in range(NP):
                tb = pi % 2
                mlr = dq(sXR[tb], lambda e, pi=pi, tb=tb: e.dma_start(
                    out=xres[tb], in_=xsrc[pi * 128:(pi + 1) * 128, :]),
                    wait=[(VE, comb["xres"][tb]), (PE, m_yfm_pe)])
                mga = gq(sYGb[tb], lambda e, pi=pi, tb=tb: e.indirect_dma_start(
                    out=ygab[tb][:, :], out_offset=None,
                    in_=yexp, in_offset=IndirectOffsetOnAxis(sloti[:, pi:pi + 1], 0)),
                    wait=[(sYS, m_yexp_done), (VE, yg_v[tb])])
                msc = v(lambda e, pi=pi, tb=tb: e.tensor_scalar(
                    out=ysc[:, :], in0=ygab[tb][:, :], scalar1=wtok[:, pi:pi + 1],
                    scalar2=None, op0=ALU.mult),
                    wait=[(sYGb[tb], mga), (VE, mad_prev)])
                yg_v[tb] = msc
                mad = v(lambda e, tb=tb: e.tensor_add(
                    outst[tb], ysc[:, :], xres[tb]),
                    wait=[(VE, msc), (sXR[tb], mlr), (sOS[tb], comb["outst"][tb])])
                comb["xres"][tb] = mad
                mad_prev = mad
                comb["outst"][tb] = dq(sOS[tb], lambda e, pi=pi, tb=tb: e.dma_start(
                    out=xdst[pi * 128:(pi + 1) * 128, :], in_=outst[tb]),
                    wait=[(VE, mad)])

        subs = [("attn", "a"), ("ffn", None), ("attn", "b"), ("ffn", None)]
        for si in range(n_sub):
            kind = subs[si]
            if kind[0] == "attn":
                emit_attn(si, kind[1])
            else:
                emit_ffn(si)
            p.barrier(skip=("sCC0", "sCC1", "sCC2", "sCC3"))

        block = stack.enter_context(nc.Block())
        p.finalize(block)
    return nc


def _layout_inputs(inputs):
    x = np.ascontiguousarray(np.asarray(inputs["L0_attn_a_x"], dtype=np.float32))

    def pcf(w):
        return np.ascontiguousarray(
            np.asarray(w).reshape(NCD, 128, D).transpose(1, 0, 2))

    w1r = np.asarray(inputs["L0_ffn_w1"])
    w2r = np.asarray(inputs["L0_ffn_w2"])
    common = dict(
        wq_a=pcf(np.asarray(inputs["L0_attn_a_Wq"]).T),
        wk_a=pcf(np.asarray(inputs["L0_attn_a_Wk"]).T),
        wv_a=pcf(np.asarray(inputs["L0_attn_a_Wv"]).T),
        wq_b=pcf(np.asarray(inputs["L0_attn_b_W_Q"]).reshape(D, D)),
        wk_b=pcf(np.asarray(inputs["L0_attn_b_W_K"]).reshape(D, D)),
        wv_b=pcf(np.asarray(inputs["L0_attn_b_W_V"]).reshape(D, D)),
        wo_b=pcf(np.transpose(
            np.asarray(inputs["L0_attn_b_W_out"]), (1, 0, 2)).reshape(D, D)),
        bo_b=np.ascontiguousarray(np.asarray(inputs["L0_attn_b_b_out"]).reshape(D, 1)),
        gate_td=np.ascontiguousarray(np.asarray(inputs["L0_ffn_gate_w"]).T),
        w1=np.ascontiguousarray(
            w1r.reshape(NE, NCD, 128, 4, 1024).transpose(0, 3, 2, 1, 4).reshape(
                NE, 4, 128, NCD * 1024)),
        w2=np.ascontiguousarray(
            w2r.reshape(NE, NFC, 128, 1024).transpose(0, 2, 1, 3).reshape(
                NE, 128, NFC * 1024)),
    )
    in_maps = []
    for c in range(8):
        m = dict(common)
        m["x_in"] = np.ascontiguousarray(x[c])
        in_maps.append(m)
    return in_maps


def kernel(**inputs):
    in_maps = _layout_inputs(inputs)
    nc = build_program(4)
    res = run_bass_kernel_spmd(nc, in_maps, core_ids=list(range(8)))
    return np.stack([res.results[c]["y_out"] for c in range(8)], axis=0)
